# version 25
# baseline (speedup 1.0000x reference)
import numpy as np
import ml_dtypes

import concourse.bass as bass
import concourse.mybir as mybir
from concourse.bass_utils import run_bass_kernel_spmd
from contextlib import ExitStack

F32 = mybir.dt.float32
BF16 = mybir.dt.bfloat16
AF = mybir.ActivationFunctionType
OP = mybir.AluOpType
AX = mybir.AxisListType

NCORES = 8
MAGIC = 12582912.0
DM = 1024
DH = 4096
NST = 16
NCH = 8


class Sem:
    __slots__ = ("h", "n")
    def __init__(self, h):
        self.h = h
        self.n = 0


class Tok:
    __slots__ = ("sem", "val", "eng", "dma")
    def __init__(self, sem, val, eng, dma=False):
        self.sem, self.val, self.eng, self.dma = sem, val, eng, dma


def flat(*deps):
    out = []
    for d in deps:
        if d is None or isinstance(d, str):
            continue
        if isinstance(d, (tuple, list)):
            out.extend(flat(*d))
        else:
            out.append(d)
    return out


class B:
    ENGINES = ("sync", "tensor", "vector", "scalar", "gpsimd")

    def __init__(self, nc):
        self.nc = nc
        self.ops = {e: [] for e in self.ENGINES}
        self.es = ExitStack()
        self.esem = {e: Sem(self.es.enter_context(nc.semaphore(f"e_{e}")))
                     for e in self.ENGINES}
        self.bar = Sem(self.es.enter_context(nc.semaphore("bar")))
        self.cc = Sem(self.es.enter_context(nc.semaphore("cc")))
        self.seen = {e: {} for e in self.ENGINES}

    def sem(self, name):
        return Sem(self.es.enter_context(self.nc.semaphore(name)))

    def sbuf(self, name, shape, dtype=F32):
        return self.es.enter_context(self.nc.sbuf_tensor(name, list(shape), dtype))

    def psum(self, name, shape):
        return self.es.enter_context(self.nc.psum_tensor(name, list(shape), F32))

    def _waits(self, eng, deps):
        waits = {}
        drain = False
        for d in flat(*deps):
            if d.eng == eng and not d.dma:
                drain = True
                continue
            sm = d.sem
            k = id(sm)
            if self.seen[eng].get(k, -1) >= d.val:
                continue
            if k in waits:
                waits[k] = (sm, max(waits[k][1], d.val))
            else:
                waits[k] = (sm, d.val)
        for k, (sm, v) in waits.items():
            self.seen[eng][k] = max(self.seen[eng].get(k, -1), v)
        return list(waits.values()), drain

    def step(self, eng, emit, deps=(), sem=None, inc=1, dma=False):
        waits, drain = self._waits(eng, deps)
        if sem is None:
            sem = self.esem[eng]
        sem.n += inc
        tok = Tok(sem, sem.n, eng, dma)
        def f(e, waits=waits, drain=drain, emit=emit, sem=sem, inc=inc):
            if drain:
                e.drain()
            for sm, v in waits:
                e.wait_ge(sm.h, v)
            ins = emit(e)
            if ins is not None:
                ins.then_inc(sem.h, inc)
            else:
                e.sem_inc(sem.h, inc)
        self.ops[eng].append(f)
        return tok

    def dma(self, eng, out, in_, deps=(), sem=None):
        deps = flat(*deps)
        if sem is not None and sem.n > 0:
            deps.append(Tok(sem, sem.n, "_", True))
        return self.step(eng, lambda e, o=out, i=in_: e.dma_start(out=o, in_=i),
                         deps, sem=sem, inc=16, dma=True)

    def barrier(self):
        self.bar.n += len(self.ENGINES)
        tgt = self.bar.n
        for eng in self.ENGINES:
            def f(e, tgt=tgt):
                e.sem_inc(self.bar.h, 1)
                e.wait_ge(self.bar.h, tgt)
            self.ops[eng].append(f)
            self.seen[eng][id(self.bar)] = tgt
        return Tok(self.bar, tgt, "_", True)

    def collective(self, kind, op, ins_, outs_, deps=()):
        def emit(e):
            return e.collective_compute(
                kind, op, replica_groups=[list(range(NCORES))],
                ins=[ins_.opt()], outs=[outs_.opt()])
        return self.step("gpsimd", emit, deps, sem=self.cc, inc=1, dma=True)

    def emit(self):
        with self.nc.Block() as block:
            for eng in self.ENGINES:
                ops = self.ops[eng]
                getattr(block, eng)(lambda e, ops=ops: [f(e) for f in ops])


def build_program(T):
    G = T // 512
    NT = T // 128
    LC = T // NCH
    nc = bass.Bass("TRN2", target_bir_lowering=False)
    b = B(nc)

    def dram(n, s, d=F32, k=None):
        t = nc.dram_tensor(n, list(s), d, kind=k) if k else nc.dram_tensor(n, list(s), d)
        return t.ap()

    x_fm = dram("x_fm", (DM, T), F32, "ExternalInput")
    x_tm = dram("x_tm", (T, DM), F32, "ExternalInput")
    wc_t = dram("wc_t", (DM, 112), F32, "ExternalInput")
    wdt_c = dram("wdt_c", (64, 1), F32, "ExternalInput")
    bdt_i = dram("bdt_i", (1, 1), F32, "ExternalInput")
    alog = dram("alog", (1, NST), F32, "ExternalInput")
    woutT = dram("woutT", (NST, DM), F32, "ExternalInput")
    dp_row = dram("dp_row", (1, DM), F32, "ExternalInput")
    d8 = dram("d8", (128, DM), F32, "ExternalInput")
    w1t = dram("w1t", (DM, DH), F32, "ExternalInput")
    b1sb = dram("b1sb", (128, 32), F32, "ExternalInput")
    w2t = dram("w2t", (DH, DM), F32, "ExternalInput")
    b2sb = dram("b2sb", (128, 8), F32, "ExternalInput")
    wsh_d = dram("wsh", (128, 8192), F32, "ExternalInput")
    eyef_d = dram("eyef", (128, 128), F32, "ExternalInput")
    eyeb_d = dram("eyeb", (128, 128), BF16, "ExternalInput")
    bdiagp_d = dram("bdiagp", (NCH, 128), F32, "ExternalInput")
    bdiagn_d = dram("bdiagn", (NCH, 128), F32, "ExternalInput")
    r48_d = dram("r48", (48, 3 * 128), F32, "ExternalInput")
    hsel_d = dram("hsel", (1, 128), F32, "ExternalInput")
    out_d = dram("out", (T, DM), F32, "ExternalOutput")

    rs48 = dram("rs48", (48, T))
    rowA = dram("rowA", (1, T))
    rowB = dram("rowB", (1, T))
    rowC = dram("rowC", (1, T))
    rowD = dram("rowD", (1, T))
    hcb = dram("hcb", (128, LC), BF16)
    h0b = dram("h0b", (1, NST))
    arow_b = dram("arow_b", (1, NST))
    x1_d = dram("x1_d", (T, DM))
    g_d = dram("g_d", (G, 32, 128, 512))
    ar0i = dram("ar0i", (1, 2)); ar0o = dram("ar0o", (1, 2))
    ag1i = dram("ag1i", (2, NST)); ag1o = dram("ag1o", (2 * NCORES, NST))
    ar2i = dram("ar2i", (1, 2)); ar2o = dram("ar2o", (1, 2))
    ar3i = dram("ar3i", (1, 2)); ar3o = dram("ar3o", (1, 2))

    sb = b.sbuf
    s_eyef = sb("s_eyef", (128, 128))
    s_eyeb = sb("s_eyeb", (128, 128), BF16)
    s_wdt = sb("s_wdt", (64, 1))
    s_bdt = sb("s_bdt", (1, 1))
    s_bdt8 = sb("s_bdt8", (NCH, 1))
    s_alog = sb("s_alog", (1, NST))
    s_apos = sb("s_apos", (1, NST))
    s_ones1f = sb("s_ones1f", (1, 128))
    s_hsel = sb("s_hsel", (1, 128))
    CS2 = sb("CS2", (NCH, 3 * 128))
    s_atil = CS2[:, 0:128]
    s_bdiagn = CS2[:, 128:256]
    s_bdA = CS2[:, 256:384]
    s_bdiagp = sb("s_bdiagp", (NCH, 128))
    s_r48 = sb("s_r48", (112, 3 * 128))
    s_r48B = s_r48[64:112, 0:128]
    s_r48C = s_r48[64:112, 128:256]
    s_r48X = s_r48[64:112, 256:384]
    CSB = sb("CSB", (128, DM), BF16)
    s_woutb = CSB[0:NST, :]
    s_dpb = CSB[32:33, :]
    s_b1 = sb("s_b1", (128, 32))
    s_b2 = sb("s_b2", (128, 8))
    s_onesb = sb("s_onesb", (128, 1), BF16)

    AR0 = sb("AR0", (128, max(8 * T, 32 * DM)), BF16)
    AR1 = sb("AR1", (128, max(8 * T, 8 * DH)), BF16)
    AR3 = sb("AR3", (128, 14336), F32)
    BFX = sb("BFX", (128, 6144), BF16)
    AR1f = AR1[:, :].bitcast(F32)

    def a0(c0, w, p0=0, p1=128):
        return AR0[p0:p1, c0:c0 + w]
    def a1(c0, w, p0=0, p1=128):
        return AR1[p0:p1, c0:c0 + w]
    def a1f(c0, w, p0=0, p1=128):
        return AR1f[p0:p1, c0:c0 + w]
    def a3(c0, w, p0=0, p1=128):
        return AR3[p0:p1, c0:c0 + w]
    def bfx(c0, w, p0=0, p1=128):
        return BFX[p0:p1, c0:c0 + w]

    xf = lambda c, g0=0, w=None: a0(c * T + g0, w if w is not None else T)
    w2qn = lambda k, d0=0, w=DM: a0(k * DM + d0, w)
    xsq = lambda c, g0=0, w=None: a1(((c + 4) % 8) * T + g0,
                                     w if w is not None else T)
    w1qn = lambda c, h0=0, w=DH: a1(c * DH + h0, w)
    out_sl = lambda i: a1f(8192 + (i % 2) * DM, DM)

    def gq2(g_, k):
        if g_ % 2 == 0:
            return a1(k * 512, 512)
        if k < 24:
            return a1(20480 + k * 512, 512)
        return bfx((k - 24) * 512, 512)
    wsh_s = a1f(0, 8192)
    p3_xt = lambda i: a1f(i * DM, DM)
    p3_x1 = lambda i: a1f(2048 + i * DM, DM)
    p3_attn = lambda i: a1f(4096 + i * DM, DM)
    p3_t2 = lambda i: a1f(8192 + i * DM, DM)
    p3_trash = a1f(10240, DM)

    rp = lambda p0, p1, c0=0, w=None: a3(c0, w if w is not None else T, p0, p1)
    SL2 = max(T, 8 * LC)
    def sl(i, p0=0, p1=128):
        if i == 10:
            return a3(13584, LC, p0, p1)
        if i < 8:
            return a3(i * LC, LC, p0, p1)
        return a3(SL2 + (i - 8) * LC, LC, p0, p1)
    RW = SL2 + 3 * LC
    r_m8 = a3(RW + 0 * LC, LC, 0, NCH)
    r_r8 = a3(RW + 1 * LC, LC, 0, NCH)
    r_sq8 = a3(RW + 2 * LC, LC, 0, NCH)
    r_dt8 = r_sq8
    r_dtraw = a3(RW + 3 * LC, LC, 0, NCH)
    r_tmp8 = a3(RW + 4 * LC, LC, 0, NCH)
    SM = RW + 5 * LC
    rrow = a3(SM, T, 0, 1)
    mrow_neg = a3(SM, T, 32, 33)
    msum_row = mrow_neg
    sqsum_row = a3(SM, T, 64, 65)
    dtraw_row = a3(SM, T, 96, 97)
    MS1 = SM + T
    pfinrow = a3(MS1, 128, 0, 1)
    bfinrow = a3(MS1 + 128, 128, 0, 1)
    pincrow = a3(MS1 + 256, 128, 0, 1)
    bincrow = a3(MS1 + 384, 128, 0, 1)
    aggrow = a3(MS1 + 512, 256, 0, 1)
    carrrow = a3(MS1 + 768, 128, 0, 1)
    runrow = a3(MS1 + 896, NST, 0, 1)
    runtrow = a3(MS1 + 912, NST, 0, 1)
    h0row = a3(MS1 + 928, NST, 0, 1)
    h0til = a3(MS1 + 944, 112, 0, 1)
    tmp112 = a3(MS1 + 1056, 112, 0, 1)
    hinrow = a3(MS1 + 1168, 128, 0, 1)
    assert MS1 + 1296 <= 14336, MS1 + 1296

    p4_gsb = lambda q: a3(q * 2048, 2048)
    p4_w1raw = lambda i: a3(4096 + i * DM, DM) if i < 2 else a3(1024 + (i - 2) * DM, DM)
    p4_w1t0 = a3(6144, DM)
    p4_w2raw = lambda i: a3(7168, DM)
    p4_w2t0 = p4_w1t0
    p4_x1r = lambda i: a3(8192 + i * DM, DM)
    p4_t0 = lambda i: a3(10240 + i * DM, DM)
    p5_graw = lambda i: a3(0 + i * 2048, 2048)
    p5_ffn = lambda dj: a3(4096 + dj * 512, 512)
    p5_t2 = lambda i: a3(8192 + i * DM, DM)
    p5_gt0 = lambda i: a3(10240 + i * 512, 512)
    p5_gf0 = lambda i: a3(11264 + i * 512, 512)
    p5_x1r = lambda i: a3(12288 + (i % 2) * DM, DM)

    hcf = lambda t0=0, w=None: bfx(t0, w if w is not None else T, 0, NST)
    s_mrowb = bfx(0, T, 32, 33)
    s_wcb = bfx(0, 8 * 112)
    s_d8b = bfx(4096, DM)
    p4_xqn = lambda i: bfx(i * DM, DM)
    AR3B = AR3[:, :].bitcast(BF16)
    def p4_xqf2(g_, c):
        if g_ % 2 == 0:
            return bfx(2048 + c * 512, 512)
        return AR3B[:, 24576 + c * 512:24576 + (c + 1) * 512]
    def p4_xqf_dst(g_, ti):
        base = BFX[:, 2048:6144] if g_ % 2 == 0 else AR3B[:, 24576:28672]
        return base.rearrange("p (c w) -> p c w", c=8)[:, :, ti * 128:(ti + 1) * 128]

    s_rt = sb("s_rt", (128, NT))
    s_aq = sb("s_aq", (128, NT)); s_bq = sb("s_bq", (128, NT))
    _p3s = lambda i: a3(i * 64, NT)
    s_suma = _p3s(0); s_sqa = _p3s(1)
    s_m4 = _p3s(2); s_r4 = _p3s(3); s_mr4 = _p3s(4)
    s_sum2 = _p3s(5); s_sq2 = _p3s(6)
    s_m2 = _p3s(7); s_r2 = _p3s(8)
    s_minT = _p3s(9); s_maxT = _p3s(10)
    s_tt1 = _p3s(11); s_tt2 = _p3s(12)
    s_cand = sb("s_cand", (128, 2)); s_cred = sb("s_cred", (2, 1))
    s_gmx = sb("s_gmx", (128, 32)); s_gmn = sb("s_gmn", (128, 32))
    s_gmxr = sb("s_gmxr", (128, 1)); s_gmnr = sb("s_gmnr", (128, 1))
    s_gtmp = sb("s_gtmp", (128, 1))
    s_wsc = sb("s_wsc", (112, 1))
    s_c08 = sb("s_c08", (NCH, 1))
    s_wsB = sb("s_wsB", (128, 1)); s_wsC = sb("s_wsC", (128, 1)); s_wsX = sb("s_wsX", (128, 1))
    fincol = sb("fincol", (128, 2))
    hincol = sb("hincol", (128, 1))
    s_q0 = sb("s_q0", (1, 16))
    s_q2 = sb("s_q2", (1, 16))
    s_qrep = sb("s_qrep", (128, 8))
    s_wq = sb("s_wq", (128, 2))
    s_bn = sb("s_bn", (128, 12)); s_ag2 = sb("s_ag2", (128, 2))
    s_kc = sb("s_kc", (128, 10))
    KCONST = [0.0, 1.0, 0.5, 1e-5, -0.5, -1.0, -1.0 / DM, 1.0 / DM,
              MAGIC, -MAGIC]
    for _i, _v in enumerate(KCONST):
        nc.const_aps.register(F32, _v, s_kc[:, _i:_i + 1])
    s_m3 = sb("s_m3", (128, 4)); s_r3 = sb("s_r3", (128, 4)); s_mr3 = sb("s_mr3", (128, 4))

    PA = b.psum("PA", (128, 1024)); PB = b.psum("PB", (128, 1024))
    PC = b.psum("PC", (128, 1024)); PD = b.psum("PD", (128, 1024))
    def bank(i, p0=0, p1=128, c0=0, w=512):
        t_ = (PA, PB, PC, PD)[i // 2]
        return t_[p0:p1, (i % 2) * 512 + c0:(i % 2) * 512 + c0 + w]

    d_boot = b.sem("d_boot"); d_bootg = b.sem("d_bootg")
    d_h1 = b.sem("d_h1"); d_h2 = b.sem("d_h2")
    d_xf = b.sem("d_xf"); d_msc = b.sem("d_msc"); d_mscs = b.sem("d_mscs")
    d_xt = [b.sem(f"d_xt{i}") for i in range(2)]
    d_x1s = b.sem("d_x1s")
    d_x1r = [b.sem(f"d_x1r{i}") for i in range(4)]
    d_w1 = [b.sem(f"d_w1{i}") for i in range(4)]
    d_w2 = [b.sem("d_w20"), b.sem("d_w21")]
    d_gst = [b.sem(f"d_gst{i}") for i in range(4)]
    d_gld = [b.sem(f"d_gld{i}") for i in range(2)]
    d_outs = [b.sem("d_outs0"), b.sem("d_outs1")]

    V, S, TE, GP, SY = "vector", "scalar", "tensor", "gpsimd", "sync"

    tk_boot = []
    boot_specs = [
        (GP, s_eyef[:, :], eyef_d), (GP, s_eyeb[:, :], eyeb_d),
        (GP, s_wdt[:, :], wdt_c), (GP, s_bdt[:, :], bdt_i),
        (GP, s_alog[:, :], alog), (GP, s_b1[:, :], b1sb), (GP, s_b2[:, :], b2sb),
        (GP, s_bdiagp[:, :], bdiagp_d), (GP, s_bdiagn[:, :], bdiagn_d),
        (GP, s_r48[64:112, :], r48_d), (GP, s_hsel[:, :], hsel_d),
        ("wsh", None, None),
        (GP, s_wcb[:, :].rearrange("p (c w) -> p c w", c=8),
         wc_t.rearrange("(c p) w -> p c w", c=8)),
        (GP, s_woutb[:, :], woutT), (GP, s_dpb[:, :], dp_row), (GP, s_d8b[:, :], d8),
    ]
    bsems = [d_boot, d_bootg, b.sem("d_boot2")]
    d_wsh = [b.sem("d_wsh0"), b.sem("d_wsh1")]
    tk_wsh = ("multi",
              b.dma(SY, wsh_s[:, 0:4096], wsh_d[:, 0:4096], sem=d_wsh[0]),
              b.dma(S, wsh_s[:, 4096:8192], wsh_d[:, 4096:8192], sem=d_wsh[1]))
    gpi = [0]
    for qe, dst, src in boot_specs:
        if qe == "wsh":
            tk_boot.append(tk_wsh)
        else:
            tk_boot.append(b.dma(GP, dst, src, sem=bsems[gpi[0] % 3]))
            gpi[0] += 1
    (BT_EYEF, BT_EYEB, BT_WDT, BT_BDT, BT_ALOG, BT_B1, BT_B2, BT_BDP,
     BT_BDN, BT_R48, BT_HSEL, BT_WSH, BT_WCB, BT_WOUT, BT_DPB,
     BT_D8B) = range(16)
    NXS = 3 if 3 * T <= 12288 else 2
    xstage = lambda c: a3((c % NXS) * T, T)
    d_xs = [b.sem(f"d_xs{i}") for i in range(NXS)]
    tk_xfld = {}
    xfp = [0]
    def pump_xf(dep):
        if xfp[0] >= 8:
            return
        c = xfp[0]
        xfp[0] += 1
        tk_xfld[c] = b.dma(SY if c % 2 == 0 else S, xstage(c),
                           x_fm[c * 128:(c + 1) * 128, :], [dep],
                           sem=d_xs[c % NXS])
    for _ in range(NXS):
        pump_xf(None)

    def memsets(e):
        e.memset(s_onesb[:, :], 1.0)
        e.memset(s_ones1f[:, :], 1.0)
        e.memset(sl(10), 0.0)
        e.memset(s_gmxr[:, :], -3e38)
        e.memset(s_gmnr[:, :], 3e38)
        ins = None
        for _i, _v in enumerate(KCONST):
            ins = e.memset(s_kc[:, _i:_i + 1], _v)
        return ins
    tk_memset = b.step(V, memsets)
    b.step(S, lambda e: None, [tk_memset])

    tk_sq = []
    tk_xf = []
    def cast_sq(c):
        tkc = b.step(V, lambda e, c=c: e.tensor_copy(xf(c), xstage(c)),
                     [tk_xfld[c]])
        tks = b.step(S, lambda e, c=c: e.activation(
            xsq(c), xstage(c), AF.Square),
            [tk_xfld[c]] + ([tk_wred] if ((c + 4) % 8) * T < 16384 else []))
        tk_xf.append(tkc)
        tk_sq.append(tks)
        pump_xf(("multi", tkc, tks))
    def wred(e):
        e.tensor_reduce(s_cand[:, 0:1], wsh_s[:, 0:4096], AX.X, OP.max,
                        apply_absolute_value=True)
        return e.tensor_reduce(s_cand[:, 1:2], wsh_s[:, 4096:8192], AX.X, OP.max,
                               apply_absolute_value=True)
    if T >= 4096:
        for c in range(4):
            cast_sq(c)
        tk_wred = b.step(V, wred, [tk_boot[BT_WSH]])
        for c in range(4, 8):
            cast_sq(c)
    else:
        tk_wred = b.step(V, wred, [tk_boot[BT_WSH]])
        for c in range(8):
            cast_sq(c)
    tk_stgdead = ("multi", tk_sq[6], tk_sq[7], tk_xf[6], tk_xf[7])

    tk_wredT = b.step(TE, lambda e: e.transpose(bank(6, 0, 2, 0, 128), s_cand[:, :],
                                                s_eyef[:, :]), [tk_wred, tk_boot[BT_EYEF]])
    tk_wcred = b.step(V, lambda e: e.tensor_reduce(s_cred[:, :], bank(6, 0, 2, 0, 128),
                                                   AX.X, OP.max), [tk_wredT])
    tk_ar0s = b.dma(GP, ar0i.rearrange("a b -> b a"), s_cred[:, :], [tk_wcred], sem=d_msc)
    tk_cc0 = b.collective("AllReduce", OP.max, ar0i, ar0o, [tk_ar0s])
    tk_ar0l = b.dma(GP, s_q0[0:1, 0:2], ar0o, [tk_cc0], sem=d_msc)
    tk_woutneg = b.step(V, lambda e: e.tensor_single_scalar(
        s_woutb[:, :], s_woutb[:, :], -1.0, OP.mult), [tk_boot[BT_WOUT]])
    tk_apos = b.step(S, lambda e: e.activation(s_apos[:, :], s_alog[:, :], AF.Exp),
                     [tk_boot[BT_ALOG]])
    tk_ab = b.dma(GP, arow_b, s_apos[:, :], [tk_apos], sem=d_msc)
    tk_atil = b.dma(GP, s_atil[:, :].rearrange("c (r t) -> c r t", r=8),
                    arow_b.broadcast_to((NCH * 8, NST)).rearrange("(c r) t -> c r t", c=NCH),
                    [tk_ab], sem=d_msc)
    tk_bdA = b.step(V, lambda e: e.tensor_mul(s_bdA[:, :], s_atil[:, :], s_bdiagn[:, :]),
                    [tk_atil, tk_boot[BT_BDN]])
    tk_bdt8mm = b.step(TE, lambda e: e.matmul(bank(4, 0, NCH, 0, 1), s_ones1f[0:1, 0:NCH],
                                              s_bdt[:, :], start=True, stop=True),
                       [tk_boot[BT_BDT], tk_memset])
    tk_bdt8 = b.step(V, lambda e: e.tensor_copy(s_bdt8[:, :], bank(4, 0, NCH, 0, 1)),
                     [tk_bdt8mm])

    tk_mevac = []
    prev = {0: None, 1: None, 2: None, 3: None}
    for gi in range(G):
        bk = gi % 4
        def mm(e, gi=gi, bk=bk):
            ins = None
            for c in range(8):
                ins = e.matmul(bank(bk, 0, 1), s_onesb[:, :],
                               xf(c, gi * 512, 512), start=(c == 0), stop=(c == 7))
            return ins
        tkm = b.step(TE, mm, [tk_xf[7], tk_memset, prev[bk]])
        tkev = b.step(V, lambda e, gi=gi, bk=bk: e.tensor_copy(
            msum_row[0:1, gi * 512:(gi + 1) * 512], bank(bk, 0, 1)), [tkm])
        prev[bk] = tkev
        tk_mevac.append(tkev)
    tk_sqevac = []
    for gi in range(G):
        bk = gi % 4
        def mm(e, gi=gi, bk=bk):
            ins = None
            for c in range(8):
                ins = e.matmul(bank(bk, 0, 1), s_onesb[:, :],
                               xsq(c, gi * 512, 512), start=(c == 0), stop=(c == 7))
            return ins
        tkm = b.step(TE, mm, [tk_sq[7], prev[bk]])
        tkev = b.step(V, lambda e, gi=gi, bk=bk: e.tensor_copy(
            sqsum_row[0:1, gi * 512:(gi + 1) * 512], bank(bk, 0, 1)), [tkm])
        prev[bk] = tkev
        tk_sqevac.append(tkev)
    sq_done = tk_sqevac[-1]

    tk_rpev = []
    prevrp = {4: None, 5: None, 6: tk_wcred, 7: None}
    for gi in range(G):
        bk = 4 + gi % 4
        def mm(e, gi=gi, bk=bk):
            ins = None
            for c in range(8):
                ins = e.matmul(bank(bk, 0, 112), s_wcb[:, c * 112:(c + 1) * 112],
                               xf(c, gi * 512, 512), start=(c == 0), stop=(c == 7))
            return ins
        tkm = b.step(TE, mm, [tk_xf[7], tk_sq[6], tk_boot[BT_WCB], prevrp[bk]])
        tkev = b.step(V, lambda e, gi=gi, bk=bk: e.tensor_copy(
            rp(0, 112, gi * 512, 512), bank(bk, 0, 112)), [tkm])
        prevrp[bk] = tkev
        tk_rpev.append(tkev)

    def mm_wsum(e):
        ins = None
        for c in range(8):
            ins = e.matmul(bank(4, 0, 112, 0, 1), s_wcb[:, c * 112:(c + 1) * 112],
                           s_onesb[:, :], start=(c == 0), stop=(c == 7))
        return ins
    tk_wsmm = b.step(TE, mm_wsum, [tk_boot[BT_WCB], tk_memset, tk_bdt8,
                                   tk_rpev[-2] if G >= 2 else tk_rpev[-1]])
    tk_wsev = b.step(V, lambda e: e.tensor_copy(s_wsc[:, :], bank(4, 0, 112, 0, 1)),
                     [tk_wsmm])
    def mm_ws2(e):
        e.matmul(bank(4, 0, 1, 0, 1), s_wdt[:, :], s_wsc[0:64, 0:1], start=True, stop=True)
        e.matmul(bank(5, 0, 128, 0, 1), s_r48B[:, :], s_wsc[64:112, 0:1],
                 start=True, stop=True)
        e.matmul(bank(5, 0, 128, 1, 1), s_r48C[:, :], s_wsc[64:112, 0:1],
                 start=True, stop=True)
        return e.matmul(bank(5, 0, 128, 2, 1), s_r48X[:, :], s_wsc[64:112, 0:1],
                        start=True, stop=True)
    tk_ws2 = b.step(TE, mm_ws2, [tk_wsev, tk_boot[BT_WDT], tk_boot[BT_R48],
                                 tk_rpev[-1]])
    tk_wsv = b.step(V, lambda e: (
        e.tensor_copy(s_wsB[:, :], bank(5, 0, 128, 0, 1)),
        e.tensor_copy(s_wsC[:, :], bank(5, 0, 128, 1, 1)),
        e.tensor_copy(s_wsX[:, :], bank(5, 0, 128, 2, 1)))[-1], [tk_ws2])
    tk_wsneg = b.step(V, lambda e: (
        e.tensor_single_scalar(s_wsB[:, :], s_wsB[:, :], -1.0, OP.mult),
        e.tensor_single_scalar(s_wsC[:, :], s_wsC[:, :], -1.0, OP.mult),
        e.tensor_single_scalar(s_wsX[:, :], s_wsX[:, :], -1.0, OP.mult))[-1], [tk_wsv])
    tk_c0ev = b.step(V, lambda e: e.tensor_copy(s_q0[0:1, 6:7], bank(4, 0, 1, 0, 1)),
                     [tk_ws2])
    tk_c0mm = b.step(TE, lambda e: e.matmul(bank(4, 0, NCH, 4, 1), s_ones1f[0:1, 0:NCH],
                                            s_q0[0:1, 6:7], start=True, stop=True),
                     [tk_c0ev])
    tk_c08 = b.step(V, lambda e: e.tensor_copy(s_c08[:, :], bank(4, 0, NCH, 4, 1)),
                    [tk_c0mm])

    tk_dtev = []
    prevdt = {0: prev[0], 1: prev[1]}
    for gi in range(G):
        bk = gi % 2
        tkm = b.step(TE, lambda e, gi=gi, bk=bk: e.matmul(
            bank(bk, 0, 1), s_wdt[:, :], rp(0, 64, gi * 512, 512),
            start=True, stop=True),
            [tk_rpev[gi], tk_boot[BT_WDT], prevdt[bk]])
        tkev = b.step(V, lambda e, gi=gi, bk=bk: e.tensor_copy(
            dtraw_row[0:1, gi * 512:(gi + 1) * 512], bank(bk, 0, 1)), [tkm])
        prevdt[bk] = tkev
        tk_dtev.append(tkev)

    def swcalc(e):
        e.tensor_scalar(s_q0[0:1, 2:4], s_q0[0:1, 0:2], 1.0 / 127.0, 1e-8, OP.mult, OP.max)
        e.drain()
        return e.reciprocal(s_q0[0:1, 4:6], s_q0[0:1, 2:4])
    tk_sw = b.step(V, swcalc, [tk_ar0l])
    tk_swmm = b.step(TE, lambda e: e.matmul(bank(6, 0, 128, 0, 2), s_ones1f[0:1, :],
                                            s_q0[0:1, 4:6], start=True, stop=True),
                     [tk_sw, tk_dtev[-1]])
    tk_wq = b.step(V, lambda e: e.tensor_copy(s_wq[:, :], bank(6, 0, 128, 0, 2)), [tk_swmm])

    tk_br1 = b.dma(SY, rowA, msum_row, [tk_mevac[-1]], sem=d_h1)
    tk_br2 = b.dma(SY, r_m8, rowA.rearrange("o (c t) -> (o c) t", c=8),
                   [tk_br1, tk_stgdead], sem=d_h1)
    tk_br3 = b.dma(S, rowB, sqsum_row, [tk_sqevac[-1]], sem=d_h2)
    tk_br4 = b.dma(S, r_sq8, rowB.rearrange("o (c t) -> (o c) t", c=8),
                   [tk_br3, tk_stgdead], sem=d_h2)
    tk_br5 = b.dma(SY, rowC, dtraw_row, [tk_dtev[-1]], sem=d_h1)
    tk_br6 = b.dma(SY, r_dtraw, rowC.rearrange("o (c t) -> (o c) t", c=8),
                   [tk_br5, tk_stgdead], sem=d_h1)

    tk_m8 = b.step(S, lambda e: e.activation(r_m8, r_m8, AF.Identity, scale=-1.0 / DM),
                   [tk_br2])
    tk_e2 = b.step(S, lambda e: e.activation(r_sq8, r_sq8, AF.Identity, scale=1.0 / DM),
                   [tk_br4])
    tk_msq = b.step(V, lambda e: e.tensor_mul(r_tmp8, r_m8, r_m8), [tk_m8])
    tk_var = b.step(V, lambda e: e.tensor_sub(r_sq8, r_sq8, r_tmp8), [tk_e2, tk_msq])
    tk_lnv = b.step(S, lambda e: e.activation(r_tmp8, r_sq8, AF.Ln, bias=1e-5), [tk_var])
    tk_r8 = b.step(S, lambda e: e.activation(r_r8, r_tmp8, AF.Exp, scale=-0.5), [tk_lnv])

    tk_dt1 = b.step(V, lambda e: e.scalar_tensor_tensor(
        r_tmp8, r_m8, s_c08[:, :], r_dtraw, OP.mult, OP.add),
        [tk_m8, tk_br6, tk_c08, tk_r8])
    tk_dt2 = b.step(V, lambda e: e.tensor_mul(r_tmp8, r_tmp8, r_r8), [tk_dt1, tk_r8])
    tk_dt3 = b.step(S, lambda e: e.activation(r_dt8, r_tmp8, AF.Exp, bias=s_bdt8[:, :]),
                    [tk_dt2, tk_bdt8])
    tk_dt4 = b.step(S, lambda e: e.activation(r_dt8, r_dt8, AF.Ln, bias=1.0), [tk_dt3])

    tk_bx1 = b.dma(S, rs48, rp(64, 112), [tk_rpev[-1]], sem=d_h2)
    tk_Br = b.dma(SY, sl(4), rs48[0:16, :].rearrange("n (c t) -> c n t", c=8),
                  [tk_bx1, tk_dtev[-1], tk_stgdead], sem=d_h1)
    tk_Cr = b.dma(S, sl(5), rs48[16:32, :].rearrange("n (c t) -> c n t", c=8),
                  [tk_bx1, tk_dtev[-1], tk_stgdead], sem=d_h2)
    tk_Xr = b.dma(SY, sl(6), rs48[32:48, :].rearrange("n (c t) -> c n t", c=8),
                  [tk_bx1, tk_dtev[-1], tk_stgdead], sem=d_h1)

    tk_op1 = b.step(TE, lambda e: e.matmul(bank(5, 0, 128, 0, LC), s_bdA[:, :], r_dt8,
                                           start=True, stop=True),
                    [tk_dt4, tk_bdA, tk_wsv])
    tk_Abar = b.step(S, lambda e: e.activation(sl(0), bank(5, 0, 128, 0, LC), AF.Exp),
                     [tk_op1, tk_bx1, tk_dtev[-1]])
    tk_op2 = b.step(TE, lambda e: e.matmul(bank(6, 0, 128, 0, LC), s_bdiagp[:, :], r_dt8,
                                           start=True, stop=True),
                    [tk_dt4, tk_boot[BT_BDP], tk_dtev[-1]])
    tk_dts = b.step(V, lambda e: e.tensor_copy(sl(1), bank(6, 0, 128, 0, LC)),
                    [tk_op2, tk_bx1])
    tk_op3 = b.step(TE, lambda e: e.matmul(bank(0, 0, 128, 0, LC), s_bdiagp[:, :], r_m8,
                                           start=True, stop=True),
                    [tk_m8, tk_op1, tk_dtev[-1]])
    tk_ms = b.step(V, lambda e: e.tensor_copy(sl(2), bank(0, 0, 128, 0, LC)),
                   [tk_op3, tk_bx1])
    tk_op4 = b.step(TE, lambda e: e.matmul(bank(1, 0, 128, 0, LC), s_bdiagp[:, :], r_r8,
                                           start=True, stop=True),
                    [tk_r8, tk_op2, tk_dtev[-1]])
    tk_rs = b.step(V, lambda e: e.tensor_copy(sl(3), bank(1, 0, 128, 0, LC)),
                   [tk_op4, tk_bx1])
    def dtr2(e):
        e.tensor_mul(sl(1), sl(1), sl(3))
        e.drain()
        return e.tensor_mul(sl(1), sl(1), sl(3))
    tk_dtr2s = b.step(V, dtr2, [tk_dts, tk_rs])

    tk_Bc = b.step(V, lambda e: e.scalar_tensor_tensor(
        sl(4), sl(2), s_wsB[:, :], sl(4), OP.mult, OP.subtract),
        [tk_ms, tk_Br, tk_wsneg])
    tk_Xc = b.step(V, lambda e: e.scalar_tensor_tensor(
        sl(6), sl(2), s_wsX[:, :], sl(6), OP.mult, OP.subtract),
        [tk_ms, tk_Xr, tk_wsneg])
    tk_Bx1 = b.step(V, lambda e: e.tensor_mul(sl(7), sl(4), sl(6)), [tk_Bc, tk_Xc])
    tk_Bx = b.step(V, lambda e: e.tensor_mul(sl(7), sl(7), sl(1)), [tk_Bx1, tk_dtr2s])

    tk_hloc = b.step(V, lambda e: e.tensor_tensor_scan(
        sl(8), sl(0), sl(7), 0.0, OP.mult, OP.add), [tk_Abar, tk_Bx])
    tk_cumP = b.step(V, lambda e: e.tensor_tensor_scan(
        sl(9), sl(0), sl(10), 1.0, OP.mult, OP.add), [tk_hloc, tk_memset])

    tk_fin = b.step(V, lambda e: (
        e.tensor_copy(fincol[:, 0:1], sl(9)[:, LC - 1:LC]),
        e.tensor_copy(fincol[:, 1:2], sl(8)[:, LC - 1:LC]))[-1], [tk_cumP])
    def finT(e):
        e.transpose(bank(7, 0, 1, 0, 128), fincol[:, 0:1], s_eyef[:, :])
        return e.transpose(bank(7, 0, 1, 128, 128), fincol[:, 1:2], s_eyef[:, :])
    tk_finT = b.step(TE, finT, [tk_fin])
    tk_finr = b.step(V, lambda e: (
        e.tensor_copy(pfinrow[:, :], bank(7, 0, 1, 0, 128)),
        e.tensor_copy(bfinrow[:, :], bank(7, 0, 1, 128, 128)))[-1], [tk_finT])

    def sumscan(e):
        Prow = pfinrow[0:1, :].rearrange("o (c n) -> o c n", c=8)
        Brow = bfinrow[0:1, :].rearrange("o (c n) -> o c n", c=8)
        pv = pincrow[0:1, :].rearrange("o (c n) -> o c n", c=8)
        bv = bincrow[0:1, :].rearrange("o (c n) -> o c n", c=8)
        ins = None
        for n in range(NST):
            e.tensor_tensor_scan(pv[:, :, n], Prow[:, :, n], sl(10)[0:1, 0:NCH],
                                 1.0, OP.mult, OP.add)
        for n in range(NST):
            ins = e.tensor_tensor_scan(bv[:, :, n], Prow[:, :, n], Brow[:, :, n],
                                       0.0, OP.mult, OP.add)
        return ins
    tk_sscan = b.step(V, sumscan, [tk_finr, tk_memset])

    tk_ag_s1 = b.dma(SY, ag1i[0:1, :], pincrow[0:1, 112:128], [tk_sscan], sem=d_h1)
    tk_ag_s2 = b.dma(S, ag1i[1:2, :], bincrow[0:1, 112:128], [tk_sscan], sem=d_h2)
    tk_cc1 = b.collective("AllGather", OP.bypass, ag1i, ag1o, [tk_ag_s1, tk_ag_s2])
    tk_agl = b.dma(SY, aggrow, ag1o.rearrange("(o a) b -> o (a b)", o=1), [tk_cc1],
                   sem=d_h1)

    def combine(e):
        e.memset(runrow[:, :], 0.0)
        for j in range(NCORES):
            if j == 4:
                e.drain()
                e.memset(runrow[:, :], 0.0)
            e.drain()
            e.tensor_copy(carrrow[0:1, NST * j:NST * (j + 1)], runrow[:, :])
            Pj = aggrow[0:1, 32 * j:32 * j + 16]
            Bj = aggrow[0:1, 32 * j + 16:32 * j + 32]
            e.drain()
            e.tensor_mul(runtrow[:, :], Pj, runrow[:, :])
            e.drain()
            e.tensor_add(runrow[:, :], runtrow[:, :], Bj)
        e.drain()
        e.tensor_mul(carrrow[:, :], carrrow[:, :], s_hsel[:, :])
        e.drain()
        e.tensor_add(carrrow[0:1, 0:64], carrrow[0:1, 0:64], carrrow[0:1, 64:128])
        e.drain()
        e.tensor_add(carrrow[0:1, 0:32], carrrow[0:1, 0:32], carrrow[0:1, 32:64])
        e.drain()
        return e.tensor_add(h0row[:, :], carrrow[0:1, 0:16], carrrow[0:1, 16:32])
    tk_h0 = b.step(V, combine, [tk_agl, tk_boot[BT_HSEL]])

    tk_h0s = b.dma(SY, h0b, h0row, [tk_h0], sem=d_h1)
    tk_h0t = b.dma(SY, h0til.rearrange("o (r t) -> o r t", r=7),
                   h0b.broadcast_to((7, NST)).rearrange("(o r) t -> o r t", o=1),
                   [tk_h0s], sem=d_h1)
    def hin(e):
        e.tensor_copy(hinrow[0:1, 0:16], h0row[:, :])
        e.tensor_mul(tmp112[:, :], pincrow[0:1, 0:112], h0til[:, :])
        e.drain()
        return e.tensor_add(hinrow[0:1, 16:128], tmp112[:, :], bincrow[0:1, 0:112])
    tk_hinr = b.step(V, hin, [tk_h0t, tk_h0, tk_sscan])
    tk_hinT = b.step(TE, lambda e: e.transpose(bank(7, 0, 128, 0, 1), hinrow[:, :],
                                               s_eyef[0:1, 0:1]),
                     [tk_hinr, tk_finT, tk_finr])
    tk_hin = b.step(V, lambda e: e.tensor_copy(hincol[:, :], bank(7, 0, 128, 0, 1)),
                    [tk_hinT])

    tk_h = b.step(V, lambda e: e.scalar_tensor_tensor(
        sl(0), sl(9), hincol[:, :], sl(8), OP.mult, OP.add),
        [tk_hin, tk_cumP, tk_hloc])
    tk_Cc = b.step(V, lambda e: e.scalar_tensor_tensor(
        sl(5), sl(2), s_wsC[:, :], sl(5), OP.mult, OP.subtract),
        [tk_Cr, tk_Bx1, tk_wsneg])
    tk_hc = b.step(V, lambda e: e.tensor_mul(sl(7), sl(0), sl(5)),
                   [tk_h, tk_Cc, tk_Bx])

    tk_hcs = b.dma(GP, hcb, sl(7), [tk_hc], sem=d_msc)
    tk_hcf = b.dma(SY, hcf()[:, :].rearrange("n (c t) -> n c t", c=8),
                   hcb.rearrange("(c n) t -> n c t", c=8),
                   [tk_hcs, tk_rpev[-1], tk_wsev], sem=d_h1)

    tk_rr1 = b.dma(S, rowD, r_r8, [tk_r8, tk_op4], sem=d_h2)
    tk_rr2 = b.dma(S, rrow.rearrange("o (c t) -> o c t", c=8),
                   rowD.rearrange("o (c t) -> o c t", c=8),
                   [tk_rr1, tk_br3, tk_br4], sem=d_h2)
    tk_mr1 = b.dma(SY, rowA, r_m8, [tk_m8, tk_br2, tk_msq, tk_dt1, tk_op3, tk_br1],
                   sem=d_h1)
    tk_mr2 = b.dma(SY, mrow_neg.rearrange("o (c t) -> o c t", c=8),
                   rowA.rearrange("o (c t) -> o c t", c=8),
                   [tk_mr1, tk_mevac[-1]], sem=d_h1)
    tk_mrb = b.step(S, lambda e: e.activation(s_mrowb[:, :], mrow_neg, AF.Identity),
                    [tk_mr2, tk_wsmm])
    def rt_mm(e):
        ins = None
        for tt in range(NT):
            ins = e.transpose(bank(7, 0, 128, tt, 1), rrow[0:1, tt * 128:(tt + 1) * 128],
                              s_eyef[0:1, 0:1])
        return ins
    tk_rtT = b.step(TE, rt_mm, [tk_rr2, tk_hinT, tk_hin])
    tk_rt = b.step(V, lambda e: e.tensor_copy(s_rt[:, :], bank(7, 0, 128, 0, NT)),
                   [tk_rtT])

    NBLK = 4
    tk_comb = {}
    tk_x1 = {}
    tk_x1st = {}
    tk_xtld = {}
    tk_sqa = {}
    xtp = [0]
    def pump_xt(dep):
        if xtp[0] >= NT:
            return
        tt = xtp[0]
        xtp[0] += 1
        tk_xtld[tt] = b.dma(SY if tt % 2 == 0 else S, p3_xt(tt % 2),
                            x_tm[tt * 128:(tt + 1) * 128, :],
                            [sq_done, dep], sem=d_xt[tt % 2])
    pump_xt(None)
    pump_xt(None)
    x1_free = [sq_done, sq_done]
    attn_free = [sq_done] * NBLK
    t2_free = [sq_done, sq_done]
    pend_sq2 = [None]
    def emit_sq2():
        nonlocal_state = pend_sq2[0]
        if nonlocal_state is None:
            return
        tt, sl_ = nonlocal_state
        pend_sq2[0] = None
        tk_s2 = b.step(S, lambda e, tt=tt, sl_=sl_: e.activation(
            p3_trash, p3_x1(sl_), AF.Square, accum_out=s_sq2[:, tt:tt + 1]),
            [tk_x1[tt], _trash[0]])
        _trash[0] = tk_s2
        _lasts2[0] = tk_s2
        x1_free[sl_] = ("multi", x1_free[sl_], tk_s2)
    _trash = [sq_done]
    _lasts2 = [None]
    ybank_free = [None, None]
    last_mn = None

    for blk in range(NT // NBLK):
        for ti in range(NBLK):
            tt = blk * NBLK + ti
            par = tt % 2
            yb = (PA, PC)[par]
            sl_ = tt % 2
            def mm(e, tt=tt, yb=yb):
                for half in range(2):
                    e.matmul(yb[0:128, half * 512:(half + 1) * 512],
                             hcf(tt * 128, 128),
                             s_woutb[:, half * 512:(half + 1) * 512],
                             start=True, stop=False)
                for c in range(8):
                    e.matmul(yb[0:128, c * 128:(c + 1) * 128],
                             xf(c, tt * 128, 128), s_d8b[:, c * 128:(c + 1) * 128],
                             start=False, stop=False)
                e.matmul(yb[0:128, 0:512], s_mrowb[0:1, tt * 128:(tt + 1) * 128],
                         s_dpb[0:1, 0:512], start=False, stop=True)
                return e.matmul(yb[0:128, 512:1024],
                                s_mrowb[0:1, tt * 128:(tt + 1) * 128],
                                s_dpb[0:1, 512:1024], start=False, stop=True)
            deps = [tk_hcf, tk_mrb, tk_woutneg, tk_boot[BT_DPB], tk_boot[BT_D8B],
                    tk_rt, ybank_free[par]]
            if tt < 2:
                deps += [tk_rtT]
            tk_mm = b.step(TE, mm, deps)
            def comb(e, tt=tt, ti=ti, yb=yb):
                return e.activation(p3_attn(ti), yb[0:128, 0:1024], AF.Identity,
                                    scale=s_rt[:, tt:tt + 1],
                                    accum_out=s_suma[:, tt:tt + 1])
            tk_comb[tt] = b.step(S, comb, [tk_mm, tk_rt, attn_free[ti]])
            ybank_free[par] = tk_comb[tt]
            tk_sqa[tt] = b.step(S, lambda e, tt=tt, ti=ti: e.activation(
                p3_trash, p3_attn(ti), AF.Square, accum_out=s_sqa[:, tt:tt + 1]),
                [tk_comb[tt], _trash[0]])
            _trash[0] = tk_sqa[tt]
        b0, b1_ = blk * NBLK, (blk + 1) * NBLK
        def bst1(e, b0=b0, b1_=b1_):
            e.activation(s_m4[:, b0:b1_], s_suma[:, b0:b1_], AF.Identity, scale=-1.0 / DM)
            e.drain()
            return e.activation(s_tt1[:, b0:b1_], s_sqa[:, b0:b1_], AF.Identity,
                                scale=1.0 / DM)
        tk_b1 = b.step(S, bst1, [tk_sqa[b1_ - 1], tk_comb[b1_ - 1]])
        def bst2(e, b0=b0, b1_=b1_):
            e.tensor_mul(s_tt2[:, b0:b1_], s_m4[:, b0:b1_], s_m4[:, b0:b1_])
            e.drain()
            return e.tensor_sub(s_tt1[:, b0:b1_], s_tt1[:, b0:b1_], s_tt2[:, b0:b1_])
        tk_b2 = b.step(V, bst2, [tk_b1])
        def bst3(e, b0=b0, b1_=b1_):
            e.activation(s_tt2[:, b0:b1_], s_tt1[:, b0:b1_], AF.Ln, bias=1e-5)
            e.drain()
            return e.activation(s_r4[:, b0:b1_], s_tt2[:, b0:b1_], AF.Exp, scale=-0.5)
        tk_b3 = b.step(S, bst3, [tk_b2])
        tk_b4 = b.step(V, lambda e, b0=b0, b1_=b1_: e.tensor_mul(
            s_mr4[:, b0:b1_], s_m4[:, b0:b1_], s_r4[:, b0:b1_]), [tk_b3])
        for ti in range(NBLK):
            tt = blk * NBLK + ti
            sl_ = tt % 2
            tk_t2 = b.step(S, lambda e, tt=tt, sl_=sl_: e.activation(
                p3_t2(sl_), p3_xt(sl_), AF.Identity, bias=s_mr4[:, tt:tt + 1]),
                [tk_xtld[tt], tk_b4, t2_free[sl_]])
            pump_xt(tk_t2)
            emit_sq2()
            def x1f(e, tt=tt, ti=ti, sl_=sl_):
                return e.scalar_tensor_tensor(
                    p3_x1(sl_), p3_attn(ti), s_r4[:, tt:tt + 1], p3_t2(sl_),
                    OP.mult, OP.add, accum_out=s_sum2[:, tt:tt + 1])
            tk_x1[tt] = b.step(V, x1f, [tk_t2, tk_comb[tt], tk_b4, x1_free[sl_]])
            attn_free[ti] = tk_x1[tt]
            t2_free[sl_] = tk_x1[tt]
            tk_mn = b.step(V, lambda e, tt=tt, sl_=sl_: (
                e.tensor_reduce(s_minT[:, tt:tt + 1], p3_x1(sl_), AX.X, OP.min),
                e.tensor_reduce(s_maxT[:, tt:tt + 1], p3_x1(sl_), AX.X, OP.max))[-1],
                [tk_x1[tt]])
            last_mn = tk_mn
            pend_sq2[0] = (tt, sl_)
            tk_x1st[tt] = b.dma(GP, x1_d[tt * 128:(tt + 1) * 128, :], p3_x1(sl_),
                                [tk_x1[tt]], sem=d_x1s)
            x1_free[sl_] = ("multi", tk_mn, tk_x1st[tt])

    emit_sq2()
    def m2c(e):
        e.activation(s_m2[:, :], s_sum2[:, :], AF.Identity, scale=-1.0 / DM)
        e.drain()
        return e.activation(s_tt1[:, :], s_sq2[:, :], AF.Identity, scale=1.0 / DM)
    tk_m2 = b.step(S, m2c, [tk_x1[NT - 1], _lasts2[0]])
    def v2a(e):
        e.tensor_mul(s_tt2[:, :], s_m2[:, :], s_m2[:, :])
        e.drain()
        return e.tensor_sub(s_tt1[:, :], s_tt1[:, :], s_tt2[:, :])
    tk_v2a = b.step(V, v2a, [tk_m2])
    def s2b(e):
        e.activation(s_tt2[:, :], s_tt1[:, :], AF.Ln, bias=1e-5)
        e.drain()
        return e.activation(s_r2[:, :], s_tt2[:, :], AF.Exp, scale=-0.5)
    tk_s2b = b.step(S, s2b, [tk_v2a])
    def cnd(e):
        e.tensor_add(s_tt1[:, :], s_maxT[:, :], s_m2[:, :])
        e.tensor_add(s_tt2[:, :], s_minT[:, :], s_m2[:, :])
        e.drain()
        e.tensor_mul(s_tt1[:, :], s_tt1[:, :], s_r2[:, :])
        e.tensor_mul(s_tt2[:, :], s_tt2[:, :], s_r2[:, :])
        e.drain()
        e.tensor_single_scalar(s_tt2[:, :], s_tt2[:, :], -1.0, OP.mult)
        e.drain()
        e.tensor_reduce(s_cand[:, 0:1], s_tt1[:, :], AX.X, OP.max)
        return e.tensor_reduce(s_cand[:, 1:2], s_tt2[:, :], AX.X, OP.max)
    tk_cnd = b.step(V, cnd, [tk_s2b, last_mn])
    tk_cndT = b.step(TE, lambda e: e.transpose(bank(7, 0, 2, 0, 128), s_cand[:, :],
                                               s_eyef[:, :]), [tk_cnd, tk_rtT, tk_rt])
    tk_cnd2 = b.step(V, lambda e: e.tensor_reduce(s_cred[:, :], bank(7, 0, 2, 0, 128),
                                                  AX.X, OP.max), [tk_cndT])
    tk_ar2s = b.dma(GP, ar2i.rearrange("a b -> b a"), s_cred[:, :], [tk_cnd2], sem=d_msc)
    tk_cc2 = b.collective("AllReduce", OP.max, ar2i, ar2o, [tk_ar2s])
    tk_ar2l = b.dma(GP, s_q0[0:1, 8:10], ar2o, [tk_cc2], sem=d_msc)

    def qs1(e):
        e.tensor_add(s_q0[0:1, 10:11], s_q0[0:1, 8:9], s_q0[0:1, 9:10])
        e.drain()
        e.tensor_scalar(s_q0[0:1, 10:11], s_q0[0:1, 10:11], 1.0 / 255.0, 1e-8,
                        OP.mult, OP.max)
        e.drain()
        e.reciprocal(s_q0[0:1, 11:12], s_q0[0:1, 10:11])
        e.drain()
        e.tensor_mul(s_q0[0:1, 12:13], s_q0[0:1, 9:10], s_q0[0:1, 11:12])
        e.drain()
        e.tensor_scalar(s_q0[0:1, 14:15], s_q0[0:1, 12:13], MAGIC, MAGIC,
                        OP.add, OP.subtract)
        e.drain()
        e.tensor_scalar(s_q0[0:1, 14:15], s_q0[0:1, 14:15], -128.0, None, OP.add)
        e.drain()
        e.tensor_scalar(s_q0[0:1, 14:15], s_q0[0:1, 14:15], -128.0, 127.0,
                        OP.max, OP.min)
        e.drain()
        e.tensor_scalar(s_q0[0:1, 12:13], s_q0[0:1, 14:15], -1.0, -128.0,
                        OP.mult, OP.add)
        e.tensor_scalar(s_q0[0:1, 13:14], s_q0[0:1, 14:15], -1.0, 127.0,
                        OP.mult, OP.add)
        e.drain()
        return e.tensor_mul(s_q0[0:1, 15:16], s_q0[0:1, 10:11], s_q0[0:1, 2:3])
    tk_qs1 = b.step(V, qs1, [tk_ar2l, tk_sw])
    tk_qrmm1 = b.step(TE, lambda e: e.matmul(bank(7, 0, 128, 0, 4), s_ones1f[0:1, :],
                                             s_q0[0:1, 12:16], start=True, stop=True),
                      [tk_qs1, tk_cndT, tk_cnd2])
    tk_qrep1 = b.step(V, lambda e: (
        e.tensor_copy(s_qrep[:, 0:1], bank(7, 0, 128, 0, 1)),
        e.tensor_copy(s_qrep[:, 1:2], bank(7, 0, 128, 1, 1)),
        e.tensor_copy(s_qrep[:, 2:3], bank(7, 0, 128, 3, 1)))[-1], [tk_qrmm1])
    tk_qrmm2 = b.step(TE, lambda e: e.matmul(bank(7, 0, 128, 4, 1), s_ones1f[0:1, :],
                                             s_q0[0:1, 11:12], start=True, stop=True),
                      [tk_qs1, tk_qrmm1])
    tk_qrep2 = b.step(V, lambda e: e.tensor_copy(s_qrep[:, 7:8], bank(7, 0, 128, 4, 1)),
                      [tk_qrmm2])
    def aqbq(e):
        e.tensor_scalar(s_aq[:, :], s_r2[:, :], s_qrep[:, 7:8], None, OP.mult)
        e.drain()
        return e.tensor_mul(s_bq[:, :], s_m2[:, :], s_aq[:, :])
    tk_aq = b.step(V, aqbq, [tk_qrep2, tk_s2b, tk_cnd])

    all_x1st = Tok(d_x1s, d_x1s.n, "_", True)

    tk_bar34 = b.barrier()

    tk_w1q = {}
    w1_last_act = [None, None, None, None]
    t0scr_free = [None]
    w1_seq = [(hh, c) for hh in range(4) for c in range(8)]
    w1_slot = lambda idx: (idx % 4) if idx < 16 else (idx % 2)
    w1_next = {}
    _seen = {}
    for _i in range(32):
        _s = w1_slot(_i)
        if _s in _seen:
            w1_next[_seen[_s]] = _i
        _seen[_s] = _i
    w1_ld = {}
    def w1_load(idx, dep):
        hh, c = w1_seq[idx]
        sl_ = w1_slot(idx)
        w1_ld[idx] = b.dma(SY if sl_ % 2 == 0 else S, p4_w1raw(sl_),
                           w1t[c * 128:(c + 1) * 128, hh * 1024:(hh + 1) * 1024],
                           [tk_cnd, dep], sem=d_w1[sl_])
    def w1_emit(idx):
        hh, c = w1_seq[idx]
        sl_ = w1_slot(idx)
        tkt0 = b.step(S, lambda e, sl_=sl_: e.activation(
            p4_w1t0, p4_w1raw(sl_), AF.Identity, scale=s_wq[:, 0:1]),
            [w1_ld[idx], tk_wq, t0scr_free[0]])
        w1_last_act[sl_] = tkt0
        tkq = b.step(V, lambda e, c=c, hh=hh: e.tensor_scalar(
            w1qn(c, hh * 1024, 1024), p4_w1t0, MAGIC, MAGIC, OP.add, OP.subtract),
            [tkt0, all_x1st])
        t0scr_free[0] = tkq
        tk_w1q[(c, hh)] = tkq
        if idx in w1_next:
            w1_load(w1_next[idx], tkt0)
    for idx in range(4):
        w1_load(idx, None)
    for idx in range(32):
        w1_emit(idx)
    gsb_free2 = [w1_last_act[2], w1_last_act[3]]

    tk_w2q = {}
    w2_last_act = {0: None, 1: None}
    w2_slotmap = lambda k: (k % 2) if G > 1 else 0
    def p4_w2slot(s):
        return p4_w2raw(0) if s == 0 else p4_w1raw(0)
    w2_ld = {}
    w2lp = [0]
    def w2_pump():
        if w2lp[0] >= 32:
            return
        k = w2lp[0]
        w2lp[0] += 1
        s = w2_slotmap(k)
        dep = w2_last_act[s]
        if s == 1 and dep is None:
            dep = w1_last_act[0]
        w2_ld[k] = b.dma(SY if k % 2 == 0 else S, p4_w2slot(s),
                         w2t[k * 128:(k + 1) * 128, :],
                         [tk_bar34, dep], sem=d_w2[s])
    def w2_emit(k):
        s = w2_slotmap(k)
        tkt0 = b.step(S, lambda e, s=s: e.activation(
            p4_w2t0, p4_w2slot(s), AF.Identity, scale=s_wq[:, 1:2]),
            [w2_ld[k], tk_wq, t0scr_free[0]])
        w2_last_act[s] = tkt0
        tkq = b.step(V, lambda e, k=k: e.tensor_scalar(
            w2qn(k), p4_w2t0, MAGIC, MAGIC, OP.add, OP.subtract), [tkt0])
        t0scr_free[0] = tkq
        tk_w2q[k] = tkq
        w2_pump()
    def w2_start():
        if w2lp[0] == 0:
            w2_pump()
            w2_pump()

    xqn_free = [tk_hcf, ("multi", tk_mrb)]
    xqf_prev = {0: ("multi", tk_hcf, tk_mrb), 1: None}
    t0_free = [None, None]
    gsb_free = gsb_free2
    h_bank_free = {4: None, 5: None, 6: None, 7: None}
    tqbank_free = [None, None]
    tk_gst = {}
    gmm_last = None

    tk_xq = {}
    tk_clip4 = {}
    last_mm1_of = {}
    tk_x4ld = {}
    x4_seq = [(g_, ti) for g_ in range(G) for ti in range(4)]
    x4p = [0]
    def pump_x4(dep):
        if x4p[0] >= len(x4_seq):
            return
        g_, ti = x4_seq[x4p[0]]
        x4p[0] += 1
        tt = g_ * 4 + ti
        tk_x4ld[(g_, ti)] = b.dma(
            SY, p4_x1r(ti % 2), x1_d[tt * 128:(tt + 1) * 128, :],
            [tk_cnd, tk_x1st[tt], dep], sem=d_x1r[ti % 2])
    pump_x4(None)
    pump_x4(None)

    tk_f04 = {}
    def q4_pre(qg, ti):
        sl_ = ti % 2
        tt = qg * 4 + ti
        tkt0 = b.step(S, lambda e, sl_=sl_, tt=tt: e.activation(
            p4_t0(sl_), p4_x1r(sl_), AF.Identity, scale=s_aq[:, tt:tt + 1],
            bias=s_bq[:, tt:tt + 1]), [tk_x4ld[(qg, ti)], tk_aq, t0_free[sl_]])
        tkf0 = b.step(V, lambda e, sl_=sl_: e.tensor_scalar(
            p4_x1r(sl_), p4_t0(sl_), MAGIC, MAGIC, OP.add, OP.subtract), [tkt0])
        t0_free[sl_] = tkf0
        tk_f04[(qg, ti)] = tkf0

    def q4_clip(qg, ti):
        sl_ = ti % 2
        tkxq = b.step(V, lambda e, sl_=sl_, ti=ti: e.tensor_scalar(
            p4_xqn(ti % 2), p4_x1r(sl_), s_qrep[:, 0:1], s_qrep[:, 1:2],
            OP.max, OP.min), [tk_f04[(qg, ti)], tk_qrep1, xqn_free[ti % 2]])
        tk_clip4[(qg, ti)] = tkxq
        pump_x4(tkxq)

    def q4_trn_ev(qg, ti):
        tb = ti % 2
        def trn(e, ti=ti, tb=tb):
            pq = (PA, PB)[tb][0:128, 0:512].bitcast(BF16)
            ins = None
            for c in range(8):
                ins = e.transpose(pq[:, c * 128:(c + 1) * 128],
                                  p4_xqn(ti % 2)[:, c * 128:(c + 1) * 128],
                                  s_eyeb[:, :])
            return ins
        tktr = b.step(TE, trn, [tk_clip4[(qg, ti)], tk_boot[BT_EYEB],
                                tqbank_free[tb]])
        xqn_free[ti % 2] = tktr
        def ev(e, qg=qg, ti=ti, tb=tb):
            dst = p4_xqf_dst(qg, ti)
            src = (PA, PB)[tb][0:128, 0:512].bitcast(BF16).rearrange(
                "p (c w) -> p c w", c=8)
            return e.activation(dst, src, AF.Identity)
        tkev = b.step(S, ev, [tktr, last_mm1_of.get(qg - 2, xqf_prev[qg % 2])])
        tqbank_free[tb] = tkev
        tk_xq[(qg, ti)] = tkev

    for it in range(G + 1):
        qg = it if it < G else None
        mg = it - 1 if it >= 1 else None
        if qg == 0:
            q4_pre(0, 0)
            q4_pre(0, 1)
            q4_clip(0, 0)
            q4_pre(0, 2)
            q4_clip(0, 1)
            q4_pre(0, 3)
        if mg is None:
            if qg == 0:
                q4_trn_ev(0, 0)
                q4_clip(0, 2)
                q4_trn_ev(0, 1)
                q4_clip(0, 3)
                q4_trn_ev(0, 2)
                q4_trn_ev(0, 3)
                if G > 1:
                    q4_pre(1, 0)
                    q4_pre(1, 1)
                    q4_clip(1, 0)
                    q4_pre(1, 2)
                    q4_clip(1, 1)
                    q4_pre(1, 3)
            continue
        if G == 1:
            w2ks = list(range(32)) if mg == 0 else []
        elif mg >= 1:
            w2ks = list(range(((mg - 1) * 32) // (G - 1), (mg * 32) // (G - 1)))
        else:
            w2ks = []
        w2i = 0
        if w2ks:
            w2_start()
        for hc in range(32):
            hh = hc // 8
            bk = 4 + (hc % 4)
            def mm1(e, mg=mg, hc=hc, bk=bk):
                ins = None
                for c in range(8):
                    ins = e.matmul(bank(bk), w1qn(c, hc * 128, 128),
                                   p4_xqf2(mg, c), start=(c == 0), stop=(c == 7))
                return ins
            deps = [tk_xq[(mg, t_)] for t_ in range(4)] + \
                   [tk_w1q[(c, hh)] for c in range(8)] + [h_bank_free[bk]]
            tkm = b.step(TE, mm1, deps)
            last_mm1_of[mg] = tkm
            q = (hc // 4) % 2
            ciq = hc % 4
            tkg = b.step(S, lambda e, hc=hc, bk=bk, q=q, ciq=ciq: e.activation(
                p4_gsb(q)[:, ciq * 512:(ciq + 1) * 512], bank(bk),
                AF.Gelu, scale=s_qrep[:, 2:3], bias=s_b1[:, hc:hc + 1]),
                [tkm, tk_qrep1, tk_boot[BT_B1]] +
                ([gsb_free[q]] if ciq == 0 else []))
            h_bank_free[bk] = tkg
            tkmn = b.step(V, lambda e, hc=hc, q=q, ciq=ciq: (
                e.tensor_reduce(s_gmx[:, hc:hc + 1],
                                p4_gsb(q)[:, ciq * 512:(ciq + 1) * 512], AX.X, OP.max),
                e.tensor_reduce(s_gmn[:, hc:hc + 1],
                                p4_gsb(q)[:, ciq * 512:(ciq + 1) * 512], AX.X,
                                OP.min))[-1], [tkg])
            if ciq == 3:
                q4 = hc // 4
                tkst = b.dma((GP, SY, GP, S)[q4 % 4],
                             g_d[mg, q4 * 4:(q4 + 1) * 4].rearrange("q p c -> p q c"),
                             p4_gsb(q), [tkg], sem=d_gst[q4 % 4])
                tk_gst[(mg, q4)] = tkst
                gsb_free[q] = ("multi", tkst, tkmn)
            gmm_last = tkmn
            if G > 1 and w2i < len(w2ks) and hc % 2 == 1 and hc // 2 >= w2i:
                w2_emit(w2ks[w2i])
                w2i += 1
            if qg is not None and hc % 8 == 2:
                hcq = hc // 8
                q4_trn_ev(qg, hcq)
                if hcq < 2:
                    q4_clip(qg, hcq + 2)
            nq = mg + 2
            if nq < G:
                if hc == 13:
                    q4_pre(nq, 0)
                elif hc == 15:
                    q4_pre(nq, 1)
                elif hc == 19:
                    q4_clip(nq, 0)
                elif hc == 21:
                    q4_pre(nq, 2)
                elif hc == 27:
                    q4_clip(nq, 1)
                elif hc == 29:
                    q4_pre(nq, 3)
        while w2i < len(w2ks):
            w2_emit(w2ks[w2i])
            w2i += 1
        def fold(e):
            e.tensor_reduce(s_gtmp[:, :], s_gmx[:, :], AX.X, OP.max)
            e.drain()
            e.tensor_tensor(s_gmxr[:, :], s_gmxr[:, :], s_gtmp[:, :], OP.max)
            e.drain()
            e.tensor_reduce(s_gtmp[:, :], s_gmn[:, :], AX.X, OP.min)
            e.drain()
            return e.tensor_tensor(s_gmnr[:, :], s_gmnr[:, :], s_gtmp[:, :], OP.min)
        gmm_last = b.step(V, fold, [gmm_last, tk_memset])

    def gneg(e):
        e.tensor_single_scalar(s_gmnr[:, :], s_gmnr[:, :], -1.0, OP.mult)
        e.drain()
        e.tensor_copy(s_cand[:, 0:1], s_gmxr[:, :])
        return e.tensor_copy(s_cand[:, 1:2], s_gmnr[:, :])
    tk_gneg = b.step(V, gneg, [gmm_last])
    tk_gT = b.step(TE, lambda e: e.transpose(bank(7, 0, 2, 0, 128), s_cand[:, :],
                                             s_eyef[:, :]), [tk_gneg, tk_qrmm2, tk_qrep2])
    tk_g2 = b.step(V, lambda e: e.tensor_reduce(s_cred[:, :], bank(7, 0, 2, 0, 128),
                                                AX.X, OP.max), [tk_gT])
    tk_ar3s = b.dma(GP, ar3i.rearrange("a b -> b a"), s_cred[:, :], [tk_g2], sem=d_msc)
    tk_cc3 = b.collective("AllReduce", OP.max, ar3i, ar3o, [tk_ar3s])
    tk_ar3l = b.dma(GP, s_q2[0:1, 0:2], ar3o, [tk_cc3], sem=d_msc)

    def qs2(e):
        e.tensor_add(s_q2[0:1, 2:3], s_q2[0:1, 0:1], s_q2[0:1, 1:2])
        e.drain()
        e.tensor_scalar(s_q2[0:1, 2:3], s_q2[0:1, 2:3], 1.0 / 255.0, 1e-8,
                        OP.mult, OP.max)
        e.drain()
        e.reciprocal(s_q2[0:1, 3:4], s_q2[0:1, 2:3])
        e.drain()
        e.tensor_mul(s_q2[0:1, 4:5], s_q2[0:1, 1:2], s_q2[0:1, 3:4])
        e.drain()
        e.tensor_scalar(s_q2[0:1, 5:6], s_q2[0:1, 4:5], MAGIC, MAGIC,
                        OP.add, OP.subtract)
        e.drain()
        e.tensor_scalar(s_q2[0:1, 6:7], s_q2[0:1, 5:6], -128.0, None, OP.add)
        e.drain()
        e.tensor_scalar(s_q2[0:1, 6:7], s_q2[0:1, 6:7], -128.0, 127.0, OP.max, OP.min)
        e.drain()
        e.tensor_scalar(s_q2[0:1, 7:8], s_q2[0:1, 6:7], -1.0, -128.0, OP.mult, OP.add)
        e.tensor_scalar(s_q2[0:1, 8:9], s_q2[0:1, 6:7], -1.0, 127.0, OP.mult, OP.add)
        e.drain()
        return e.tensor_mul(s_q2[0:1, 9:10], s_q2[0:1, 2:3], s_q0[0:1, 3:4])
    tk_qs2 = b.step(V, qs2, [tk_ar3l, tk_sw])
    def qrmm3(e):
        e.matmul(bank(7, 0, 128, 0, 1), s_ones1f[0:1, :], s_q2[0:1, 3:4],
                 start=True, stop=True)
        return e.matmul(bank(7, 0, 128, 1, 3), s_ones1f[0:1, :], s_q2[0:1, 7:10],
                        start=True, stop=True)
    tk_qrmm3 = b.step(TE, qrmm3, [tk_qs2, tk_gT, tk_g2])
    tk_qrep3 = b.step(V, lambda e: (
        e.tensor_copy(s_qrep[:, 3:4], bank(7, 0, 128, 0, 1)),
        e.tensor_copy(s_qrep[:, 4:5], bank(7, 0, 128, 1, 1)),
        e.tensor_copy(s_qrep[:, 5:6], bank(7, 0, 128, 2, 1)),
        e.tensor_copy(s_qrep[:, 6:7], bank(7, 0, 128, 3, 1)))[-1], [tk_qrmm3])

    graw_free = [gsb_free[0], gsb_free[1]]
    gt0_free = [None, None]
    gf0_free = [None, None]
    mm2_bank_free = {6: None, 7: None}
    tm_bank_free = [None, None, None]
    ffn_free = [None] * 8
    out_free = [None, None]
    t25_free = [None, None]

    tk_gld = {}
    tk_gq = {}
    tk_x15 = {}
    tk_ffnev = {}
    tk_trnA = {}
    last_mm2_of = {}
    last_trn_of = {}

    def plan_gld(gi, q):
        sl_ = q % 2
        tk_gld[(gi, q)] = b.dma(
            SY if sl_ == 0 else S, p5_graw(sl_),
            g_d[gi, q * 4:(q + 1) * 4].rearrange("q p c -> p q c"),
            [tk_gst[(gi, q)], graw_free[sl_]], sem=d_gld[sl_])

    odd_g = [g_ for g_ in range(G) if g_ % 2 == 1]
    x15dep = last_mm1_of.get(odd_g[-1]) if odd_g else None
    x1q_seq = [(g_, ti) for g_ in range(G) for ti in range(4)]
    x1p = [0]
    def pump_x1(dep):
        if x1p[0] >= len(x1q_seq):
            return
        g_, ti = x1q_seq[x1p[0]]
        x1p[0] += 1
        tt = g_ * 4 + ti
        tk_x15[(g_, ti)] = b.dma(
            SY, p5_x1r(ti), x1_d[tt * 128:(tt + 1) * 128, :],
            [x15dep, tk_x1st[tt], dep], sem=d_x1r[2 + ti % 2])

    plan_gld(0, 0)
    plan_gld(0, 1)
    pump_x1(None)
    pump_x1(None)

    tk_bar45 = b.barrier()
    for it in range(G + 1):
        qg = it if it < G else None
        mg = it - 1 if it >= 1 else None
        for q in range(8):
            if qg is not None:
                for j in range(4):
                    k = q * 4 + j
                    st = k % 2
                    sl_ = q % 2
                    koff = j * 512
                    mbias = MAGIC if qg == 0 else 0.0
                    tkt0 = b.step(S, lambda e, sl_=sl_, koff=koff, st=st,
                                  mbias=mbias: e.activation(
                                      p5_gt0(st), p5_graw(sl_)[:, koff:koff + 512],
                                      AF.Identity, scale=s_qrep[:, 3:4],
                                      bias=mbias),
                                  [tk_gld[(qg, q)], tk_qrep3, gt0_free[st]])
                    if qg == 0:
                        tkf0 = b.step(S, lambda e, st=st: e.activation(
                            p5_gf0(st), p5_gt0(st), AF.Identity, bias=-MAGIC),
                            [tkt0, gf0_free[st]])
                    else:
                        tkf0 = b.step(V, lambda e, st=st: e.tensor_scalar(
                            p5_gf0(st), p5_gt0(st), MAGIC, MAGIC,
                            OP.add, OP.subtract),
                            [tkt0, gf0_free[st]])
                    gt0_free[st] = tkf0
                    tkq = b.step(V, lambda e, qg=qg, k=k, st=st: e.tensor_scalar(
                        gq2(qg, k), p5_gf0(st), s_qrep[:, 4:5], s_qrep[:, 5:6],
                        OP.max, OP.min),
                        [tkf0, tk_bar45, last_mm2_of.get(qg - 2)])
                    gf0_free[st] = tkq
                    tk_gq[(qg, k)] = tkq
                    if j == 3:
                        graw_free[sl_] = tkt0
                        gqq = qg * 8 + q + 2
                        if gqq < G * 8:
                            plan_gld(gqq // 8, gqq % 8)
            if mg is not None:
                dj = q
                bk = 6 + dj % 2
                def mm2(e, mg=mg, dj=dj, bk=bk, k=None):
                    if k is not None:
                        return e.matmul(bank(bk), w2qn(k, dj * 128, 128),
                                        gq2(mg, k), start=(k == 0), stop=(k == 31))
                    ins = None
                    for k_ in range(32):
                        ins = e.matmul(bank(bk), w2qn(k_, dj * 128, 128), gq2(mg, k_),
                                       start=(k_ == 0), stop=(k_ == 31))
                    return ins
                if mg == 0:
                    tkm = None
                    for k in range(32):
                        kdeps = [tk_gq[(mg, k)], tk_w2q[k]] + \
                                ([mm2_bank_free[bk]] if k == 0 else [])
                        tkm = b.step(TE, lambda e, mm2=mm2, k=k: mm2(e, k=k), kdeps)
                else:
                    deps = [tk_gq[(mg, k)] for k in range(32)] + \
                           [tk_w2q[k] for k in range(32)] + [mm2_bank_free[bk]]
                    tkm = b.step(TE, mm2, deps)
                last_mm2_of[mg] = tkm
                tkev = b.step(S, lambda e, dj=dj, bk=bk: e.activation(
                    p5_ffn(dj), bank(bk), AF.Identity, scale=s_qrep[:, 6:7],
                    bias=s_b2[:, dj:dj + 1]),
                    [tkm, tk_qrep3, tk_boot[BT_B2], ffn_free[dj]])
                mm2_bank_free[bk] = tkev
                tk_ffnev[dj] = tkev
                if dj == 3:
                    for ti_ in range(3):
                        tt_ = mg * 4 + ti_
                        b3_ = tt_ % 3
                        tmb_ = (PA, PB, PC)[b3_]
                        def trnha(e, ti_=ti_, tmb_=tmb_):
                            ins = None
                            for dj_ in range(4):
                                ins = e.transpose(
                                    tmb_[0:128, dj_ * 128:(dj_ + 1) * 128],
                                    p5_ffn(dj_)[:, ti_ * 128:(ti_ + 1) * 128],
                                    s_eyef[:, :])
                            return ins
                        tk_trnA[(mg, ti_)] = b.step(
                            TE, trnha,
                            [tk_ffnev[d_] for d_ in range(4)] +
                            [tk_boot[BT_EYEF], tm_bank_free[b3_]])
        if mg is not None:
            for ti in range(4):
                tt = mg * 4 + ti
                par = ti % 2
                b3 = tt % 3
                tmb = (PA, PB, PC)[b3]
                def trnh(e, ti=ti, tmb=tmb, djs=None):
                    ins = None
                    for dj in djs:
                        ins = e.transpose(tmb[0:128, dj * 128:(dj + 1) * 128],
                                          p5_ffn(dj)[:, ti * 128:(ti + 1) * 128],
                                          s_eyef[:, :])
                    return ins
                if (mg, ti) not in tk_trnA:
                    tk_trnA[(mg, ti)] = b.step(
                        TE, lambda e, trnh=trnh: trnh(e, djs=range(4)),
                        [tk_ffnev[dj] for dj in range(4)] +
                        [tk_boot[BT_EYEF], tm_bank_free[b3]])
                tktr = b.step(TE, lambda e, trnh=trnh: trnh(e, djs=range(4, 8)),
                              [tk_ffnev[dj] for dj in range(4, 8)] +
                              [tk_trnA[(mg, ti)]])
                last_trn_of[mg] = tktr
                def bn3(e, tmb=tmb):
                    e.bn_stats(s_bn[:, 0:6], tmb[0:128, 0:512])
                    e.bn_stats(s_bn[:, 6:12], tmb[0:128, 512:1024])
                    e.drain()
                    return e.bn_aggr(s_ag2[:, :], s_bn[:, :])
                tkbn = b.step(V, bn3, [tktr])
                def st3(e, ti=ti):
                    e.activation(s_m3[:, ti:ti + 1], s_ag2[:, 0:1], AF.Identity,
                                 scale=-1.0)
                    e.activation(s_r3[:, ti:ti + 1], s_ag2[:, 1:2], AF.Ln, bias=1e-5)
                    e.drain()
                    return e.activation(s_r3[:, ti:ti + 1], s_r3[:, ti:ti + 1],
                                        AF.Exp, scale=-0.5)
                tkst3 = b.step(S, st3, [tkbn])
                tkmr3 = b.step(V, lambda e, ti=ti: e.tensor_mul(
                    s_mr3[:, ti:ti + 1], s_m3[:, ti:ti + 1], s_r3[:, ti:ti + 1]),
                    [tkst3])
                tkt2 = b.step(S, lambda e, ti=ti, par=par: e.activation(
                    p5_t2(par), p5_x1r(ti), AF.Identity, bias=s_mr3[:, ti:ti + 1]),
                    [tk_x15[(mg, ti)], tkmr3, t25_free[par]])
                pump_x1(tkt2)
                def x2f(e, ti=ti, par=par, tmb=tmb):
                    return e.scalar_tensor_tensor(
                        out_sl(ti), tmb[0:128, 0:1024], s_r3[:, ti:ti + 1],
                        p5_t2(par), OP.mult, OP.add)
                tkx2 = b.step(V, x2f, [tkt2, tkbn, tkmr3, out_free[ti % 2]])
                t25_free[par] = tkx2
                tm_bank_free[b3] = tkx2
                tkos = b.dma(SY if ti % 2 == 0 else S,
                             out_d[tt * 128:(tt + 1) * 128, :], out_sl(ti),
                             [tkx2], sem=d_outs[ti % 2])
                out_free[ti % 2] = tkos
            for dj in range(8):
                ffn_free[dj] = last_trn_of[mg]

    b.step(SY, lambda e: None, [Tok(d_outs[0], d_outs[0].n, "_", True),
                                Tok(d_outs[1], d_outs[1].n, "_", True)])

    b.emit()
    return nc


def _host_layout(inputs, T):
    x = np.ascontiguousarray(np.asarray(inputs["L0_attn_x"], dtype=np.float32))
    Wxp = np.asarray(inputs["L0_attn_Wxproj"], dtype=np.float32)
    Wdt = np.asarray(inputs["L0_attn_Wdt"], dtype=np.float32)
    bdt = np.asarray(inputs["L0_attn_bdt"], dtype=np.float32)
    Wxs = np.asarray(inputs["L0_attn_Wxs"], dtype=np.float32)
    Alog = np.asarray(inputs["L0_attn_A_log"], dtype=np.float32)
    Wout = np.asarray(inputs["L0_attn_Wout"], dtype=np.float32)
    Dp = np.asarray(inputs["L0_attn_Dp"], dtype=np.float32)
    W1 = np.asarray(inputs["L0_ffn_W1"], dtype=np.float32)
    b1 = np.asarray(inputs["L0_ffn_b1"], dtype=np.float32)
    W2 = np.asarray(inputs["L0_ffn_W2"], dtype=np.float32)
    b2 = np.asarray(inputs["L0_ffn_b2"], dtype=np.float32)

    Bsz, L, D = x.shape
    xs = x.reshape(Bsz * L, D)
    wc_t = np.ascontiguousarray(np.concatenate([Wxp, Wxs], 0).T)
    w1t = np.ascontiguousarray(W1.T)
    w2t = np.ascontiguousarray(W2.T)
    d8 = np.zeros((128, DM), np.float32)
    idx = np.arange(128)
    for cb in range(8):
        d8[idx, cb * 128 + idx] = Dp[cb * 128:(cb + 1) * 128]
    bdiagp = np.zeros((NCH, 128), np.float32)
    for c in range(NCH):
        bdiagp[c, c * NST:(c + 1) * NST] = 1.0
    r48 = np.zeros((48, 3 * 128), np.float32)
    for c in range(NCH):
        for n in range(NST):
            r48[n, c * NST + n] = 1.0
            r48[16 + n, 128 + c * NST + n] = 1.0
            r48[32 + n, 256 + c * NST + n] = 1.0
    common = dict(
        wc_t=wc_t, wdt_c=np.ascontiguousarray(Wdt.reshape(64, 1)),
        bdt_i=bdt.reshape(1, 1), alog=Alog.reshape(1, NST),
        woutT=np.ascontiguousarray(Wout.T), dp_row=Dp.reshape(1, DM), d8=d8,
        w1t=w1t, b1sb=np.ascontiguousarray(b1.reshape(32, 128).T),
        w2t=w2t, b2sb=np.ascontiguousarray(b2.reshape(8, 128).T),
        eyef=np.eye(128, dtype=np.float32),
        eyeb=np.eye(128, dtype=ml_dtypes.bfloat16),
        bdiagp=bdiagp, bdiagn=-bdiagp, r48=r48,
    )
    in_maps = []
    for i in range(NCORES):
        shard = xs[i * T:(i + 1) * T]
        hsel = np.zeros((1, 128), np.float32)
        hsel[0, NST * i:NST * (i + 1)] = 1.0
        wsh = np.ascontiguousarray(np.concatenate(
            [w1t[i * 128:(i + 1) * 128, :],
             w2t[i * 512:(i + 1) * 512, :].reshape(128, 4096)], axis=1))
        m = dict(common)
        m.update(x_fm=np.ascontiguousarray(shard.T),
                 x_tm=np.ascontiguousarray(shard), hsel=hsel, wsh=wsh)
        in_maps.append(m)
    return in_maps, (Bsz, L, D)


def kernel(**inputs):
    x = np.asarray(inputs["L0_attn_x"])
    Bsz, L, D = x.shape
    T = (Bsz * L) // NCORES
    in_maps, shape = _host_layout(inputs, T)
    nc = build_program(T)
    res = run_bass_kernel_spmd(nc, in_maps, core_ids=list(range(NCORES)))
    out = np.concatenate([np.asarray(r["out"]) for r in res.results], axis=0)
    return out.reshape(Bsz, L, D).astype(np.float32)


# version 26
# speedup vs baseline: 1.0137x; 1.0137x over previous
import numpy as np
import ml_dtypes

import concourse.bass as bass
import concourse.mybir as mybir
from concourse.bass_utils import run_bass_kernel_spmd
from contextlib import ExitStack

F32 = mybir.dt.float32
BF16 = mybir.dt.bfloat16
AF = mybir.ActivationFunctionType
OP = mybir.AluOpType
AX = mybir.AxisListType

NCORES = 8
MAGIC = 12582912.0
DM = 1024
DH = 4096
NST = 16
NCH = 8


class Sem:
    __slots__ = ("h", "n")
    def __init__(self, h):
        self.h = h
        self.n = 0


class Tok:
    __slots__ = ("sem", "val", "eng", "dma")
    def __init__(self, sem, val, eng, dma=False):
        self.sem, self.val, self.eng, self.dma = sem, val, eng, dma


def flat(*deps):
    out = []
    for d in deps:
        if d is None or isinstance(d, str):
            continue
        if isinstance(d, (tuple, list)):
            out.extend(flat(*d))
        else:
            out.append(d)
    return out


class B:
    ENGINES = ("sync", "tensor", "vector", "scalar", "gpsimd")

    def __init__(self, nc):
        self.nc = nc
        self.ops = {e: [] for e in self.ENGINES}
        self.es = ExitStack()
        self.esem = {e: Sem(self.es.enter_context(nc.semaphore(f"e_{e}")))
                     for e in self.ENGINES}
        self.bar = Sem(self.es.enter_context(nc.semaphore("bar")))
        self.cc = Sem(self.es.enter_context(nc.semaphore("cc")))
        self.seen = {e: {} for e in self.ENGINES}

    def sem(self, name):
        return Sem(self.es.enter_context(self.nc.semaphore(name)))

    def sbuf(self, name, shape, dtype=F32):
        return self.es.enter_context(self.nc.sbuf_tensor(name, list(shape), dtype))

    def psum(self, name, shape):
        return self.es.enter_context(self.nc.psum_tensor(name, list(shape), F32))

    def _waits(self, eng, deps):
        waits = {}
        drain = False
        for d in flat(*deps):
            if d.eng == eng and not d.dma:
                drain = True
                continue
            sm = d.sem
            k = id(sm)
            if self.seen[eng].get(k, -1) >= d.val:
                continue
            if k in waits:
                waits[k] = (sm, max(waits[k][1], d.val))
            else:
                waits[k] = (sm, d.val)
        for k, (sm, v) in waits.items():
            self.seen[eng][k] = max(self.seen[eng].get(k, -1), v)
        return list(waits.values()), drain

    def step(self, eng, emit, deps=(), sem=None, inc=1, dma=False):
        waits, drain = self._waits(eng, deps)
        if sem is None:
            sem = self.esem[eng]
        sem.n += inc
        tok = Tok(sem, sem.n, eng, dma)
        def f(e, waits=waits, drain=drain, emit=emit, sem=sem, inc=inc):
            if drain:
                e.drain()
            for sm, v in waits:
                e.wait_ge(sm.h, v)
            ins = emit(e)
            if ins is not None:
                ins.then_inc(sem.h, inc)
            else:
                e.sem_inc(sem.h, inc)
        self.ops[eng].append(f)
        return tok

    def dma(self, eng, out, in_, deps=(), sem=None):
        deps = flat(*deps)
        if sem is not None and sem.n > 0:
            deps.append(Tok(sem, sem.n, "_", True))
        return self.step(eng, lambda e, o=out, i=in_: e.dma_start(out=o, in_=i),
                         deps, sem=sem, inc=16, dma=True)

    def barrier(self):
        self.bar.n += len(self.ENGINES)
        tgt = self.bar.n
        for eng in self.ENGINES:
            def f(e, tgt=tgt):
                e.sem_inc(self.bar.h, 1)
                e.wait_ge(self.bar.h, tgt)
            self.ops[eng].append(f)
            self.seen[eng][id(self.bar)] = tgt
        return Tok(self.bar, tgt, "_", True)

    def collective(self, kind, op, ins_, outs_, deps=()):
        def emit(e):
            return e.collective_compute(
                kind, op, replica_groups=[list(range(NCORES))],
                ins=[ins_.opt()], outs=[outs_.opt()])
        return self.step("gpsimd", emit, deps, sem=self.cc, inc=1, dma=True)

    def emit(self):
        with self.nc.Block() as block:
            for eng in self.ENGINES:
                ops = self.ops[eng]
                getattr(block, eng)(lambda e, ops=ops: [f(e) for f in ops])


def build_program(T):
    G = T // 512
    NT = T // 128
    LC = T // NCH
    nc = bass.Bass("TRN2", target_bir_lowering=False)
    b = B(nc)

    def dram(n, s, d=F32, k=None):
        t = nc.dram_tensor(n, list(s), d, kind=k) if k else nc.dram_tensor(n, list(s), d)
        return t.ap()

    x_fm = dram("x_fm", (DM, T), F32, "ExternalInput")
    x_tm = dram("x_tm", (T, DM), F32, "ExternalInput")
    wc_t = dram("wc_t", (DM, 112), F32, "ExternalInput")
    wdt_c = dram("wdt_c", (64, 1), F32, "ExternalInput")
    bdt_i = dram("bdt_i", (1, 1), F32, "ExternalInput")
    alog = dram("alog", (1, NST), F32, "ExternalInput")
    woutT = dram("woutT", (NST, DM), F32, "ExternalInput")
    dp_row = dram("dp_row", (1, DM), F32, "ExternalInput")
    d8 = dram("d8", (128, DM), F32, "ExternalInput")
    w1t = dram("w1t", (DM, DH), F32, "ExternalInput")
    b1sb = dram("b1sb", (128, 32), F32, "ExternalInput")
    w2t = dram("w2t", (DH, DM), F32, "ExternalInput")
    b2sb = dram("b2sb", (128, 8), F32, "ExternalInput")
    wsh_d = dram("wsh", (128, 8192), F32, "ExternalInput")
    eyef_d = dram("eyef", (128, 128), F32, "ExternalInput")
    eyeb_d = dram("eyeb", (128, 128), BF16, "ExternalInput")
    bdiagp_d = dram("bdiagp", (NCH, 128), F32, "ExternalInput")
    bdiagn_d = dram("bdiagn", (NCH, 128), F32, "ExternalInput")
    r48_d = dram("r48", (48, 3 * 128), F32, "ExternalInput")
    hsel_d = dram("hsel", (1, 128), F32, "ExternalInput")
    out_d = dram("out", (T, DM), F32, "ExternalOutput")

    rs48 = dram("rs48", (48, T))
    rowA = dram("rowA", (1, T))
    rowB = dram("rowB", (1, T))
    rowC = dram("rowC", (1, T))
    rowD = dram("rowD", (1, T))
    hcb = dram("hcb", (128, LC), BF16)
    h0b = dram("h0b", (1, NST))
    arow_b = dram("arow_b", (1, NST))
    x1_d = dram("x1_d", (T, DM))
    g_d = dram("g_d", (G, 32, 128, 512))
    ar0i = dram("ar0i", (1, 2)); ar0o = dram("ar0o", (1, 2))
    ag1i = dram("ag1i", (2, NST)); ag1o = dram("ag1o", (2 * NCORES, NST))
    ar2i = dram("ar2i", (1, 2)); ar2o = dram("ar2o", (1, 2))
    ar3i = dram("ar3i", (1, 2)); ar3o = dram("ar3o", (1, 2))

    sb = b.sbuf
    s_eyef = sb("s_eyef", (128, 128))
    s_eyeb = sb("s_eyeb", (128, 128), BF16)
    s_wdt = sb("s_wdt", (64, 1))
    s_bdt = sb("s_bdt", (1, 1))
    s_bdt8 = sb("s_bdt8", (NCH, 1))
    s_alog = sb("s_alog", (1, NST))
    s_apos = sb("s_apos", (1, NST))
    s_ones1f = sb("s_ones1f", (1, 128))
    s_hsel = sb("s_hsel", (1, 128))
    CS2 = sb("CS2", (NCH, 3 * 128))
    s_atil = CS2[:, 0:128]
    s_bdiagn = CS2[:, 128:256]
    s_bdA = CS2[:, 256:384]
    s_bdiagp = sb("s_bdiagp", (NCH, 128))
    s_r48 = sb("s_r48", (112, 3 * 128))
    s_r48B = s_r48[64:112, 0:128]
    s_r48C = s_r48[64:112, 128:256]
    s_r48X = s_r48[64:112, 256:384]
    CSB = sb("CSB", (128, DM), BF16)
    s_woutb = CSB[0:NST, :]
    s_dpb = CSB[32:33, :]
    s_b1 = sb("s_b1", (128, 32))
    s_b2 = sb("s_b2", (128, 8))
    s_onesb = sb("s_onesb", (128, 1), BF16)

    AR0 = sb("AR0", (128, max(8 * T, 32 * DM)), BF16)
    AR1 = sb("AR1", (128, max(8 * T, 8 * DH)), BF16)
    AR3 = sb("AR3", (128, 14336), F32)
    BFX = sb("BFX", (128, 6144), BF16)
    AR1f = AR1[:, :].bitcast(F32)

    def a0(c0, w, p0=0, p1=128):
        return AR0[p0:p1, c0:c0 + w]
    def a1(c0, w, p0=0, p1=128):
        return AR1[p0:p1, c0:c0 + w]
    def a1f(c0, w, p0=0, p1=128):
        return AR1f[p0:p1, c0:c0 + w]
    def a3(c0, w, p0=0, p1=128):
        return AR3[p0:p1, c0:c0 + w]
    def bfx(c0, w, p0=0, p1=128):
        return BFX[p0:p1, c0:c0 + w]

    xf = lambda c, g0=0, w=None: a0(c * T + g0, w if w is not None else T)
    w2qn = lambda k, d0=0, w=DM: a0(k * DM + d0, w)
    xsq = lambda c, g0=0, w=None: a1(((c + 4) % 8) * T + g0,
                                     w if w is not None else T)
    w1qn = lambda c, h0=0, w=DH: a1(c * DH + h0, w)
    out_sl = lambda i: a1f(8192 + (i % 2) * DM, DM)

    def gq2(g_, k):
        if g_ % 2 == 0:
            return a1(k * 512, 512)
        if k < 24:
            return a1(20480 + k * 512, 512)
        return bfx((k - 24) * 512, 512)
    wsh_s = a1f(0, 8192)
    p3_xt = lambda i: a1f(i * DM, DM)
    p3_x1 = lambda i: a1f(2048 + i * DM, DM)
    p3_attn = lambda i: a1f(4096 + i * DM, DM)
    p3_t2 = lambda i: a1f(8192 + i * DM, DM)
    p3_trash = a1f(10240, DM)

    rp = lambda p0, p1, c0=0, w=None: a3(c0, w if w is not None else T, p0, p1)
    SL2 = max(T, 8 * LC)
    def sl(i, p0=0, p1=128):
        if i == 10:
            return a3(13584, LC, p0, p1)
        if i < 8:
            return a3(i * LC, LC, p0, p1)
        return a3(SL2 + (i - 8) * LC, LC, p0, p1)
    RW = SL2 + 3 * LC
    r_m8 = a3(RW + 0 * LC, LC, 0, NCH)
    r_r8 = a3(RW + 1 * LC, LC, 0, NCH)
    r_sq8 = a3(RW + 2 * LC, LC, 0, NCH)
    r_dt8 = r_sq8
    r_dtraw = a3(RW + 3 * LC, LC, 0, NCH)
    r_tmp8 = a3(RW + 4 * LC, LC, 0, NCH)
    SM = RW + 5 * LC
    rrow = a3(SM, T, 0, 1)
    mrow_neg = a3(SM, T, 32, 33)
    msum_row = mrow_neg
    sqsum_row = a3(SM, T, 64, 65)
    dtraw_row = a3(SM, T, 96, 97)
    MS1 = SM + T
    pfinrow = a3(MS1, 128, 0, 1)
    bfinrow = a3(MS1 + 128, 128, 0, 1)
    pincrow = a3(MS1 + 256, 128, 0, 1)
    bincrow = a3(MS1 + 384, 128, 0, 1)
    aggrow = a3(MS1 + 512, 256, 0, 1)
    carrrow = a3(MS1 + 768, 128, 0, 1)
    runrow = a3(MS1 + 896, NST, 0, 1)
    runtrow = a3(MS1 + 912, NST, 0, 1)
    h0row = a3(MS1 + 928, NST, 0, 1)
    h0til = a3(MS1 + 944, 112, 0, 1)
    tmp112 = a3(MS1 + 1056, 112, 0, 1)
    hinrow = a3(MS1 + 1168, 128, 0, 1)
    assert MS1 + 1296 <= 14336, MS1 + 1296

    p4_gsb = lambda q: a3(q * 2048, 2048)
    p4_w1raw = lambda i: a3(4096 + i * DM, DM) if i < 2 else a3(1024 + (i - 2) * DM, DM)
    p4_w1t0 = a3(6144, DM)
    p4_w2raw = lambda i: a3(7168, DM)
    p4_w2t0 = p4_w1t0
    p4_x1r = lambda i: a3(8192 + i * DM, DM)
    p4_t0 = lambda i: a3(10240 + i * DM, DM)
    p5_graw = lambda i: a3(0 + i * 2048, 2048)
    p5_ffn = lambda dj: a3(4096 + dj * 512, 512)
    p5_t2 = lambda i: a3(8192 + i * DM, DM)
    p5_gt0 = lambda i: a3(10240 + i * 512, 512)
    p5_gf0 = lambda i: a3(11264 + i * 512, 512)
    p5_x1r = lambda i: a3(12288 + (i % 2) * DM, DM)

    hcf = lambda t0=0, w=None: bfx(t0, w if w is not None else T, 0, NST)
    s_mrowb = bfx(0, T, 32, 33)
    s_wcb = bfx(0, 8 * 112)
    s_d8b = bfx(4096, DM)
    p4_xqn = lambda i: bfx(i * DM, DM)
    AR3B = AR3[:, :].bitcast(BF16)
    def p4_xqf2(g_, c):
        if g_ % 2 == 0:
            return bfx(2048 + c * 512, 512)
        return AR3B[:, 24576 + c * 512:24576 + (c + 1) * 512]
    def p4_xqf_dst(g_, ti):
        base = BFX[:, 2048:6144] if g_ % 2 == 0 else AR3B[:, 24576:28672]
        return base.rearrange("p (c w) -> p c w", c=8)[:, :, ti * 128:(ti + 1) * 128]

    s_rt = sb("s_rt", (128, NT))
    s_aq = sb("s_aq", (128, NT)); s_bq = sb("s_bq", (128, NT))
    _p3s = lambda i: a3(i * 64, NT)
    s_suma = _p3s(0); s_sqa = _p3s(1)
    s_m4 = _p3s(2); s_r4 = _p3s(3); s_mr4 = _p3s(4)
    s_sum2 = _p3s(5); s_sq2 = _p3s(6)
    s_m2 = _p3s(7); s_r2 = _p3s(8)
    s_minT = _p3s(9); s_maxT = _p3s(10)
    s_tt1 = _p3s(11); s_tt2 = _p3s(12)
    s_cand = sb("s_cand", (128, 2)); s_cred = sb("s_cred", (2, 1))
    s_gmx = sb("s_gmx", (128, 32)); s_gmn = sb("s_gmn", (128, 32))
    s_gmxr = sb("s_gmxr", (128, 1)); s_gmnr = sb("s_gmnr", (128, 1))
    s_gtmp = sb("s_gtmp", (128, 1))
    s_wsc = sb("s_wsc", (112, 1))
    s_c08 = sb("s_c08", (NCH, 1))
    s_wsB = sb("s_wsB", (128, 1)); s_wsC = sb("s_wsC", (128, 1)); s_wsX = sb("s_wsX", (128, 1))
    fincol = sb("fincol", (128, 2))
    hincol = sb("hincol", (128, 1))
    s_q0 = sb("s_q0", (1, 16))
    s_q2 = sb("s_q2", (1, 16))
    s_qrep = sb("s_qrep", (128, 8))
    s_wq = sb("s_wq", (128, 2))
    s_bn = sb("s_bn", (128, 12)); s_ag2 = sb("s_ag2", (128, 2))
    s_kc = sb("s_kc", (128, 8))
    KCONST = [0.0, 1.0, 0.5, 1e-5, -0.5, -1.0, -1.0 / DM, 1.0 / DM]
    for _i, _v in enumerate(KCONST):
        nc.const_aps.register(F32, _v, s_kc[:, _i:_i + 1])
    s_m3 = sb("s_m3", (128, 4)); s_r3 = sb("s_r3", (128, 4)); s_mr3 = sb("s_mr3", (128, 4))

    PA = b.psum("PA", (128, 1024)); PB = b.psum("PB", (128, 1024))
    PC = b.psum("PC", (128, 1024)); PD = b.psum("PD", (128, 1024))
    def bank(i, p0=0, p1=128, c0=0, w=512):
        t_ = (PA, PB, PC, PD)[i // 2]
        return t_[p0:p1, (i % 2) * 512 + c0:(i % 2) * 512 + c0 + w]

    d_boot = b.sem("d_boot"); d_bootg = b.sem("d_bootg")
    d_h1 = b.sem("d_h1"); d_h2 = b.sem("d_h2")
    d_xf = b.sem("d_xf"); d_msc = b.sem("d_msc"); d_mscs = b.sem("d_mscs")
    d_xt = [b.sem(f"d_xt{i}") for i in range(2)]
    d_x1s = b.sem("d_x1s")
    d_x1r = [b.sem(f"d_x1r{i}") for i in range(4)]
    d_w1 = [b.sem(f"d_w1{i}") for i in range(4)]
    d_w2 = [b.sem("d_w20"), b.sem("d_w21")]
    d_gst = [b.sem(f"d_gst{i}") for i in range(4)]
    d_gld = [b.sem(f"d_gld{i}") for i in range(2)]
    d_outs = [b.sem("d_outs0"), b.sem("d_outs1")]

    V, S, TE, GP, SY = "vector", "scalar", "tensor", "gpsimd", "sync"

    tk_boot = []
    boot_specs = [
        (GP, s_eyef[:, :], eyef_d), (GP, s_eyeb[:, :], eyeb_d),
        (GP, s_wdt[:, :], wdt_c), (GP, s_bdt[:, :], bdt_i),
        (GP, s_alog[:, :], alog), (GP, s_b1[:, :], b1sb), (GP, s_b2[:, :], b2sb),
        (GP, s_bdiagp[:, :], bdiagp_d), (GP, s_bdiagn[:, :], bdiagn_d),
        (GP, s_r48[64:112, :], r48_d), (GP, s_hsel[:, :], hsel_d),
        ("wsh", None, None),
        (GP, s_wcb[:, :].rearrange("p (c w) -> p c w", c=8),
         wc_t.rearrange("(c p) w -> p c w", c=8)),
        (GP, s_woutb[:, :], woutT), (GP, s_dpb[:, :], dp_row), (GP, s_d8b[:, :], d8),
    ]
    bsems = [d_boot, d_bootg, b.sem("d_boot2")]
    d_wsh = [b.sem("d_wsh0"), b.sem("d_wsh1")]
    tk_wsh = ("multi",
              b.dma(SY, wsh_s[:, 0:4096], wsh_d[:, 0:4096], sem=d_wsh[0]),
              b.dma(S, wsh_s[:, 4096:8192], wsh_d[:, 4096:8192], sem=d_wsh[1]))
    gpi = [0]
    for qe, dst, src in boot_specs:
        if qe == "wsh":
            tk_boot.append(tk_wsh)
        else:
            tk_boot.append(b.dma(GP, dst, src, sem=bsems[gpi[0] % 3]))
            gpi[0] += 1
    (BT_EYEF, BT_EYEB, BT_WDT, BT_BDT, BT_ALOG, BT_B1, BT_B2, BT_BDP,
     BT_BDN, BT_R48, BT_HSEL, BT_WSH, BT_WCB, BT_WOUT, BT_DPB,
     BT_D8B) = range(16)
    NXS = 3 if 3 * T <= 12288 else 2
    xstage = lambda c: a3((c % NXS) * T, T)
    d_xs = [b.sem(f"d_xs{i}") for i in range(NXS)]
    tk_xfld = {}
    xfp = [0]
    def pump_xf(dep):
        if xfp[0] >= 8:
            return
        c = xfp[0]
        xfp[0] += 1
        tk_xfld[c] = b.dma(SY if c % 2 == 0 else S, xstage(c),
                           x_fm[c * 128:(c + 1) * 128, :], [dep],
                           sem=d_xs[c % NXS])
    for _ in range(NXS):
        pump_xf(None)

    def memsets(e):
        e.memset(s_onesb[:, :], 1.0)
        e.memset(s_ones1f[:, :], 1.0)
        e.memset(sl(10), 0.0)
        e.memset(s_gmxr[:, :], -3e38)
        e.memset(s_gmnr[:, :], 3e38)
        ins = None
        for _i, _v in enumerate(KCONST):
            ins = e.memset(s_kc[:, _i:_i + 1], _v)
        return ins
    tk_memset = b.step(V, memsets)
    b.step(S, lambda e: None, [tk_memset])

    tk_sq = []
    tk_xf = []
    def cast_sq(c):
        tkc = b.step(V, lambda e, c=c: e.tensor_copy(xf(c), xstage(c)),
                     [tk_xfld[c]])
        tks = b.step(S, lambda e, c=c: e.activation(
            xsq(c), xstage(c), AF.Square),
            [tk_xfld[c]] + ([tk_wred] if ((c + 4) % 8) * T < 16384 else []))
        tk_xf.append(tkc)
        tk_sq.append(tks)
        pump_xf(("multi", tkc, tks))
    def wred(e):
        e.tensor_reduce(s_cand[:, 0:1], wsh_s[:, 0:4096], AX.X, OP.max,
                        apply_absolute_value=True)
        return e.tensor_reduce(s_cand[:, 1:2], wsh_s[:, 4096:8192], AX.X, OP.max,
                               apply_absolute_value=True)
    if T >= 4096:
        for c in range(4):
            cast_sq(c)
        tk_wred = b.step(V, wred, [tk_boot[BT_WSH]])
        for c in range(4, 8):
            cast_sq(c)
    else:
        tk_wred = b.step(V, wred, [tk_boot[BT_WSH]])
        for c in range(8):
            cast_sq(c)
    tk_stgdead = ("multi", tk_sq[6], tk_sq[7], tk_xf[6], tk_xf[7])

    tk_wredT = b.step(TE, lambda e: e.transpose(bank(6, 0, 2, 0, 128), s_cand[:, :],
                                                s_eyef[:, :]), [tk_wred, tk_boot[BT_EYEF]])
    tk_wcred = b.step(V, lambda e: e.tensor_reduce(s_cred[:, :], bank(6, 0, 2, 0, 128),
                                                   AX.X, OP.max), [tk_wredT])
    tk_ar0s = b.dma(GP, ar0i.rearrange("a b -> b a"), s_cred[:, :], [tk_wcred], sem=d_msc)
    tk_cc0 = b.collective("AllReduce", OP.max, ar0i, ar0o, [tk_ar0s])
    tk_ar0l = b.dma(GP, s_q0[0:1, 0:2], ar0o, [tk_cc0], sem=d_msc)
    tk_woutneg = b.step(V, lambda e: e.tensor_single_scalar(
        s_woutb[:, :], s_woutb[:, :], -1.0, OP.mult), [tk_boot[BT_WOUT]])
    tk_apos = b.step(S, lambda e: e.activation(s_apos[:, :], s_alog[:, :], AF.Exp),
                     [tk_boot[BT_ALOG]])
    tk_ab = b.dma(GP, arow_b, s_apos[:, :], [tk_apos], sem=d_msc)
    tk_atil = b.dma(GP, s_atil[:, :].rearrange("c (r t) -> c r t", r=8),
                    arow_b.broadcast_to((NCH * 8, NST)).rearrange("(c r) t -> c r t", c=NCH),
                    [tk_ab], sem=d_msc)
    tk_bdA = b.step(V, lambda e: e.tensor_mul(s_bdA[:, :], s_atil[:, :], s_bdiagn[:, :]),
                    [tk_atil, tk_boot[BT_BDN]])
    tk_bdt8mm = b.step(TE, lambda e: e.matmul(bank(4, 0, NCH, 0, 1), s_ones1f[0:1, 0:NCH],
                                              s_bdt[:, :], start=True, stop=True),
                       [tk_boot[BT_BDT], tk_memset])
    tk_bdt8 = b.step(V, lambda e: e.tensor_copy(s_bdt8[:, :], bank(4, 0, NCH, 0, 1)),
                     [tk_bdt8mm])

    tk_mevac = []
    prev = {0: None, 1: None, 2: None, 3: None}
    for gi in range(G):
        bk = gi % 4
        def mm(e, gi=gi, bk=bk):
            ins = None
            for c in range(8):
                ins = e.matmul(bank(bk, 0, 1), s_onesb[:, :],
                               xf(c, gi * 512, 512), start=(c == 0), stop=(c == 7))
            return ins
        tkm = b.step(TE, mm, [tk_xf[7], tk_memset, prev[bk]])
        tkev = b.step(V, lambda e, gi=gi, bk=bk: e.tensor_copy(
            msum_row[0:1, gi * 512:(gi + 1) * 512], bank(bk, 0, 1)), [tkm])
        prev[bk] = tkev
        tk_mevac.append(tkev)
    tk_sqevac = []
    for gi in range(G):
        bk = gi % 4
        def mm(e, gi=gi, bk=bk):
            ins = None
            for c in range(8):
                ins = e.matmul(bank(bk, 0, 1), s_onesb[:, :],
                               xsq(c, gi * 512, 512), start=(c == 0), stop=(c == 7))
            return ins
        tkm = b.step(TE, mm, [tk_sq[7], prev[bk]])
        tkev = b.step(V, lambda e, gi=gi, bk=bk: e.tensor_copy(
            sqsum_row[0:1, gi * 512:(gi + 1) * 512], bank(bk, 0, 1)), [tkm])
        prev[bk] = tkev
        tk_sqevac.append(tkev)
    sq_done = tk_sqevac[-1]

    tk_rpev = []
    prevrp = {4: None, 5: None, 6: tk_wcred, 7: None}
    for gi in range(G):
        bk = 4 + gi % 4
        def mm(e, gi=gi, bk=bk):
            ins = None
            for c in range(8):
                ins = e.matmul(bank(bk, 0, 112), s_wcb[:, c * 112:(c + 1) * 112],
                               xf(c, gi * 512, 512), start=(c == 0), stop=(c == 7))
            return ins
        tkm = b.step(TE, mm, [tk_xf[7], tk_sq[6], tk_boot[BT_WCB], prevrp[bk]])
        tkev = b.step(V, lambda e, gi=gi, bk=bk: e.tensor_copy(
            rp(0, 112, gi * 512, 512), bank(bk, 0, 112)), [tkm])
        prevrp[bk] = tkev
        tk_rpev.append(tkev)

    def mm_wsum(e):
        ins = None
        for c in range(8):
            ins = e.matmul(bank(4, 0, 112, 0, 1), s_wcb[:, c * 112:(c + 1) * 112],
                           s_onesb[:, :], start=(c == 0), stop=(c == 7))
        return ins
    tk_wsmm = b.step(TE, mm_wsum, [tk_boot[BT_WCB], tk_memset, tk_bdt8,
                                   tk_rpev[-2] if G >= 2 else tk_rpev[-1]])
    tk_wsev = b.step(V, lambda e: e.tensor_copy(s_wsc[:, :], bank(4, 0, 112, 0, 1)),
                     [tk_wsmm])
    def mm_ws2(e):
        e.matmul(bank(4, 0, 1, 0, 1), s_wdt[:, :], s_wsc[0:64, 0:1], start=True, stop=True)
        e.matmul(bank(5, 0, 128, 0, 1), s_r48B[:, :], s_wsc[64:112, 0:1],
                 start=True, stop=True)
        e.matmul(bank(5, 0, 128, 1, 1), s_r48C[:, :], s_wsc[64:112, 0:1],
                 start=True, stop=True)
        return e.matmul(bank(5, 0, 128, 2, 1), s_r48X[:, :], s_wsc[64:112, 0:1],
                        start=True, stop=True)
    tk_ws2 = b.step(TE, mm_ws2, [tk_wsev, tk_boot[BT_WDT], tk_boot[BT_R48],
                                 tk_rpev[-1]])
    tk_wsv = b.step(V, lambda e: (
        e.tensor_copy(s_wsB[:, :], bank(5, 0, 128, 0, 1)),
        e.tensor_copy(s_wsC[:, :], bank(5, 0, 128, 1, 1)),
        e.tensor_copy(s_wsX[:, :], bank(5, 0, 128, 2, 1)))[-1], [tk_ws2])
    tk_wsneg = b.step(V, lambda e: (
        e.tensor_single_scalar(s_wsB[:, :], s_wsB[:, :], -1.0, OP.mult),
        e.tensor_single_scalar(s_wsC[:, :], s_wsC[:, :], -1.0, OP.mult),
        e.tensor_single_scalar(s_wsX[:, :], s_wsX[:, :], -1.0, OP.mult))[-1], [tk_wsv])
    tk_c0ev = b.step(V, lambda e: e.tensor_copy(s_q0[0:1, 6:7], bank(4, 0, 1, 0, 1)),
                     [tk_ws2])
    tk_c0mm = b.step(TE, lambda e: e.matmul(bank(4, 0, NCH, 4, 1), s_ones1f[0:1, 0:NCH],
                                            s_q0[0:1, 6:7], start=True, stop=True),
                     [tk_c0ev])
    tk_c08 = b.step(V, lambda e: e.tensor_copy(s_c08[:, :], bank(4, 0, NCH, 4, 1)),
                    [tk_c0mm])

    tk_dtev = []
    prevdt = {0: prev[0], 1: prev[1]}
    for gi in range(G):
        bk = gi % 2
        tkm = b.step(TE, lambda e, gi=gi, bk=bk: e.matmul(
            bank(bk, 0, 1), s_wdt[:, :], rp(0, 64, gi * 512, 512),
            start=True, stop=True),
            [tk_rpev[gi], tk_boot[BT_WDT], prevdt[bk]])
        tkev = b.step(V, lambda e, gi=gi, bk=bk: e.tensor_copy(
            dtraw_row[0:1, gi * 512:(gi + 1) * 512], bank(bk, 0, 1)), [tkm])
        prevdt[bk] = tkev
        tk_dtev.append(tkev)

    def swcalc(e):
        e.tensor_scalar(s_q0[0:1, 2:4], s_q0[0:1, 0:2], 1.0 / 127.0, 1e-8, OP.mult, OP.max)
        e.drain()
        return e.reciprocal(s_q0[0:1, 4:6], s_q0[0:1, 2:4])
    tk_sw = b.step(V, swcalc, [tk_ar0l])
    tk_swmm = b.step(TE, lambda e: e.matmul(bank(6, 0, 128, 0, 2), s_ones1f[0:1, :],
                                            s_q0[0:1, 4:6], start=True, stop=True),
                     [tk_sw, tk_dtev[-1]])
    tk_wq = b.step(V, lambda e: e.tensor_copy(s_wq[:, :], bank(6, 0, 128, 0, 2)), [tk_swmm])

    tk_br1 = b.dma(SY, rowA, msum_row, [tk_mevac[-1]], sem=d_h1)
    tk_br2 = b.dma(SY, r_m8, rowA.rearrange("o (c t) -> (o c) t", c=8),
                   [tk_br1, tk_stgdead], sem=d_h1)
    tk_br3 = b.dma(S, rowB, sqsum_row, [tk_sqevac[-1]], sem=d_h2)
    tk_br4 = b.dma(S, r_sq8, rowB.rearrange("o (c t) -> (o c) t", c=8),
                   [tk_br3, tk_stgdead], sem=d_h2)
    tk_br5 = b.dma(SY, rowC, dtraw_row, [tk_dtev[-1]], sem=d_h1)
    tk_br6 = b.dma(SY, r_dtraw, rowC.rearrange("o (c t) -> (o c) t", c=8),
                   [tk_br5, tk_stgdead], sem=d_h1)

    tk_m8 = b.step(S, lambda e: e.activation(r_m8, r_m8, AF.Identity, scale=-1.0 / DM),
                   [tk_br2])
    tk_e2 = b.step(S, lambda e: e.activation(r_sq8, r_sq8, AF.Identity, scale=1.0 / DM),
                   [tk_br4])
    tk_msq = b.step(V, lambda e: e.tensor_mul(r_tmp8, r_m8, r_m8), [tk_m8])
    tk_var = b.step(V, lambda e: e.tensor_sub(r_sq8, r_sq8, r_tmp8), [tk_e2, tk_msq])
    tk_lnv = b.step(S, lambda e: e.activation(r_tmp8, r_sq8, AF.Ln, bias=1e-5), [tk_var])
    tk_r8 = b.step(S, lambda e: e.activation(r_r8, r_tmp8, AF.Exp, scale=-0.5), [tk_lnv])

    tk_dt1 = b.step(V, lambda e: e.scalar_tensor_tensor(
        r_tmp8, r_m8, s_c08[:, :], r_dtraw, OP.mult, OP.add),
        [tk_m8, tk_br6, tk_c08, tk_r8])
    tk_dt2 = b.step(V, lambda e: e.tensor_mul(r_tmp8, r_tmp8, r_r8), [tk_dt1, tk_r8])
    tk_dt3 = b.step(S, lambda e: e.activation(r_dt8, r_tmp8, AF.Exp, bias=s_bdt8[:, :]),
                    [tk_dt2, tk_bdt8])
    tk_dt4 = b.step(S, lambda e: e.activation(r_dt8, r_dt8, AF.Ln, bias=1.0), [tk_dt3])

    tk_bx1 = b.dma(S, rs48, rp(64, 112), [tk_rpev[-1]], sem=d_h2)
    tk_Br = b.dma(SY, sl(4), rs48[0:16, :].rearrange("n (c t) -> c n t", c=8),
                  [tk_bx1, tk_dtev[-1], tk_stgdead], sem=d_h1)
    tk_Cr = b.dma(S, sl(5), rs48[16:32, :].rearrange("n (c t) -> c n t", c=8),
                  [tk_bx1, tk_dtev[-1], tk_stgdead], sem=d_h2)
    tk_Xr = b.dma(SY, sl(6), rs48[32:48, :].rearrange("n (c t) -> c n t", c=8),
                  [tk_bx1, tk_dtev[-1], tk_stgdead], sem=d_h1)

    tk_op1 = b.step(TE, lambda e: e.matmul(bank(5, 0, 128, 0, LC), s_bdA[:, :], r_dt8,
                                           start=True, stop=True),
                    [tk_dt4, tk_bdA, tk_wsv])
    tk_Abar = b.step(S, lambda e: e.activation(sl(0), bank(5, 0, 128, 0, LC), AF.Exp),
                     [tk_op1, tk_bx1, tk_dtev[-1]])
    tk_op2 = b.step(TE, lambda e: e.matmul(bank(6, 0, 128, 0, LC), s_bdiagp[:, :], r_dt8,
                                           start=True, stop=True),
                    [tk_dt4, tk_boot[BT_BDP], tk_dtev[-1]])
    tk_dts = b.step(V, lambda e: e.tensor_copy(sl(1), bank(6, 0, 128, 0, LC)),
                    [tk_op2, tk_bx1])
    tk_op3 = b.step(TE, lambda e: e.matmul(bank(0, 0, 128, 0, LC), s_bdiagp[:, :], r_m8,
                                           start=True, stop=True),
                    [tk_m8, tk_op1, tk_dtev[-1]])
    tk_ms = b.step(V, lambda e: e.tensor_copy(sl(2), bank(0, 0, 128, 0, LC)),
                   [tk_op3, tk_bx1])
    tk_op4 = b.step(TE, lambda e: e.matmul(bank(1, 0, 128, 0, LC), s_bdiagp[:, :], r_r8,
                                           start=True, stop=True),
                    [tk_r8, tk_op2, tk_dtev[-1]])
    tk_rs = b.step(V, lambda e: e.tensor_copy(sl(3), bank(1, 0, 128, 0, LC)),
                   [tk_op4, tk_bx1])
    def dtr2(e):
        e.tensor_mul(sl(1), sl(1), sl(3))
        e.drain()
        return e.tensor_mul(sl(1), sl(1), sl(3))
    tk_dtr2s = b.step(V, dtr2, [tk_dts, tk_rs])

    tk_Bc = b.step(V, lambda e: e.scalar_tensor_tensor(
        sl(4), sl(2), s_wsB[:, :], sl(4), OP.mult, OP.subtract),
        [tk_ms, tk_Br, tk_wsneg])
    tk_Xc = b.step(V, lambda e: e.scalar_tensor_tensor(
        sl(6), sl(2), s_wsX[:, :], sl(6), OP.mult, OP.subtract),
        [tk_ms, tk_Xr, tk_wsneg])
    tk_Bx1 = b.step(V, lambda e: e.tensor_mul(sl(7), sl(4), sl(6)), [tk_Bc, tk_Xc])
    tk_Bx = b.step(V, lambda e: e.tensor_mul(sl(7), sl(7), sl(1)), [tk_Bx1, tk_dtr2s])

    tk_hloc = b.step(V, lambda e: e.tensor_tensor_scan(
        sl(8), sl(0), sl(7), 0.0, OP.mult, OP.add), [tk_Abar, tk_Bx])
    tk_cumP = b.step(V, lambda e: e.tensor_tensor_scan(
        sl(9), sl(0), sl(10), 1.0, OP.mult, OP.add), [tk_hloc, tk_memset])

    tk_fin = b.step(V, lambda e: (
        e.tensor_copy(fincol[:, 0:1], sl(9)[:, LC - 1:LC]),
        e.tensor_copy(fincol[:, 1:2], sl(8)[:, LC - 1:LC]))[-1], [tk_cumP])
    def finT(e):
        e.transpose(bank(7, 0, 1, 0, 128), fincol[:, 0:1], s_eyef[:, :])
        return e.transpose(bank(7, 0, 1, 128, 128), fincol[:, 1:2], s_eyef[:, :])
    tk_finT = b.step(TE, finT, [tk_fin])
    tk_finr = b.step(V, lambda e: (
        e.tensor_copy(pfinrow[:, :], bank(7, 0, 1, 0, 128)),
        e.tensor_copy(bfinrow[:, :], bank(7, 0, 1, 128, 128)))[-1], [tk_finT])

    def sumscan(e):
        Prow = pfinrow[0:1, :].rearrange("o (c n) -> o c n", c=8)
        Brow = bfinrow[0:1, :].rearrange("o (c n) -> o c n", c=8)
        pv = pincrow[0:1, :].rearrange("o (c n) -> o c n", c=8)
        bv = bincrow[0:1, :].rearrange("o (c n) -> o c n", c=8)
        ins = None
        for n in range(NST):
            e.tensor_tensor_scan(pv[:, :, n], Prow[:, :, n], sl(10)[0:1, 0:NCH],
                                 1.0, OP.mult, OP.add)
        for n in range(NST):
            ins = e.tensor_tensor_scan(bv[:, :, n], Prow[:, :, n], Brow[:, :, n],
                                       0.0, OP.mult, OP.add)
        return ins
    tk_sscan = b.step(V, sumscan, [tk_finr, tk_memset])

    tk_ag_s1 = b.dma(SY, ag1i[0:1, :], pincrow[0:1, 112:128], [tk_sscan], sem=d_h1)
    tk_ag_s2 = b.dma(S, ag1i[1:2, :], bincrow[0:1, 112:128], [tk_sscan], sem=d_h2)
    tk_cc1 = b.collective("AllGather", OP.bypass, ag1i, ag1o, [tk_ag_s1, tk_ag_s2])
    tk_agl = b.dma(SY, aggrow, ag1o.rearrange("(o a) b -> o (a b)", o=1), [tk_cc1],
                   sem=d_h1)

    def combine(e):
        e.memset(runrow[:, :], 0.0)
        for j in range(NCORES):
            if j == 4:
                e.drain()
                e.memset(runrow[:, :], 0.0)
            e.drain()
            e.tensor_copy(carrrow[0:1, NST * j:NST * (j + 1)], runrow[:, :])
            Pj = aggrow[0:1, 32 * j:32 * j + 16]
            Bj = aggrow[0:1, 32 * j + 16:32 * j + 32]
            e.drain()
            e.tensor_mul(runtrow[:, :], Pj, runrow[:, :])
            e.drain()
            e.tensor_add(runrow[:, :], runtrow[:, :], Bj)
        e.drain()
        e.tensor_mul(carrrow[:, :], carrrow[:, :], s_hsel[:, :])
        e.drain()
        e.tensor_add(carrrow[0:1, 0:64], carrrow[0:1, 0:64], carrrow[0:1, 64:128])
        e.drain()
        e.tensor_add(carrrow[0:1, 0:32], carrrow[0:1, 0:32], carrrow[0:1, 32:64])
        e.drain()
        return e.tensor_add(h0row[:, :], carrrow[0:1, 0:16], carrrow[0:1, 16:32])
    tk_h0 = b.step(V, combine, [tk_agl, tk_boot[BT_HSEL]])

    tk_h0s = b.dma(SY, h0b, h0row, [tk_h0], sem=d_h1)
    tk_h0t = b.dma(SY, h0til.rearrange("o (r t) -> o r t", r=7),
                   h0b.broadcast_to((7, NST)).rearrange("(o r) t -> o r t", o=1),
                   [tk_h0s], sem=d_h1)
    def hin(e):
        e.tensor_copy(hinrow[0:1, 0:16], h0row[:, :])
        e.tensor_mul(tmp112[:, :], pincrow[0:1, 0:112], h0til[:, :])
        e.drain()
        return e.tensor_add(hinrow[0:1, 16:128], tmp112[:, :], bincrow[0:1, 0:112])
    tk_hinr = b.step(V, hin, [tk_h0t, tk_h0, tk_sscan])
    tk_hinT = b.step(TE, lambda e: e.transpose(bank(7, 0, 128, 0, 1), hinrow[:, :],
                                               s_eyef[0:1, 0:1]),
                     [tk_hinr, tk_finT, tk_finr])
    tk_hin = b.step(V, lambda e: e.tensor_copy(hincol[:, :], bank(7, 0, 128, 0, 1)),
                    [tk_hinT])

    tk_h = b.step(V, lambda e: e.scalar_tensor_tensor(
        sl(0), sl(9), hincol[:, :], sl(8), OP.mult, OP.add),
        [tk_hin, tk_cumP, tk_hloc])
    tk_Cc = b.step(V, lambda e: e.scalar_tensor_tensor(
        sl(5), sl(2), s_wsC[:, :], sl(5), OP.mult, OP.subtract),
        [tk_Cr, tk_Bx1, tk_wsneg])
    tk_hc = b.step(V, lambda e: e.tensor_mul(sl(7), sl(0), sl(5)),
                   [tk_h, tk_Cc, tk_Bx])

    tk_hcs = b.dma(GP, hcb, sl(7), [tk_hc], sem=d_msc)
    tk_hcf = b.dma(SY, hcf()[:, :].rearrange("n (c t) -> n c t", c=8),
                   hcb.rearrange("(c n) t -> n c t", c=8),
                   [tk_hcs, tk_rpev[-1], tk_wsev], sem=d_h1)

    tk_rr1 = b.dma(S, rowD, r_r8, [tk_r8, tk_op4], sem=d_h2)
    tk_rr2 = b.dma(S, rrow.rearrange("o (c t) -> o c t", c=8),
                   rowD.rearrange("o (c t) -> o c t", c=8),
                   [tk_rr1, tk_br3, tk_br4], sem=d_h2)
    tk_mr1 = b.dma(SY, rowA, r_m8, [tk_m8, tk_br2, tk_msq, tk_dt1, tk_op3, tk_br1],
                   sem=d_h1)
    tk_mr2 = b.dma(SY, mrow_neg.rearrange("o (c t) -> o c t", c=8),
                   rowA.rearrange("o (c t) -> o c t", c=8),
                   [tk_mr1, tk_mevac[-1]], sem=d_h1)
    tk_mrb = b.step(S, lambda e: e.activation(s_mrowb[:, :], mrow_neg, AF.Identity),
                    [tk_mr2, tk_wsmm])
    def rt_mm(e):
        ins = None
        for tt in range(NT):
            ins = e.transpose(bank(7, 0, 128, tt, 1), rrow[0:1, tt * 128:(tt + 1) * 128],
                              s_eyef[0:1, 0:1])
        return ins
    tk_rtT = b.step(TE, rt_mm, [tk_rr2, tk_hinT, tk_hin])
    tk_rt = b.step(V, lambda e: e.tensor_copy(s_rt[:, :], bank(7, 0, 128, 0, NT)),
                   [tk_rtT])

    NBLK = 4
    tk_comb = {}
    tk_x1 = {}
    tk_x1st = {}
    tk_xtld = {}
    tk_sqa = {}
    xtp = [0]
    def pump_xt(dep):
        if xtp[0] >= NT:
            return
        tt = xtp[0]
        xtp[0] += 1
        tk_xtld[tt] = b.dma(SY if tt % 2 == 0 else S, p3_xt(tt % 2),
                            x_tm[tt * 128:(tt + 1) * 128, :],
                            [sq_done, dep], sem=d_xt[tt % 2])
    pump_xt(None)
    pump_xt(None)
    x1_free = [sq_done, sq_done]
    attn_free = [sq_done] * NBLK
    t2_free = [sq_done, sq_done]
    pend_sq2 = [None]
    def emit_sq2():
        nonlocal_state = pend_sq2[0]
        if nonlocal_state is None:
            return
        tt, sl_ = nonlocal_state
        pend_sq2[0] = None
        tk_s2 = b.step(S, lambda e, tt=tt, sl_=sl_: e.activation(
            p3_trash, p3_x1(sl_), AF.Square, accum_out=s_sq2[:, tt:tt + 1]),
            [tk_x1[tt], _trash[0]])
        _trash[0] = tk_s2
        _lasts2[0] = tk_s2
        x1_free[sl_] = ("multi", x1_free[sl_], tk_s2)
    _trash = [sq_done]
    _lasts2 = [None]
    ybank_free = [None, None]
    last_mn = None

    for blk in range(NT // NBLK):
        for ti in range(NBLK):
            tt = blk * NBLK + ti
            par = tt % 2
            yb = (PA, PC)[par]
            sl_ = tt % 2
            def mm(e, tt=tt, yb=yb):
                for half in range(2):
                    e.matmul(yb[0:128, half * 512:(half + 1) * 512],
                             hcf(tt * 128, 128),
                             s_woutb[:, half * 512:(half + 1) * 512],
                             start=True, stop=False)
                for c in range(8):
                    e.matmul(yb[0:128, c * 128:(c + 1) * 128],
                             xf(c, tt * 128, 128), s_d8b[:, c * 128:(c + 1) * 128],
                             start=False, stop=False)
                e.matmul(yb[0:128, 0:512], s_mrowb[0:1, tt * 128:(tt + 1) * 128],
                         s_dpb[0:1, 0:512], start=False, stop=True)
                return e.matmul(yb[0:128, 512:1024],
                                s_mrowb[0:1, tt * 128:(tt + 1) * 128],
                                s_dpb[0:1, 512:1024], start=False, stop=True)
            deps = [tk_hcf, tk_mrb, tk_woutneg, tk_boot[BT_DPB], tk_boot[BT_D8B],
                    tk_rt, ybank_free[par]]
            if tt < 2:
                deps += [tk_rtT]
            tk_mm = b.step(TE, mm, deps)
            def comb(e, tt=tt, ti=ti, yb=yb):
                return e.activation(p3_attn(ti), yb[0:128, 0:1024], AF.Identity,
                                    scale=s_rt[:, tt:tt + 1],
                                    accum_out=s_suma[:, tt:tt + 1])
            tk_comb[tt] = b.step(S, comb, [tk_mm, tk_rt, attn_free[ti]])
            ybank_free[par] = tk_comb[tt]
            tk_sqa[tt] = b.step(S, lambda e, tt=tt, ti=ti: e.activation(
                p3_trash, p3_attn(ti), AF.Square, accum_out=s_sqa[:, tt:tt + 1]),
                [tk_comb[tt], _trash[0]])
            _trash[0] = tk_sqa[tt]
        b0, b1_ = blk * NBLK, (blk + 1) * NBLK
        def bst1(e, b0=b0, b1_=b1_):
            e.activation(s_m4[:, b0:b1_], s_suma[:, b0:b1_], AF.Identity, scale=-1.0 / DM)
            e.drain()
            return e.activation(s_tt1[:, b0:b1_], s_sqa[:, b0:b1_], AF.Identity,
                                scale=1.0 / DM)
        tk_b1 = b.step(S, bst1, [tk_sqa[b1_ - 1], tk_comb[b1_ - 1]])
        def bst2(e, b0=b0, b1_=b1_):
            e.tensor_mul(s_tt2[:, b0:b1_], s_m4[:, b0:b1_], s_m4[:, b0:b1_])
            e.drain()
            return e.tensor_sub(s_tt1[:, b0:b1_], s_tt1[:, b0:b1_], s_tt2[:, b0:b1_])
        tk_b2 = b.step(V, bst2, [tk_b1])
        def bst3(e, b0=b0, b1_=b1_):
            e.activation(s_tt2[:, b0:b1_], s_tt1[:, b0:b1_], AF.Ln, bias=1e-5)
            e.drain()
            return e.activation(s_r4[:, b0:b1_], s_tt2[:, b0:b1_], AF.Exp, scale=-0.5)
        tk_b3 = b.step(S, bst3, [tk_b2])
        tk_b4 = b.step(V, lambda e, b0=b0, b1_=b1_: e.tensor_mul(
            s_mr4[:, b0:b1_], s_m4[:, b0:b1_], s_r4[:, b0:b1_]), [tk_b3])
        for ti in range(NBLK):
            tt = blk * NBLK + ti
            sl_ = tt % 2
            tk_t2 = b.step(S, lambda e, tt=tt, sl_=sl_: e.activation(
                p3_t2(sl_), p3_xt(sl_), AF.Identity, bias=s_mr4[:, tt:tt + 1]),
                [tk_xtld[tt], tk_b4, t2_free[sl_]])
            pump_xt(tk_t2)
            emit_sq2()
            def x1f(e, tt=tt, ti=ti, sl_=sl_):
                return e.scalar_tensor_tensor(
                    p3_x1(sl_), p3_attn(ti), s_r4[:, tt:tt + 1], p3_t2(sl_),
                    OP.mult, OP.add, accum_out=s_sum2[:, tt:tt + 1])
            tk_x1[tt] = b.step(V, x1f, [tk_t2, tk_comb[tt], tk_b4, x1_free[sl_]])
            attn_free[ti] = tk_x1[tt]
            t2_free[sl_] = tk_x1[tt]
            tk_mn = b.step(V, lambda e, tt=tt, sl_=sl_: (
                e.tensor_reduce(s_minT[:, tt:tt + 1], p3_x1(sl_), AX.X, OP.min),
                e.tensor_reduce(s_maxT[:, tt:tt + 1], p3_x1(sl_), AX.X, OP.max))[-1],
                [tk_x1[tt]])
            last_mn = tk_mn
            pend_sq2[0] = (tt, sl_)
            tk_x1st[tt] = b.dma(GP, x1_d[tt * 128:(tt + 1) * 128, :], p3_x1(sl_),
                                [tk_x1[tt]], sem=d_x1s)
            x1_free[sl_] = ("multi", tk_mn, tk_x1st[tt])

    emit_sq2()
    def m2c(e):
        e.activation(s_m2[:, :], s_sum2[:, :], AF.Identity, scale=-1.0 / DM)
        e.drain()
        return e.activation(s_tt1[:, :], s_sq2[:, :], AF.Identity, scale=1.0 / DM)
    tk_m2 = b.step(S, m2c, [tk_x1[NT - 1], _lasts2[0]])
    def v2a(e):
        e.tensor_mul(s_tt2[:, :], s_m2[:, :], s_m2[:, :])
        e.drain()
        return e.tensor_sub(s_tt1[:, :], s_tt1[:, :], s_tt2[:, :])
    tk_v2a = b.step(V, v2a, [tk_m2])
    def s2b(e):
        e.activation(s_tt2[:, :], s_tt1[:, :], AF.Ln, bias=1e-5)
        e.drain()
        return e.activation(s_r2[:, :], s_tt2[:, :], AF.Exp, scale=-0.5)
    tk_s2b = b.step(S, s2b, [tk_v2a])
    def cnd(e):
        e.tensor_add(s_tt1[:, :], s_maxT[:, :], s_m2[:, :])
        e.tensor_add(s_tt2[:, :], s_minT[:, :], s_m2[:, :])
        e.drain()
        e.tensor_mul(s_tt1[:, :], s_tt1[:, :], s_r2[:, :])
        e.tensor_mul(s_tt2[:, :], s_tt2[:, :], s_r2[:, :])
        e.drain()
        e.tensor_single_scalar(s_tt2[:, :], s_tt2[:, :], -1.0, OP.mult)
        e.drain()
        e.tensor_reduce(s_cand[:, 0:1], s_tt1[:, :], AX.X, OP.max)
        return e.tensor_reduce(s_cand[:, 1:2], s_tt2[:, :], AX.X, OP.max)
    tk_cnd = b.step(V, cnd, [tk_s2b, last_mn])
    tk_cndT = b.step(TE, lambda e: e.transpose(bank(7, 0, 2, 0, 128), s_cand[:, :],
                                               s_eyef[:, :]), [tk_cnd, tk_rtT, tk_rt])
    tk_cnd2 = b.step(V, lambda e: e.tensor_reduce(s_cred[:, :], bank(7, 0, 2, 0, 128),
                                                  AX.X, OP.max), [tk_cndT])
    tk_ar2s = b.dma(GP, ar2i.rearrange("a b -> b a"), s_cred[:, :], [tk_cnd2], sem=d_msc)
    tk_cc2 = b.collective("AllReduce", OP.max, ar2i, ar2o, [tk_ar2s])
    tk_ar2l = b.dma(GP, s_q0[0:1, 8:10], ar2o, [tk_cc2], sem=d_msc)

    def qs1(e):
        e.tensor_add(s_q0[0:1, 10:11], s_q0[0:1, 8:9], s_q0[0:1, 9:10])
        e.drain()
        e.tensor_scalar(s_q0[0:1, 10:11], s_q0[0:1, 10:11], 1.0 / 255.0, 1e-8,
                        OP.mult, OP.max)
        e.drain()
        e.reciprocal(s_q0[0:1, 11:12], s_q0[0:1, 10:11])
        e.drain()
        e.tensor_mul(s_q0[0:1, 12:13], s_q0[0:1, 9:10], s_q0[0:1, 11:12])
        e.drain()
        e.tensor_scalar(s_q0[0:1, 14:15], s_q0[0:1, 12:13], MAGIC, MAGIC,
                        OP.add, OP.subtract)
        e.drain()
        e.tensor_scalar(s_q0[0:1, 14:15], s_q0[0:1, 14:15], -128.0, None, OP.add)
        e.drain()
        e.tensor_scalar(s_q0[0:1, 14:15], s_q0[0:1, 14:15], -128.0, 127.0,
                        OP.max, OP.min)
        e.drain()
        e.tensor_scalar(s_q0[0:1, 12:13], s_q0[0:1, 14:15], -1.0, -128.0,
                        OP.mult, OP.add)
        e.tensor_scalar(s_q0[0:1, 13:14], s_q0[0:1, 14:15], -1.0, 127.0,
                        OP.mult, OP.add)
        e.drain()
        return e.tensor_mul(s_q0[0:1, 15:16], s_q0[0:1, 10:11], s_q0[0:1, 2:3])
    tk_qs1 = b.step(V, qs1, [tk_ar2l, tk_sw])
    tk_qrmm1 = b.step(TE, lambda e: e.matmul(bank(7, 0, 128, 0, 4), s_ones1f[0:1, :],
                                             s_q0[0:1, 12:16], start=True, stop=True),
                      [tk_qs1, tk_cndT, tk_cnd2])
    tk_qrep1 = b.step(V, lambda e: (
        e.tensor_copy(s_qrep[:, 0:1], bank(7, 0, 128, 0, 1)),
        e.tensor_copy(s_qrep[:, 1:2], bank(7, 0, 128, 1, 1)),
        e.tensor_copy(s_qrep[:, 2:3], bank(7, 0, 128, 3, 1)))[-1], [tk_qrmm1])
    tk_qrmm2 = b.step(TE, lambda e: e.matmul(bank(7, 0, 128, 4, 1), s_ones1f[0:1, :],
                                             s_q0[0:1, 11:12], start=True, stop=True),
                      [tk_qs1, tk_qrmm1])
    tk_qrep2 = b.step(V, lambda e: e.tensor_copy(s_qrep[:, 7:8], bank(7, 0, 128, 4, 1)),
                      [tk_qrmm2])
    def aqbq(e):
        e.tensor_scalar(s_aq[:, :], s_r2[:, :], s_qrep[:, 7:8], None, OP.mult)
        e.drain()
        return e.tensor_mul(s_bq[:, :], s_m2[:, :], s_aq[:, :])
    tk_aq = b.step(V, aqbq, [tk_qrep2, tk_s2b, tk_cnd])

    all_x1st = Tok(d_x1s, d_x1s.n, "_", True)

    tk_bar34 = b.barrier()

    tk_w1q = {}
    w1_last_act = [None, None, None, None]
    t0scr_free = [None]
    w1_seq = [(hh, c) for hh in range(4) for c in range(8)]
    w1_slot = lambda idx: (idx % 4) if idx < 16 else (idx % 2)
    w1_next = {}
    _seen = {}
    for _i in range(32):
        _s = w1_slot(_i)
        if _s in _seen:
            w1_next[_seen[_s]] = _i
        _seen[_s] = _i
    w1_ld = {}
    def w1_load(idx, dep):
        hh, c = w1_seq[idx]
        sl_ = w1_slot(idx)
        w1_ld[idx] = b.dma(SY if sl_ % 2 == 0 else S, p4_w1raw(sl_),
                           w1t[c * 128:(c + 1) * 128, hh * 1024:(hh + 1) * 1024],
                           [tk_cnd, dep], sem=d_w1[sl_])
    def w1_emit(idx):
        hh, c = w1_seq[idx]
        sl_ = w1_slot(idx)
        tkt0 = b.step(S, lambda e, sl_=sl_: e.activation(
            p4_w1t0, p4_w1raw(sl_), AF.Identity, scale=s_wq[:, 0:1]),
            [w1_ld[idx], tk_wq, t0scr_free[0]])
        w1_last_act[sl_] = tkt0
        tkq = b.step(V, lambda e, c=c, hh=hh: e.tensor_scalar(
            w1qn(c, hh * 1024, 1024), p4_w1t0, MAGIC, MAGIC, OP.add, OP.subtract),
            [tkt0, all_x1st])
        t0scr_free[0] = tkq
        tk_w1q[(c, hh)] = tkq
        if idx in w1_next:
            w1_load(w1_next[idx], tkt0)
    for idx in range(4):
        w1_load(idx, None)
    for idx in range(32):
        w1_emit(idx)
    gsb_free2 = [w1_last_act[2], w1_last_act[3]]

    tk_w2q = {}
    w2_last_act = {0: None, 1: None}
    w2_slotmap = lambda k: (k % 2) if G > 1 else 0
    def p4_w2slot(s):
        return p4_w2raw(0) if s == 0 else p4_w1raw(0)
    w2_ld = {}
    w2lp = [0]
    def w2_pump():
        if w2lp[0] >= 32:
            return
        k = w2lp[0]
        w2lp[0] += 1
        s = w2_slotmap(k)
        dep = w2_last_act[s]
        if s == 1 and dep is None:
            dep = w1_last_act[0]
        w2_ld[k] = b.dma(SY if k % 2 == 0 else S, p4_w2slot(s),
                         w2t[k * 128:(k + 1) * 128, :],
                         [tk_bar34, dep], sem=d_w2[s])
    def w2_emit(k):
        s = w2_slotmap(k)
        tkt0 = b.step(S, lambda e, s=s: e.activation(
            p4_w2t0, p4_w2slot(s), AF.Identity, scale=s_wq[:, 1:2]),
            [w2_ld[k], tk_wq, t0scr_free[0]])
        w2_last_act[s] = tkt0
        tkq = b.step(V, lambda e, k=k: e.tensor_scalar(
            w2qn(k), p4_w2t0, MAGIC, MAGIC, OP.add, OP.subtract), [tkt0])
        t0scr_free[0] = tkq
        tk_w2q[k] = tkq
        w2_pump()
    def w2_start():
        if w2lp[0] == 0:
            w2_pump()
            w2_pump()

    xqn_free = [tk_hcf, ("multi", tk_mrb)]
    xqf_prev = {0: ("multi", tk_hcf, tk_mrb), 1: None}
    t0_free = [None, None]
    gsb_free = gsb_free2
    h_bank_free = {4: None, 5: None, 6: None, 7: None}
    tqbank_free = [None, None]
    tk_gst = {}
    gmm_last = None

    tk_xq = {}
    tk_clip4 = {}
    last_mm1_of = {}
    tk_x4ld = {}
    x4_seq = [(g_, ti) for g_ in range(G) for ti in range(4)]
    x4p = [0]
    def pump_x4(dep):
        if x4p[0] >= len(x4_seq):
            return
        g_, ti = x4_seq[x4p[0]]
        x4p[0] += 1
        tt = g_ * 4 + ti
        tk_x4ld[(g_, ti)] = b.dma(
            SY, p4_x1r(ti % 2), x1_d[tt * 128:(tt + 1) * 128, :],
            [tk_cnd, tk_x1st[tt], dep], sem=d_x1r[ti % 2])
    pump_x4(None)
    pump_x4(None)

    tk_f04 = {}
    def q4_pre(qg, ti):
        sl_ = ti % 2
        tt = qg * 4 + ti
        tkt0 = b.step(S, lambda e, sl_=sl_, tt=tt: e.activation(
            p4_t0(sl_), p4_x1r(sl_), AF.Identity, scale=s_aq[:, tt:tt + 1],
            bias=s_bq[:, tt:tt + 1]), [tk_x4ld[(qg, ti)], tk_aq, t0_free[sl_]])
        tkf0 = b.step(V, lambda e, sl_=sl_: e.tensor_scalar(
            p4_x1r(sl_), p4_t0(sl_), MAGIC, MAGIC, OP.add, OP.subtract), [tkt0])
        t0_free[sl_] = tkf0
        tk_f04[(qg, ti)] = tkf0

    def q4_clip(qg, ti):
        sl_ = ti % 2
        tkxq = b.step(V, lambda e, sl_=sl_, ti=ti: e.tensor_scalar(
            p4_xqn(ti % 2), p4_x1r(sl_), s_qrep[:, 0:1], s_qrep[:, 1:2],
            OP.max, OP.min), [tk_f04[(qg, ti)], tk_qrep1, xqn_free[ti % 2]])
        tk_clip4[(qg, ti)] = tkxq
        pump_x4(tkxq)

    def q4_trn_ev(qg, ti):
        tb = ti % 2
        def trn(e, ti=ti, tb=tb):
            pq = (PA, PB)[tb][0:128, 0:512].bitcast(BF16)
            ins = None
            for c in range(8):
                ins = e.transpose(pq[:, c * 128:(c + 1) * 128],
                                  p4_xqn(ti % 2)[:, c * 128:(c + 1) * 128],
                                  s_eyeb[:, :])
            return ins
        tktr = b.step(TE, trn, [tk_clip4[(qg, ti)], tk_boot[BT_EYEB],
                                tqbank_free[tb]])
        xqn_free[ti % 2] = tktr
        def ev(e, qg=qg, ti=ti, tb=tb):
            dst = p4_xqf_dst(qg, ti)
            src = (PA, PB)[tb][0:128, 0:512].bitcast(BF16).rearrange(
                "p (c w) -> p c w", c=8)
            return e.activation(dst, src, AF.Identity)
        tkev = b.step(S, ev, [tktr, last_mm1_of.get(qg - 2, xqf_prev[qg % 2])])
        tqbank_free[tb] = tkev
        tk_xq[(qg, ti)] = tkev

    for it in range(G + 1):
        qg = it if it < G else None
        mg = it - 1 if it >= 1 else None
        if qg == 0:
            q4_pre(0, 0)
            q4_pre(0, 1)
            q4_clip(0, 0)
            q4_pre(0, 2)
            q4_clip(0, 1)
            q4_pre(0, 3)
        if mg is None:
            if qg == 0:
                q4_trn_ev(0, 0)
                q4_clip(0, 2)
                q4_trn_ev(0, 1)
                q4_clip(0, 3)
                q4_trn_ev(0, 2)
                q4_trn_ev(0, 3)
                if G > 1:
                    q4_pre(1, 0)
                    q4_pre(1, 1)
                    q4_clip(1, 0)
                    q4_pre(1, 2)
                    q4_clip(1, 1)
                    q4_pre(1, 3)
            continue
        if G == 1:
            w2ks = list(range(32)) if mg == 0 else []
        elif mg >= 1:
            w2ks = list(range(((mg - 1) * 32) // (G - 1), (mg * 32) // (G - 1)))
        else:
            w2ks = []
        w2i = 0
        if w2ks:
            w2_start()
        for hc in range(32):
            hh = hc // 8
            bk = 4 + (hc % 4)
            def mm1(e, mg=mg, hc=hc, bk=bk):
                ins = None
                for c in range(8):
                    ins = e.matmul(bank(bk), w1qn(c, hc * 128, 128),
                                   p4_xqf2(mg, c), start=(c == 0), stop=(c == 7))
                return ins
            deps = [tk_xq[(mg, t_)] for t_ in range(4)] + \
                   [tk_w1q[(c, hh)] for c in range(8)] + [h_bank_free[bk]]
            tkm = b.step(TE, mm1, deps)
            last_mm1_of[mg] = tkm
            q = (hc // 4) % 2
            ciq = hc % 4
            tkg = b.step(S, lambda e, hc=hc, bk=bk, q=q, ciq=ciq: e.activation(
                p4_gsb(q)[:, ciq * 512:(ciq + 1) * 512], bank(bk),
                AF.Gelu, scale=s_qrep[:, 2:3], bias=s_b1[:, hc:hc + 1]),
                [tkm, tk_qrep1, tk_boot[BT_B1]] +
                ([gsb_free[q]] if ciq == 0 else []))
            h_bank_free[bk] = tkg
            tkmn = b.step(V, lambda e, hc=hc, q=q, ciq=ciq: (
                e.tensor_reduce(s_gmx[:, hc:hc + 1],
                                p4_gsb(q)[:, ciq * 512:(ciq + 1) * 512], AX.X, OP.max),
                e.tensor_reduce(s_gmn[:, hc:hc + 1],
                                p4_gsb(q)[:, ciq * 512:(ciq + 1) * 512], AX.X,
                                OP.min))[-1], [tkg])
            if ciq == 3:
                q4 = hc // 4
                tkst = b.dma((GP, SY, GP, S)[q4 % 4],
                             g_d[mg, q4 * 4:(q4 + 1) * 4].rearrange("q p c -> p q c"),
                             p4_gsb(q), [tkg], sem=d_gst[q4 % 4])
                tk_gst[(mg, q4)] = tkst
                gsb_free[q] = ("multi", tkst, tkmn)
            gmm_last = tkmn
            if G > 1 and w2i < len(w2ks) and hc % 2 == 1 and hc // 2 >= w2i:
                w2_emit(w2ks[w2i])
                w2i += 1
            if qg is not None and hc % 8 == 2:
                hcq = hc // 8
                q4_trn_ev(qg, hcq)
                if hcq < 2:
                    q4_clip(qg, hcq + 2)
            nq = mg + 2
            if nq < G:
                if hc == 13:
                    q4_pre(nq, 0)
                elif hc == 15:
                    q4_pre(nq, 1)
                elif hc == 19:
                    q4_clip(nq, 0)
                elif hc == 21:
                    q4_pre(nq, 2)
                elif hc == 27:
                    q4_clip(nq, 1)
                elif hc == 29:
                    q4_pre(nq, 3)
        while w2i < len(w2ks):
            w2_emit(w2ks[w2i])
            w2i += 1
        def fold(e):
            e.tensor_reduce(s_gtmp[:, :], s_gmx[:, :], AX.X, OP.max)
            e.drain()
            e.tensor_tensor(s_gmxr[:, :], s_gmxr[:, :], s_gtmp[:, :], OP.max)
            e.drain()
            e.tensor_reduce(s_gtmp[:, :], s_gmn[:, :], AX.X, OP.min)
            e.drain()
            return e.tensor_tensor(s_gmnr[:, :], s_gmnr[:, :], s_gtmp[:, :], OP.min)
        gmm_last = b.step(V, fold, [gmm_last, tk_memset])

    def gneg(e):
        e.tensor_single_scalar(s_gmnr[:, :], s_gmnr[:, :], -1.0, OP.mult)
        e.drain()
        e.tensor_copy(s_cand[:, 0:1], s_gmxr[:, :])
        return e.tensor_copy(s_cand[:, 1:2], s_gmnr[:, :])
    tk_gneg = b.step(V, gneg, [gmm_last])
    tk_gT = b.step(TE, lambda e: e.transpose(bank(7, 0, 2, 0, 128), s_cand[:, :],
                                             s_eyef[:, :]), [tk_gneg, tk_qrmm2, tk_qrep2])
    tk_g2 = b.step(V, lambda e: e.tensor_reduce(s_cred[:, :], bank(7, 0, 2, 0, 128),
                                                AX.X, OP.max), [tk_gT])
    tk_ar3s = b.dma(GP, ar3i.rearrange("a b -> b a"), s_cred[:, :], [tk_g2], sem=d_msc)
    tk_cc3 = b.collective("AllReduce", OP.max, ar3i, ar3o, [tk_ar3s])
    tk_ar3l = b.dma(GP, s_q2[0:1, 0:2], ar3o, [tk_cc3], sem=d_msc)

    def qs2(e):
        e.tensor_add(s_q2[0:1, 2:3], s_q2[0:1, 0:1], s_q2[0:1, 1:2])
        e.drain()
        e.tensor_scalar(s_q2[0:1, 2:3], s_q2[0:1, 2:3], 1.0 / 255.0, 1e-8,
                        OP.mult, OP.max)
        e.drain()
        e.reciprocal(s_q2[0:1, 3:4], s_q2[0:1, 2:3])
        e.drain()
        e.tensor_mul(s_q2[0:1, 4:5], s_q2[0:1, 1:2], s_q2[0:1, 3:4])
        e.drain()
        e.tensor_scalar(s_q2[0:1, 5:6], s_q2[0:1, 4:5], MAGIC, MAGIC,
                        OP.add, OP.subtract)
        e.drain()
        e.tensor_scalar(s_q2[0:1, 6:7], s_q2[0:1, 5:6], -128.0, None, OP.add)
        e.drain()
        e.tensor_scalar(s_q2[0:1, 6:7], s_q2[0:1, 6:7], -128.0, 127.0, OP.max, OP.min)
        e.drain()
        e.tensor_scalar(s_q2[0:1, 7:8], s_q2[0:1, 6:7], -1.0, -128.0, OP.mult, OP.add)
        e.tensor_scalar(s_q2[0:1, 8:9], s_q2[0:1, 6:7], -1.0, 127.0, OP.mult, OP.add)
        e.drain()
        return e.tensor_mul(s_q2[0:1, 9:10], s_q2[0:1, 2:3], s_q0[0:1, 3:4])
    tk_qs2 = b.step(V, qs2, [tk_ar3l, tk_sw])
    def qrmm3(e):
        e.matmul(bank(7, 0, 128, 0, 1), s_ones1f[0:1, :], s_q2[0:1, 3:4],
                 start=True, stop=True)
        return e.matmul(bank(7, 0, 128, 1, 3), s_ones1f[0:1, :], s_q2[0:1, 7:10],
                        start=True, stop=True)
    tk_qrmm3 = b.step(TE, qrmm3, [tk_qs2, tk_gT, tk_g2])
    tk_qrep3 = b.step(V, lambda e: (
        e.tensor_copy(s_qrep[:, 3:4], bank(7, 0, 128, 0, 1)),
        e.tensor_copy(s_qrep[:, 4:5], bank(7, 0, 128, 1, 1)),
        e.tensor_copy(s_qrep[:, 5:6], bank(7, 0, 128, 2, 1)),
        e.tensor_copy(s_qrep[:, 6:7], bank(7, 0, 128, 3, 1)))[-1], [tk_qrmm3])

    graw_free = [gsb_free[0], gsb_free[1]]
    gt0_free = [None, None]
    gf0_free = [None, None]
    mm2_bank_free = {6: None, 7: None}
    tm_bank_free = [None, None, None]
    ffn_free = [None] * 8
    out_free = [None, None]
    t25_free = [None, None]

    tk_gld = {}
    tk_gq = {}
    tk_x15 = {}
    tk_ffnev = {}
    tk_trnA = {}
    last_mm2_of = {}
    last_trn_of = {}

    def plan_gld(gi, q):
        sl_ = q % 2
        tk_gld[(gi, q)] = b.dma(
            SY if sl_ == 0 else S, p5_graw(sl_),
            g_d[gi, q * 4:(q + 1) * 4].rearrange("q p c -> p q c"),
            [tk_gst[(gi, q)], graw_free[sl_]], sem=d_gld[sl_])

    odd_g = [g_ for g_ in range(G) if g_ % 2 == 1]
    x15dep = last_mm1_of.get(odd_g[-1]) if odd_g else None
    x1q_seq = [(g_, ti) for g_ in range(G) for ti in range(4)]
    x1p = [0]
    def pump_x1(dep):
        if x1p[0] >= len(x1q_seq):
            return
        g_, ti = x1q_seq[x1p[0]]
        x1p[0] += 1
        tt = g_ * 4 + ti
        tk_x15[(g_, ti)] = b.dma(
            SY, p5_x1r(ti), x1_d[tt * 128:(tt + 1) * 128, :],
            [x15dep, tk_x1st[tt], dep], sem=d_x1r[2 + ti % 2])

    plan_gld(0, 0)
    plan_gld(0, 1)
    pump_x1(None)
    pump_x1(None)

    tk_bar45 = b.barrier()
    for it in range(G + 1):
        qg = it if it < G else None
        mg = it - 1 if it >= 1 else None
        for q in range(8):
            if qg is not None:
                for j in range(4):
                    k = q * 4 + j
                    st = k % 2
                    sl_ = q % 2
                    koff = j * 512
                    tkt0 = b.step(S, lambda e, sl_=sl_, koff=koff, st=st:
                                  e.activation(
                                      p5_gt0(st), p5_graw(sl_)[:, koff:koff + 512],
                                      AF.Identity, scale=s_qrep[:, 3:4]),
                                  [tk_gld[(qg, q)], tk_qrep3, gt0_free[st]])
                    tkf0 = b.step(V, lambda e, st=st: e.tensor_scalar(
                        p5_gf0(st), p5_gt0(st), MAGIC, MAGIC, OP.add, OP.subtract),
                        [tkt0, gf0_free[st]])
                    gt0_free[st] = tkf0
                    tkq = b.step(V, lambda e, qg=qg, k=k, st=st: e.tensor_scalar(
                        gq2(qg, k), p5_gf0(st), s_qrep[:, 4:5], s_qrep[:, 5:6],
                        OP.max, OP.min),
                        [tkf0, tk_bar45, last_mm2_of.get(qg - 2)])
                    gf0_free[st] = tkq
                    tk_gq[(qg, k)] = tkq
                    if j == 3:
                        graw_free[sl_] = tkt0
                        gqq = qg * 8 + q + 2
                        if gqq < G * 8:
                            plan_gld(gqq // 8, gqq % 8)
            if mg is not None:
                dj = q
                bk = 6 + dj % 2
                def mm2(e, mg=mg, dj=dj, bk=bk, k=None):
                    if k is not None:
                        return e.matmul(bank(bk), w2qn(k, dj * 128, 128),
                                        gq2(mg, k), start=(k == 0), stop=(k == 31))
                    ins = None
                    for k_ in range(32):
                        ins = e.matmul(bank(bk), w2qn(k_, dj * 128, 128), gq2(mg, k_),
                                       start=(k_ == 0), stop=(k_ == 31))
                    return ins
                if mg == 0:
                    tkm = None
                    for k in range(32):
                        kdeps = [tk_gq[(mg, k)], tk_w2q[k]] + \
                                ([mm2_bank_free[bk]] if k == 0 else [])
                        tkm = b.step(TE, lambda e, mm2=mm2, k=k: mm2(e, k=k), kdeps)
                else:
                    deps = [tk_gq[(mg, k)] for k in range(32)] + \
                           [tk_w2q[k] for k in range(32)] + [mm2_bank_free[bk]]
                    tkm = b.step(TE, mm2, deps)
                last_mm2_of[mg] = tkm
                tkev = b.step(S, lambda e, dj=dj, bk=bk: e.activation(
                    p5_ffn(dj), bank(bk), AF.Identity, scale=s_qrep[:, 6:7],
                    bias=s_b2[:, dj:dj + 1]),
                    [tkm, tk_qrep3, tk_boot[BT_B2], ffn_free[dj]])
                mm2_bank_free[bk] = tkev
                tk_ffnev[dj] = tkev
                if dj == 3:
                    for ti_ in range(3):
                        tt_ = mg * 4 + ti_
                        b3_ = tt_ % 3
                        tmb_ = (PA, PB, PC)[b3_]
                        def trnha(e, ti_=ti_, tmb_=tmb_):
                            ins = None
                            for dj_ in range(4):
                                ins = e.transpose(
                                    tmb_[0:128, dj_ * 128:(dj_ + 1) * 128],
                                    p5_ffn(dj_)[:, ti_ * 128:(ti_ + 1) * 128],
                                    s_eyef[:, :])
                            return ins
                        tk_trnA[(mg, ti_)] = b.step(
                            TE, trnha,
                            [tk_ffnev[d_] for d_ in range(4)] +
                            [tk_boot[BT_EYEF], tm_bank_free[b3_]])
        if mg is not None:
            for ti in range(4):
                tt = mg * 4 + ti
                par = ti % 2
                b3 = tt % 3
                tmb = (PA, PB, PC)[b3]
                def trnh(e, ti=ti, tmb=tmb, djs=None):
                    ins = None
                    for dj in djs:
                        ins = e.transpose(tmb[0:128, dj * 128:(dj + 1) * 128],
                                          p5_ffn(dj)[:, ti * 128:(ti + 1) * 128],
                                          s_eyef[:, :])
                    return ins
                if (mg, ti) not in tk_trnA:
                    tk_trnA[(mg, ti)] = b.step(
                        TE, lambda e, trnh=trnh: trnh(e, djs=range(4)),
                        [tk_ffnev[dj] for dj in range(4)] +
                        [tk_boot[BT_EYEF], tm_bank_free[b3]])
                tktr = b.step(TE, lambda e, trnh=trnh: trnh(e, djs=range(4, 8)),
                              [tk_ffnev[dj] for dj in range(4, 8)] +
                              [tk_trnA[(mg, ti)]])
                last_trn_of[mg] = tktr
                def bn3(e, tmb=tmb):
                    e.bn_stats(s_bn[:, 0:6], tmb[0:128, 0:512])
                    e.bn_stats(s_bn[:, 6:12], tmb[0:128, 512:1024])
                    e.drain()
                    return e.bn_aggr(s_ag2[:, :], s_bn[:, :])
                tkbn = b.step(V, bn3, [tktr])
                def st3(e, ti=ti):
                    e.activation(s_m3[:, ti:ti + 1], s_ag2[:, 0:1], AF.Identity,
                                 scale=-1.0)
                    e.activation(s_r3[:, ti:ti + 1], s_ag2[:, 1:2], AF.Ln, bias=1e-5)
                    e.drain()
                    return e.activation(s_r3[:, ti:ti + 1], s_r3[:, ti:ti + 1],
                                        AF.Exp, scale=-0.5)
                tkst3 = b.step(S, st3, [tkbn])
                tkmr3 = b.step(V, lambda e, ti=ti: e.tensor_mul(
                    s_mr3[:, ti:ti + 1], s_m3[:, ti:ti + 1], s_r3[:, ti:ti + 1]),
                    [tkst3])
                tkt2 = b.step(S, lambda e, ti=ti, par=par: e.activation(
                    p5_t2(par), p5_x1r(ti), AF.Identity, bias=s_mr3[:, ti:ti + 1]),
                    [tk_x15[(mg, ti)], tkmr3, t25_free[par]])
                pump_x1(tkt2)
                def x2f(e, ti=ti, par=par, tmb=tmb):
                    return e.scalar_tensor_tensor(
                        out_sl(ti), tmb[0:128, 0:1024], s_r3[:, ti:ti + 1],
                        p5_t2(par), OP.mult, OP.add)
                tkx2 = b.step(V, x2f, [tkt2, tkbn, tkmr3, out_free[ti % 2]])
                t25_free[par] = tkx2
                tm_bank_free[b3] = tkx2
                tkos = b.dma(SY if ti % 2 == 0 else S,
                             out_d[tt * 128:(tt + 1) * 128, :], out_sl(ti),
                             [tkx2], sem=d_outs[ti % 2])
                out_free[ti % 2] = tkos
            for dj in range(8):
                ffn_free[dj] = last_trn_of[mg]

    b.step(SY, lambda e: None, [Tok(d_outs[0], d_outs[0].n, "_", True),
                                Tok(d_outs[1], d_outs[1].n, "_", True)])

    b.emit()
    return nc


def _host_layout(inputs, T):
    x = np.ascontiguousarray(np.asarray(inputs["L0_attn_x"], dtype=np.float32))
    Wxp = np.asarray(inputs["L0_attn_Wxproj"], dtype=np.float32)
    Wdt = np.asarray(inputs["L0_attn_Wdt"], dtype=np.float32)
    bdt = np.asarray(inputs["L0_attn_bdt"], dtype=np.float32)
    Wxs = np.asarray(inputs["L0_attn_Wxs"], dtype=np.float32)
    Alog = np.asarray(inputs["L0_attn_A_log"], dtype=np.float32)
    Wout = np.asarray(inputs["L0_attn_Wout"], dtype=np.float32)
    Dp = np.asarray(inputs["L0_attn_Dp"], dtype=np.float32)
    W1 = np.asarray(inputs["L0_ffn_W1"], dtype=np.float32)
    b1 = np.asarray(inputs["L0_ffn_b1"], dtype=np.float32)
    W2 = np.asarray(inputs["L0_ffn_W2"], dtype=np.float32)
    b2 = np.asarray(inputs["L0_ffn_b2"], dtype=np.float32)

    Bsz, L, D = x.shape
    xs = x.reshape(Bsz * L, D)
    wc_t = np.ascontiguousarray(np.concatenate([Wxp, Wxs], 0).T)
    w1t = np.ascontiguousarray(W1.T)
    w2t = np.ascontiguousarray(W2.T)
    d8 = np.zeros((128, DM), np.float32)
    idx = np.arange(128)
    for cb in range(8):
        d8[idx, cb * 128 + idx] = Dp[cb * 128:(cb + 1) * 128]
    bdiagp = np.zeros((NCH, 128), np.float32)
    for c in range(NCH):
        bdiagp[c, c * NST:(c + 1) * NST] = 1.0
    r48 = np.zeros((48, 3 * 128), np.float32)
    for c in range(NCH):
        for n in range(NST):
            r48[n, c * NST + n] = 1.0
            r48[16 + n, 128 + c * NST + n] = 1.0
            r48[32 + n, 256 + c * NST + n] = 1.0
    common = dict(
        wc_t=wc_t, wdt_c=np.ascontiguousarray(Wdt.reshape(64, 1)),
        bdt_i=bdt.reshape(1, 1), alog=Alog.reshape(1, NST),
        woutT=np.ascontiguousarray(Wout.T), dp_row=Dp.reshape(1, DM), d8=d8,
        w1t=w1t, b1sb=np.ascontiguousarray(b1.reshape(32, 128).T),
        w2t=w2t, b2sb=np.ascontiguousarray(b2.reshape(8, 128).T),
        eyef=np.eye(128, dtype=np.float32),
        eyeb=np.eye(128, dtype=ml_dtypes.bfloat16),
        bdiagp=bdiagp, bdiagn=-bdiagp, r48=r48,
    )
    in_maps = []
    for i in range(NCORES):
        shard = xs[i * T:(i + 1) * T]
        hsel = np.zeros((1, 128), np.float32)
        hsel[0, NST * i:NST * (i + 1)] = 1.0
        wsh = np.ascontiguousarray(np.concatenate(
            [w1t[i * 128:(i + 1) * 128, :],
             w2t[i * 512:(i + 1) * 512, :].reshape(128, 4096)], axis=1))
        m = dict(common)
        m.update(x_fm=np.ascontiguousarray(shard.T),
                 x_tm=np.ascontiguousarray(shard), hsel=hsel, wsh=wsh)
        in_maps.append(m)
    return in_maps, (Bsz, L, D)


def kernel(**inputs):
    x = np.asarray(inputs["L0_attn_x"])
    Bsz, L, D = x.shape
    T = (Bsz * L) // NCORES
    in_maps, shape = _host_layout(inputs, T)
    nc = build_program(T)
    res = run_bass_kernel_spmd(nc, in_maps, core_ids=list(range(NCORES)))
    out = np.concatenate([np.asarray(r["out"]) for r in res.results], axis=0)
    return out.reshape(Bsz, L, D).astype(np.float32)


# version 27 (final)
# speedup vs baseline: 1.0182x; 1.0044x over previous
import numpy as np
import ml_dtypes

import concourse.bass as bass
import concourse.mybir as mybir
from concourse.bass_utils import run_bass_kernel_spmd
from contextlib import ExitStack

F32 = mybir.dt.float32
BF16 = mybir.dt.bfloat16
AF = mybir.ActivationFunctionType
OP = mybir.AluOpType
AX = mybir.AxisListType

NCORES = 8
MAGIC = 12582912.0
DM = 1024
DH = 4096
NST = 16
NCH = 8


class Sem:
    __slots__ = ("h", "n")
    def __init__(self, h):
        self.h = h
        self.n = 0


class Tok:
    __slots__ = ("sem", "val", "eng", "dma")
    def __init__(self, sem, val, eng, dma=False):
        self.sem, self.val, self.eng, self.dma = sem, val, eng, dma


def flat(*deps):
    out = []
    for d in deps:
        if d is None or isinstance(d, str):
            continue
        if isinstance(d, (tuple, list)):
            out.extend(flat(*d))
        else:
            out.append(d)
    return out


class B:
    ENGINES = ("sync", "tensor", "vector", "scalar", "gpsimd")

    def __init__(self, nc):
        self.nc = nc
        self.ops = {e: [] for e in self.ENGINES}
        self.es = ExitStack()
        self.esem = {e: Sem(self.es.enter_context(nc.semaphore(f"e_{e}")))
                     for e in self.ENGINES}
        self.bar = Sem(self.es.enter_context(nc.semaphore("bar")))
        self.cc = Sem(self.es.enter_context(nc.semaphore("cc")))
        self.seen = {e: {} for e in self.ENGINES}

    def sem(self, name):
        return Sem(self.es.enter_context(self.nc.semaphore(name)))

    def sbuf(self, name, shape, dtype=F32):
        return self.es.enter_context(self.nc.sbuf_tensor(name, list(shape), dtype))

    def psum(self, name, shape):
        return self.es.enter_context(self.nc.psum_tensor(name, list(shape), F32))

    def _waits(self, eng, deps):
        waits = {}
        drain = False
        for d in flat(*deps):
            if d.eng == eng and not d.dma:
                drain = True
                continue
            sm = d.sem
            k = id(sm)
            if self.seen[eng].get(k, -1) >= d.val:
                continue
            if k in waits:
                waits[k] = (sm, max(waits[k][1], d.val))
            else:
                waits[k] = (sm, d.val)
        for k, (sm, v) in waits.items():
            self.seen[eng][k] = max(self.seen[eng].get(k, -1), v)
        return list(waits.values()), drain

    def step(self, eng, emit, deps=(), sem=None, inc=1, dma=False):
        waits, drain = self._waits(eng, deps)
        if sem is None:
            sem = self.esem[eng]
        sem.n += inc
        tok = Tok(sem, sem.n, eng, dma)
        def f(e, waits=waits, drain=drain, emit=emit, sem=sem, inc=inc):
            if drain:
                e.drain()
            for sm, v in waits:
                e.wait_ge(sm.h, v)
            ins = emit(e)
            if ins is not None:
                ins.then_inc(sem.h, inc)
            else:
                e.sem_inc(sem.h, inc)
        self.ops[eng].append(f)
        return tok

    def dma(self, eng, out, in_, deps=(), sem=None):
        deps = flat(*deps)
        if sem is not None and sem.n > 0:
            deps.append(Tok(sem, sem.n, "_", True))
        return self.step(eng, lambda e, o=out, i=in_: e.dma_start(out=o, in_=i),
                         deps, sem=sem, inc=16, dma=True)

    def barrier(self):
        self.bar.n += len(self.ENGINES)
        tgt = self.bar.n
        for eng in self.ENGINES:
            def f(e, tgt=tgt):
                e.sem_inc(self.bar.h, 1)
                e.wait_ge(self.bar.h, tgt)
            self.ops[eng].append(f)
            self.seen[eng][id(self.bar)] = tgt
        return Tok(self.bar, tgt, "_", True)

    def collective(self, kind, op, ins_, outs_, deps=()):
        def emit(e):
            return e.collective_compute(
                kind, op, replica_groups=[list(range(NCORES))],
                ins=[ins_.opt()], outs=[outs_.opt()])
        return self.step("gpsimd", emit, deps, sem=self.cc, inc=1, dma=True)

    def emit(self):
        with self.nc.Block() as block:
            for eng in self.ENGINES:
                ops = self.ops[eng]
                getattr(block, eng)(lambda e, ops=ops: [f(e) for f in ops])


def build_program(T):
    G = T // 512
    NT = T // 128
    LC = T // NCH
    nc = bass.Bass("TRN2", target_bir_lowering=False)
    b = B(nc)

    def dram(n, s, d=F32, k=None):
        t = nc.dram_tensor(n, list(s), d, kind=k) if k else nc.dram_tensor(n, list(s), d)
        return t.ap()

    x_fm = dram("x_fm", (DM, T), F32, "ExternalInput")
    x_tm = dram("x_tm", (T, DM), F32, "ExternalInput")
    wc_t = dram("wc_t", (DM, 112), F32, "ExternalInput")
    wdt_c = dram("wdt_c", (64, 1), F32, "ExternalInput")
    bdt_i = dram("bdt_i", (1, 1), F32, "ExternalInput")
    alog = dram("alog", (1, NST), F32, "ExternalInput")
    woutT = dram("woutT", (NST, DM), F32, "ExternalInput")
    dp_row = dram("dp_row", (1, DM), F32, "ExternalInput")
    d8 = dram("d8", (128, DM), F32, "ExternalInput")
    w1t = dram("w1t", (DM, DH), F32, "ExternalInput")
    b1sb = dram("b1sb", (128, 32), F32, "ExternalInput")
    w2t = dram("w2t", (DH, DM), F32, "ExternalInput")
    b2sb = dram("b2sb", (128, 8), F32, "ExternalInput")
    wsh_d = dram("wsh", (128, 8192), F32, "ExternalInput")
    eyef_d = dram("eyef", (128, 128), F32, "ExternalInput")
    eyeb_d = dram("eyeb", (128, 128), BF16, "ExternalInput")
    bdiagp_d = dram("bdiagp", (NCH, 128), F32, "ExternalInput")
    bdiagn_d = dram("bdiagn", (NCH, 128), F32, "ExternalInput")
    r48_d = dram("r48", (48, 3 * 128), F32, "ExternalInput")
    hsel_d = dram("hsel", (1, 128), F32, "ExternalInput")
    out_d = dram("out", (T, DM), F32, "ExternalOutput")

    rs48 = dram("rs48", (48, T))
    rowA = dram("rowA", (1, T))
    rowB = dram("rowB", (1, T))
    rowC = dram("rowC", (1, T))
    rowD = dram("rowD", (1, T))
    hcb = dram("hcb", (128, LC), BF16)
    h0b = dram("h0b", (1, NST))
    arow_b = dram("arow_b", (1, NST))
    x1_d = dram("x1_d", (T, DM))
    g_d = dram("g_d", (G, 32, 128, 512))
    ar0i = dram("ar0i", (1, 2)); ar0o = dram("ar0o", (1, 2))
    ag1i = dram("ag1i", (2, NST)); ag1o = dram("ag1o", (2 * NCORES, NST))
    ar2i = dram("ar2i", (1, 2)); ar2o = dram("ar2o", (1, 2))
    ar3i = dram("ar3i", (1, 2)); ar3o = dram("ar3o", (1, 2))

    sb = b.sbuf
    s_eyef = sb("s_eyef", (128, 128))
    s_eyeb = sb("s_eyeb", (128, 128), BF16)
    s_wdt = sb("s_wdt", (64, 1))
    s_bdt = sb("s_bdt", (1, 1))
    s_bdt8 = sb("s_bdt8", (NCH, 1))
    s_alog = sb("s_alog", (1, NST))
    s_apos = sb("s_apos", (1, NST))
    s_ones1f = sb("s_ones1f", (1, 128))
    s_hsel = sb("s_hsel", (1, 128))
    CS2 = sb("CS2", (NCH, 3 * 128))
    s_atil = CS2[:, 0:128]
    s_bdiagn = CS2[:, 128:256]
    s_bdA = CS2[:, 256:384]
    s_bdiagp = sb("s_bdiagp", (NCH, 128))
    s_r48 = sb("s_r48", (112, 3 * 128))
    s_r48B = s_r48[64:112, 0:128]
    s_r48C = s_r48[64:112, 128:256]
    s_r48X = s_r48[64:112, 256:384]
    CSB = sb("CSB", (128, DM), BF16)
    s_woutb = CSB[0:NST, :]
    s_dpb = CSB[32:33, :]
    s_b1 = sb("s_b1", (128, 32))
    s_b2 = sb("s_b2", (128, 8))
    s_onesb = sb("s_onesb", (128, 1), BF16)

    AR0 = sb("AR0", (128, max(8 * T, 32 * DM)), BF16)
    AR1 = sb("AR1", (128, max(8 * T, 8 * DH)), BF16)
    AR3 = sb("AR3", (128, 14336), F32)
    BFX = sb("BFX", (128, 6144), BF16)
    AR1f = AR1[:, :].bitcast(F32)

    def a0(c0, w, p0=0, p1=128):
        return AR0[p0:p1, c0:c0 + w]
    def a1(c0, w, p0=0, p1=128):
        return AR1[p0:p1, c0:c0 + w]
    def a1f(c0, w, p0=0, p1=128):
        return AR1f[p0:p1, c0:c0 + w]
    def a3(c0, w, p0=0, p1=128):
        return AR3[p0:p1, c0:c0 + w]
    def bfx(c0, w, p0=0, p1=128):
        return BFX[p0:p1, c0:c0 + w]

    xf = lambda c, g0=0, w=None: a0(c * T + g0, w if w is not None else T)
    w2qn = lambda k, d0=0, w=DM: a0(k * DM + d0, w)
    xsq = lambda c, g0=0, w=None: a1(((c + 4) % 8) * T + g0,
                                     w if w is not None else T)
    w1qn = lambda c, h0=0, w=DH: a1(c * DH + h0, w)
    out_sl = lambda i: a1f(8192 + (i % 2) * DM, DM)

    def gq2(g_, k):
        if g_ % 2 == 0:
            return a1(k * 512, 512)
        if k < 24:
            return a1(20480 + k * 512, 512)
        return bfx((k - 24) * 512, 512)
    wsh_s = a1f(0, 8192)
    p3_xt = lambda i: a1f(i * DM, DM)
    p3_x1 = lambda i: a1f(2048 + i * DM, DM)
    p3_attn = lambda i: a1f(4096 + i * DM, DM)
    p3_t2 = lambda i: a1f(8192 + i * DM, DM)
    p3_trash = a1f(10240, DM)

    rp = lambda p0, p1, c0=0, w=None: a3(c0, w if w is not None else T, p0, p1)
    SL2 = max(T, 8 * LC)
    def sl(i, p0=0, p1=128):
        if i == 10:
            return a3(13584, LC, p0, p1)
        if i < 8:
            return a3(i * LC, LC, p0, p1)
        return a3(SL2 + (i - 8) * LC, LC, p0, p1)
    RW = SL2 + 3 * LC
    r_m8 = a3(RW + 0 * LC, LC, 0, NCH)
    r_r8 = a3(RW + 1 * LC, LC, 0, NCH)
    r_sq8 = a3(RW + 2 * LC, LC, 0, NCH)
    r_dt8 = r_sq8
    r_dtraw = a3(RW + 3 * LC, LC, 0, NCH)
    r_tmp8 = a3(RW + 4 * LC, LC, 0, NCH)
    SM = RW + 5 * LC
    rrow = a3(SM, T, 0, 1)
    mrow_neg = a3(SM, T, 32, 33)
    msum_row = mrow_neg
    sqsum_row = a3(SM, T, 64, 65)
    dtraw_row = a3(SM, T, 96, 97)
    MS1 = SM + T
    pfinrow = a3(MS1, 128, 0, 1)
    bfinrow = a3(MS1 + 128, 128, 0, 1)
    pincrow = a3(MS1 + 256, 128, 0, 1)
    bincrow = a3(MS1 + 384, 128, 0, 1)
    aggrow = a3(MS1 + 512, 256, 0, 1)
    carrrow = a3(MS1 + 768, 128, 0, 1)
    runrow = a3(MS1 + 896, NST, 0, 1)
    runtrow = a3(MS1 + 912, NST, 0, 1)
    h0row = a3(MS1 + 928, NST, 0, 1)
    h0til = a3(MS1 + 944, 112, 0, 1)
    tmp112 = a3(MS1 + 1056, 112, 0, 1)
    hinrow = a3(MS1 + 1168, 128, 0, 1)
    assert MS1 + 1296 <= 14336, MS1 + 1296

    p4_gsb = lambda q: a3(q * 2048, 2048)
    p4_w1raw = lambda i: a3(4096 + i * DM, DM) if i < 2 else a3(1024 + (i - 2) * DM, DM)
    p4_w1t0 = a3(6144, DM)
    p4_w2raw = lambda i: a3(7168, DM)
    p4_w2t0 = p4_w1t0
    p4_x1r = lambda i: a3(8192 + i * DM, DM)
    p4_t0 = lambda i: a3(10240 + i * DM, DM)
    p5_graw = lambda i: a3(0 + i * 2048, 2048)
    p5_ffn = lambda dj: a3(4096 + dj * 512, 512)
    p5_t2 = lambda i: a3(8192 + i * DM, DM)
    p5_gt0 = lambda i: a3(10240 + i * 512, 512)
    p5_gf0 = lambda i: a3(11264 + i * 512, 512)
    p5_x1r = lambda i: a3(12288 + (i % 2) * DM, DM)

    hcf = lambda t0=0, w=None: bfx(t0, w if w is not None else T, 0, NST)
    s_mrowb = bfx(0, T, 32, 33)
    s_wcb = bfx(0, 8 * 112)
    s_d8b = bfx(4096, DM)
    p4_xqn = lambda i: bfx(i * DM, DM)
    AR3B = AR3[:, :].bitcast(BF16)
    def p4_xqf2(g_, c):
        if g_ % 2 == 0:
            return bfx(2048 + c * 512, 512)
        return AR3B[:, 24576 + c * 512:24576 + (c + 1) * 512]
    def p4_xqf_dst(g_, ti):
        base = BFX[:, 2048:6144] if g_ % 2 == 0 else AR3B[:, 24576:28672]
        return base.rearrange("p (c w) -> p c w", c=8)[:, :, ti * 128:(ti + 1) * 128]

    s_rt = sb("s_rt", (128, NT))
    s_aq = sb("s_aq", (128, NT)); s_bq = sb("s_bq", (128, NT))
    _p3s = lambda i: a3(i * 64, NT)
    s_suma = _p3s(0); s_sqa = _p3s(1)
    s_m4 = _p3s(2); s_r4 = _p3s(3); s_mr4 = _p3s(4)
    s_sum2 = _p3s(5); s_sq2 = _p3s(6)
    s_m2 = _p3s(7); s_r2 = _p3s(8)
    s_minT = _p3s(9); s_maxT = _p3s(10)
    s_tt1 = _p3s(11); s_tt2 = _p3s(12)
    s_cand = sb("s_cand", (128, 2)); s_cred = sb("s_cred", (2, 1))
    s_gmx = sb("s_gmx", (128, 32)); s_gmn = sb("s_gmn", (128, 32))
    s_gmxr = sb("s_gmxr", (128, 1)); s_gmnr = sb("s_gmnr", (128, 1))
    s_gtmp = sb("s_gtmp", (128, 1))
    s_wsc = sb("s_wsc", (112, 1))
    s_c08 = sb("s_c08", (NCH, 1))
    s_wsB = sb("s_wsB", (128, 1)); s_wsC = sb("s_wsC", (128, 1)); s_wsX = sb("s_wsX", (128, 1))
    fincol = sb("fincol", (128, 2))
    hincol = sb("hincol", (128, 1))
    s_q0 = sb("s_q0", (1, 16))
    s_q2 = sb("s_q2", (1, 16))
    s_qrep = sb("s_qrep", (128, 8))
    s_wq = sb("s_wq", (128, 2))
    s_bn = sb("s_bn", (128, 12)); s_ag2 = sb("s_ag2", (128, 2))
    s_kc = sb("s_kc", (128, 8))
    KCONST = [0.0, 1.0, 0.5, 1e-5, -0.5, -1.0, -1.0 / DM, 1.0 / DM]
    for _i, _v in enumerate(KCONST):
        nc.const_aps.register(F32, _v, s_kc[:, _i:_i + 1])
    s_m3 = sb("s_m3", (128, 4)); s_r3 = sb("s_r3", (128, 4)); s_mr3 = sb("s_mr3", (128, 4))

    PA = b.psum("PA", (128, 1024)); PB = b.psum("PB", (128, 1024))
    PC = b.psum("PC", (128, 1024)); PD = b.psum("PD", (128, 1024))
    def bank(i, p0=0, p1=128, c0=0, w=512):
        t_ = (PA, PB, PC, PD)[i // 2]
        return t_[p0:p1, (i % 2) * 512 + c0:(i % 2) * 512 + c0 + w]

    d_boot = b.sem("d_boot"); d_bootg = b.sem("d_bootg")
    d_h1 = b.sem("d_h1"); d_h2 = b.sem("d_h2")
    d_xf = b.sem("d_xf"); d_msc = b.sem("d_msc"); d_mscs = b.sem("d_mscs")
    d_xt = [b.sem(f"d_xt{i}") for i in range(2)]
    d_x1s = b.sem("d_x1s")
    d_x1r = [b.sem(f"d_x1r{i}") for i in range(4)]
    d_w1 = [b.sem(f"d_w1{i}") for i in range(4)]
    d_w2 = [b.sem("d_w20"), b.sem("d_w21")]
    d_gst = [b.sem(f"d_gst{i}") for i in range(4)]
    d_gld = [b.sem(f"d_gld{i}") for i in range(2)]
    d_outs = [b.sem("d_outs0"), b.sem("d_outs1")]

    V, S, TE, GP, SY = "vector", "scalar", "tensor", "gpsimd", "sync"

    tk_boot = []
    boot_specs = [
        (GP, s_eyef[:, :], eyef_d), (GP, s_eyeb[:, :], eyeb_d),
        (GP, s_wdt[:, :], wdt_c), (GP, s_bdt[:, :], bdt_i),
        (GP, s_alog[:, :], alog), (GP, s_b1[:, :], b1sb), (GP, s_b2[:, :], b2sb),
        (GP, s_bdiagp[:, :], bdiagp_d), (GP, s_bdiagn[:, :], bdiagn_d),
        (GP, s_r48[64:112, :], r48_d), (GP, s_hsel[:, :], hsel_d),
        ("wsh", None, None),
        (GP, s_wcb[:, :].rearrange("p (c w) -> p c w", c=8),
         wc_t.rearrange("(c p) w -> p c w", c=8)),
        (GP, s_woutb[:, :], woutT), (GP, s_dpb[:, :], dp_row), (GP, s_d8b[:, :], d8),
    ]
    bsems = [d_boot, d_bootg, b.sem("d_boot2")]
    d_wsh = [b.sem("d_wsh0"), b.sem("d_wsh1")]
    tk_wsh = ("multi",
              b.dma(SY, wsh_s[:, 0:4096], wsh_d[:, 0:4096], sem=d_wsh[0]),
              b.dma(S, wsh_s[:, 4096:8192], wsh_d[:, 4096:8192], sem=d_wsh[1]))
    gpi = [0]
    for qe, dst, src in boot_specs:
        if qe == "wsh":
            tk_boot.append(tk_wsh)
        else:
            tk_boot.append(b.dma(GP, dst, src, sem=bsems[gpi[0] % 3]))
            gpi[0] += 1
    (BT_EYEF, BT_EYEB, BT_WDT, BT_BDT, BT_ALOG, BT_B1, BT_B2, BT_BDP,
     BT_BDN, BT_R48, BT_HSEL, BT_WSH, BT_WCB, BT_WOUT, BT_DPB,
     BT_D8B) = range(16)
    NXS = 3 if 3 * T <= 12288 else 2
    xstage = lambda c: a3((c % NXS) * T, T)
    d_xs = [b.sem(f"d_xs{i}") for i in range(NXS)]
    tk_xfld = {}
    xfp = [0]
    def pump_xf(dep):
        if xfp[0] >= 8:
            return
        c = xfp[0]
        xfp[0] += 1
        tk_xfld[c] = b.dma(SY if c % 2 == 0 else S, xstage(c),
                           x_fm[c * 128:(c + 1) * 128, :], [dep],
                           sem=d_xs[c % NXS])
    for _ in range(NXS):
        pump_xf(None)

    def memsets(e):
        e.memset(s_onesb[:, :], 1.0)
        e.memset(s_ones1f[:, :], 1.0)
        e.memset(sl(10), 0.0)
        e.memset(s_gmxr[:, :], -3e38)
        e.memset(s_gmnr[:, :], 3e38)
        ins = None
        for _i, _v in enumerate(KCONST):
            ins = e.memset(s_kc[:, _i:_i + 1], _v)
        return ins
    tk_memset = b.step(V, memsets)
    b.step(S, lambda e: None, [tk_memset])

    tk_sq = []
    tk_xf = []
    def cast_sq(c):
        tkc = b.step(V, lambda e, c=c: e.tensor_copy(xf(c), xstage(c)),
                     [tk_xfld[c]])
        tks = b.step(S, lambda e, c=c: e.activation(
            xsq(c), xstage(c), AF.Square),
            [tk_xfld[c]] + ([tk_wred] if ((c + 4) % 8) * T < 16384 else []))
        tk_xf.append(tkc)
        tk_sq.append(tks)
        pump_xf(("multi", tkc, tks))
    def wred(e):
        e.tensor_reduce(s_cand[:, 0:1], wsh_s[:, 0:4096], AX.X, OP.max,
                        apply_absolute_value=True)
        return e.tensor_reduce(s_cand[:, 1:2], wsh_s[:, 4096:8192], AX.X, OP.max,
                               apply_absolute_value=True)
    if T >= 4096:
        for c in range(4):
            cast_sq(c)
        tk_wred = b.step(V, wred, [tk_boot[BT_WSH]])
        for c in range(4, 8):
            cast_sq(c)
    else:
        tk_wred = b.step(V, wred, [tk_boot[BT_WSH]])
        for c in range(8):
            cast_sq(c)
    tk_stgdead = ("multi", tk_sq[6], tk_sq[7], tk_xf[6], tk_xf[7])

    tk_wredT = b.step(TE, lambda e: e.transpose(bank(6, 0, 2, 0, 128), s_cand[:, :],
                                                s_eyef[:, :]), [tk_wred, tk_boot[BT_EYEF]])
    tk_wcred = b.step(V, lambda e: e.tensor_reduce(s_cred[:, :], bank(6, 0, 2, 0, 128),
                                                   AX.X, OP.max), [tk_wredT])
    tk_ar0s = b.dma(GP, ar0i.rearrange("a b -> b a"), s_cred[:, :], [tk_wcred], sem=d_msc)
    tk_cc0 = b.collective("AllReduce", OP.max, ar0i, ar0o, [tk_ar0s])
    tk_ar0l = b.dma(GP, s_q0[0:1, 0:2], ar0o, [tk_cc0], sem=d_msc)
    tk_woutneg = b.step(V, lambda e: e.tensor_single_scalar(
        s_woutb[:, :], s_woutb[:, :], -1.0, OP.mult), [tk_boot[BT_WOUT]])
    tk_apos = b.step(S, lambda e: e.activation(s_apos[:, :], s_alog[:, :], AF.Exp),
                     [tk_boot[BT_ALOG]])
    tk_ab = b.dma(GP, arow_b, s_apos[:, :], [tk_apos], sem=d_msc)
    tk_atil = b.dma(GP, s_atil[:, :].rearrange("c (r t) -> c r t", r=8),
                    arow_b.broadcast_to((NCH * 8, NST)).rearrange("(c r) t -> c r t", c=NCH),
                    [tk_ab], sem=d_msc)
    tk_bdA = b.step(V, lambda e: e.tensor_mul(s_bdA[:, :], s_atil[:, :], s_bdiagn[:, :]),
                    [tk_atil, tk_boot[BT_BDN]])
    tk_bdt8mm = b.step(TE, lambda e: e.matmul(bank(4, 0, NCH, 0, 1), s_ones1f[0:1, 0:NCH],
                                              s_bdt[:, :], start=True, stop=True),
                       [tk_boot[BT_BDT], tk_memset])
    tk_bdt8 = b.step(V, lambda e: e.tensor_copy(s_bdt8[:, :], bank(4, 0, NCH, 0, 1)),
                     [tk_bdt8mm])

    tk_mevac = []
    prev = {0: None, 1: None, 2: None, 3: None}
    for gi in range(G):
        bk = gi % 4
        def mm(e, gi=gi, bk=bk):
            ins = None
            for c in range(8):
                ins = e.matmul(bank(bk, 0, 1), s_onesb[:, :],
                               xf(c, gi * 512, 512), start=(c == 0), stop=(c == 7))
            return ins
        tkm = b.step(TE, mm, [tk_xf[7], tk_memset, prev[bk]])
        tkev = b.step(V, lambda e, gi=gi, bk=bk: e.tensor_copy(
            msum_row[0:1, gi * 512:(gi + 1) * 512], bank(bk, 0, 1)), [tkm])
        prev[bk] = tkev
        tk_mevac.append(tkev)
    tk_sqevac = []
    for gi in range(G):
        bk = gi % 4
        def mm(e, gi=gi, bk=bk):
            ins = None
            for c in range(8):
                ins = e.matmul(bank(bk, 0, 1), s_onesb[:, :],
                               xsq(c, gi * 512, 512), start=(c == 0), stop=(c == 7))
            return ins
        tkm = b.step(TE, mm, [tk_sq[7], prev[bk]])
        tkev = b.step(V, lambda e, gi=gi, bk=bk: e.tensor_copy(
            sqsum_row[0:1, gi * 512:(gi + 1) * 512], bank(bk, 0, 1)), [tkm])
        prev[bk] = tkev
        tk_sqevac.append(tkev)
    sq_done = tk_sqevac[-1]

    tk_rpev = []
    prevrp = {4: None, 5: None, 6: tk_wcred, 7: None}
    for gi in range(G):
        bk = 4 + gi % 4
        def mm(e, gi=gi, bk=bk):
            ins = None
            for c in range(8):
                ins = e.matmul(bank(bk, 0, 112), s_wcb[:, c * 112:(c + 1) * 112],
                               xf(c, gi * 512, 512), start=(c == 0), stop=(c == 7))
            return ins
        tkm = b.step(TE, mm, [tk_xf[7], tk_sq[6], tk_boot[BT_WCB], prevrp[bk]])
        tkev = b.step(V, lambda e, gi=gi, bk=bk: e.tensor_copy(
            rp(0, 112, gi * 512, 512), bank(bk, 0, 112)), [tkm])
        prevrp[bk] = tkev
        tk_rpev.append(tkev)

    def mm_wsum(e):
        ins = None
        for c in range(8):
            ins = e.matmul(bank(4, 0, 112, 0, 1), s_wcb[:, c * 112:(c + 1) * 112],
                           s_onesb[:, :], start=(c == 0), stop=(c == 7))
        return ins
    tk_wsmm = b.step(TE, mm_wsum, [tk_boot[BT_WCB], tk_memset, tk_bdt8,
                                   tk_rpev[-2] if G >= 2 else tk_rpev[-1]])
    tk_wsev = b.step(V, lambda e: e.tensor_copy(s_wsc[:, :], bank(4, 0, 112, 0, 1)),
                     [tk_wsmm])
    def mm_ws2(e):
        e.matmul(bank(4, 0, 1, 0, 1), s_wdt[:, :], s_wsc[0:64, 0:1], start=True, stop=True)
        e.matmul(bank(5, 0, 128, 0, 1), s_r48B[:, :], s_wsc[64:112, 0:1],
                 start=True, stop=True)
        e.matmul(bank(5, 0, 128, 1, 1), s_r48C[:, :], s_wsc[64:112, 0:1],
                 start=True, stop=True)
        return e.matmul(bank(5, 0, 128, 2, 1), s_r48X[:, :], s_wsc[64:112, 0:1],
                        start=True, stop=True)
    tk_ws2 = b.step(TE, mm_ws2, [tk_wsev, tk_boot[BT_WDT], tk_boot[BT_R48],
                                 tk_rpev[-1]])
    tk_wsv = b.step(V, lambda e: (
        e.tensor_copy(s_wsB[:, :], bank(5, 0, 128, 0, 1)),
        e.tensor_copy(s_wsC[:, :], bank(5, 0, 128, 1, 1)),
        e.tensor_copy(s_wsX[:, :], bank(5, 0, 128, 2, 1)))[-1], [tk_ws2])
    tk_wsneg = b.step(V, lambda e: (
        e.tensor_single_scalar(s_wsB[:, :], s_wsB[:, :], -1.0, OP.mult),
        e.tensor_single_scalar(s_wsC[:, :], s_wsC[:, :], -1.0, OP.mult),
        e.tensor_single_scalar(s_wsX[:, :], s_wsX[:, :], -1.0, OP.mult))[-1], [tk_wsv])
    tk_c0ev = b.step(V, lambda e: e.tensor_copy(s_q0[0:1, 6:7], bank(4, 0, 1, 0, 1)),
                     [tk_ws2])
    tk_c0mm = b.step(TE, lambda e: e.matmul(bank(4, 0, NCH, 4, 1), s_ones1f[0:1, 0:NCH],
                                            s_q0[0:1, 6:7], start=True, stop=True),
                     [tk_c0ev])
    tk_c08 = b.step(V, lambda e: e.tensor_copy(s_c08[:, :], bank(4, 0, NCH, 4, 1)),
                    [tk_c0mm])

    tk_dtev = []
    prevdt = {0: prev[0], 1: prev[1]}
    for gi in range(G):
        bk = gi % 2
        tkm = b.step(TE, lambda e, gi=gi, bk=bk: e.matmul(
            bank(bk, 0, 1), s_wdt[:, :], rp(0, 64, gi * 512, 512),
            start=True, stop=True),
            [tk_rpev[gi], tk_boot[BT_WDT], prevdt[bk]])
        tkev = b.step(V, lambda e, gi=gi, bk=bk: e.tensor_copy(
            dtraw_row[0:1, gi * 512:(gi + 1) * 512], bank(bk, 0, 1)), [tkm])
        prevdt[bk] = tkev
        tk_dtev.append(tkev)

    def swcalc(e):
        e.tensor_scalar(s_q0[0:1, 2:4], s_q0[0:1, 0:2], 1.0 / 127.0, 1e-8, OP.mult, OP.max)
        e.drain()
        return e.reciprocal(s_q0[0:1, 4:6], s_q0[0:1, 2:4])
    tk_sw = b.step(V, swcalc, [tk_ar0l])
    tk_swmm = b.step(TE, lambda e: e.matmul(bank(6, 0, 128, 0, 2), s_ones1f[0:1, :],
                                            s_q0[0:1, 4:6], start=True, stop=True),
                     [tk_sw, tk_dtev[-1]])
    tk_wq = b.step(V, lambda e: e.tensor_copy(s_wq[:, :], bank(6, 0, 128, 0, 2)), [tk_swmm])

    tk_br1 = b.dma(SY, rowA, msum_row, [tk_mevac[-1]], sem=d_h1)
    tk_br2 = b.dma(SY, r_m8, rowA.rearrange("o (c t) -> (o c) t", c=8),
                   [tk_br1, tk_stgdead], sem=d_h1)
    tk_br3 = b.dma(S, rowB, sqsum_row, [tk_sqevac[-1]], sem=d_h2)
    tk_br4 = b.dma(S, r_sq8, rowB.rearrange("o (c t) -> (o c) t", c=8),
                   [tk_br3, tk_stgdead], sem=d_h2)
    tk_br5 = b.dma(SY, rowC, dtraw_row, [tk_dtev[-1]], sem=d_h1)
    tk_br6 = b.dma(SY, r_dtraw, rowC.rearrange("o (c t) -> (o c) t", c=8),
                   [tk_br5, tk_stgdead], sem=d_h1)

    tk_m8 = b.step(S, lambda e: e.activation(r_m8, r_m8, AF.Identity, scale=-1.0 / DM),
                   [tk_br2])
    tk_e2 = b.step(S, lambda e: e.activation(r_sq8, r_sq8, AF.Identity, scale=1.0 / DM),
                   [tk_br4])
    tk_msq = b.step(V, lambda e: e.tensor_mul(r_tmp8, r_m8, r_m8), [tk_m8])
    tk_var = b.step(V, lambda e: e.tensor_sub(r_sq8, r_sq8, r_tmp8), [tk_e2, tk_msq])
    tk_lnv = b.step(S, lambda e: e.activation(r_tmp8, r_sq8, AF.Ln, bias=1e-5), [tk_var])
    tk_r8 = b.step(S, lambda e: e.activation(r_r8, r_tmp8, AF.Exp, scale=-0.5), [tk_lnv])

    tk_dt1 = b.step(V, lambda e: e.scalar_tensor_tensor(
        r_tmp8, r_m8, s_c08[:, :], r_dtraw, OP.mult, OP.add),
        [tk_m8, tk_br6, tk_c08, tk_r8])
    tk_dt2 = b.step(V, lambda e: e.tensor_mul(r_tmp8, r_tmp8, r_r8), [tk_dt1, tk_r8])
    tk_dt3 = b.step(S, lambda e: e.activation(r_dt8, r_tmp8, AF.Exp, bias=s_bdt8[:, :]),
                    [tk_dt2, tk_bdt8])
    tk_dt4 = b.step(S, lambda e: e.activation(r_dt8, r_dt8, AF.Ln, bias=1.0), [tk_dt3])

    tk_bx1 = b.dma(S, rs48, rp(64, 112), [tk_rpev[-1]], sem=d_h2)
    tk_Br = b.dma(SY, sl(4), rs48[0:16, :].rearrange("n (c t) -> c n t", c=8),
                  [tk_bx1, tk_dtev[-1], tk_stgdead], sem=d_h1)
    tk_Cr = b.dma(S, sl(5), rs48[16:32, :].rearrange("n (c t) -> c n t", c=8),
                  [tk_bx1, tk_dtev[-1], tk_stgdead], sem=d_h2)
    tk_Xr = b.dma(SY, sl(6), rs48[32:48, :].rearrange("n (c t) -> c n t", c=8),
                  [tk_bx1, tk_dtev[-1], tk_stgdead], sem=d_h1)

    tk_op1 = b.step(TE, lambda e: e.matmul(bank(5, 0, 128, 0, LC), s_bdA[:, :], r_dt8,
                                           start=True, stop=True),
                    [tk_dt4, tk_bdA, tk_wsv])
    tk_Abar = b.step(S, lambda e: e.activation(sl(0), bank(5, 0, 128, 0, LC), AF.Exp),
                     [tk_op1, tk_bx1, tk_dtev[-1]])
    tk_op2 = b.step(TE, lambda e: e.matmul(bank(6, 0, 128, 0, LC), s_bdiagp[:, :], r_dt8,
                                           start=True, stop=True),
                    [tk_dt4, tk_boot[BT_BDP], tk_dtev[-1]])
    tk_dts = b.step(V, lambda e: e.tensor_copy(sl(1), bank(6, 0, 128, 0, LC)),
                    [tk_op2, tk_bx1])
    tk_op3 = b.step(TE, lambda e: e.matmul(bank(0, 0, 128, 0, LC), s_bdiagp[:, :], r_m8,
                                           start=True, stop=True),
                    [tk_m8, tk_op1, tk_dtev[-1]])
    tk_ms = b.step(V, lambda e: e.tensor_copy(sl(2), bank(0, 0, 128, 0, LC)),
                   [tk_op3, tk_bx1])
    tk_op4 = b.step(TE, lambda e: e.matmul(bank(1, 0, 128, 0, LC), s_bdiagp[:, :], r_r8,
                                           start=True, stop=True),
                    [tk_r8, tk_op2, tk_dtev[-1]])
    tk_rs = b.step(V, lambda e: e.tensor_copy(sl(3), bank(1, 0, 128, 0, LC)),
                   [tk_op4, tk_bx1])
    def dtr2(e):
        e.tensor_mul(sl(1), sl(1), sl(3))
        e.drain()
        return e.tensor_mul(sl(1), sl(1), sl(3))
    tk_dtr2s = b.step(V, dtr2, [tk_dts, tk_rs])

    tk_Bc = b.step(V, lambda e: e.scalar_tensor_tensor(
        sl(4), sl(2), s_wsB[:, :], sl(4), OP.mult, OP.subtract),
        [tk_ms, tk_Br, tk_wsneg])
    tk_Xc = b.step(V, lambda e: e.scalar_tensor_tensor(
        sl(6), sl(2), s_wsX[:, :], sl(6), OP.mult, OP.subtract),
        [tk_ms, tk_Xr, tk_wsneg])
    tk_Bx1 = b.step(V, lambda e: e.tensor_mul(sl(7), sl(4), sl(6)), [tk_Bc, tk_Xc])
    tk_Bx = b.step(V, lambda e: e.tensor_mul(sl(7), sl(7), sl(1)), [tk_Bx1, tk_dtr2s])

    tk_hloc = b.step(V, lambda e: e.tensor_tensor_scan(
        sl(8), sl(0), sl(7), 0.0, OP.mult, OP.add), [tk_Abar, tk_Bx])
    tk_cumP = b.step(V, lambda e: e.tensor_tensor_scan(
        sl(9), sl(0), sl(10), 1.0, OP.mult, OP.add), [tk_hloc, tk_memset])

    tk_fin = b.step(V, lambda e: (
        e.tensor_copy(fincol[:, 0:1], sl(9)[:, LC - 1:LC]),
        e.tensor_copy(fincol[:, 1:2], sl(8)[:, LC - 1:LC]))[-1], [tk_cumP])
    def finT(e):
        e.transpose(bank(7, 0, 1, 0, 128), fincol[:, 0:1], s_eyef[:, :])
        return e.transpose(bank(7, 0, 1, 128, 128), fincol[:, 1:2], s_eyef[:, :])
    tk_finT = b.step(TE, finT, [tk_fin])
    tk_finr = b.step(V, lambda e: (
        e.tensor_copy(pfinrow[:, :], bank(7, 0, 1, 0, 128)),
        e.tensor_copy(bfinrow[:, :], bank(7, 0, 1, 128, 128)))[-1], [tk_finT])

    def sumscan(e):
        Prow = pfinrow[0:1, :].rearrange("o (c n) -> o c n", c=8)
        Brow = bfinrow[0:1, :].rearrange("o (c n) -> o c n", c=8)
        pv = pincrow[0:1, :].rearrange("o (c n) -> o c n", c=8)
        bv = bincrow[0:1, :].rearrange("o (c n) -> o c n", c=8)
        ins = None
        for n in range(NST):
            e.tensor_tensor_scan(pv[:, :, n], Prow[:, :, n], sl(10)[0:1, 0:NCH],
                                 1.0, OP.mult, OP.add)
        for n in range(NST):
            ins = e.tensor_tensor_scan(bv[:, :, n], Prow[:, :, n], Brow[:, :, n],
                                       0.0, OP.mult, OP.add)
        return ins
    tk_sscan = b.step(V, sumscan, [tk_finr, tk_memset])

    tk_ag_s1 = b.dma(SY, ag1i[0:1, :], pincrow[0:1, 112:128], [tk_sscan], sem=d_h1)
    tk_ag_s2 = b.dma(S, ag1i[1:2, :], bincrow[0:1, 112:128], [tk_sscan], sem=d_h2)
    tk_cc1 = b.collective("AllGather", OP.bypass, ag1i, ag1o, [tk_ag_s1, tk_ag_s2])
    tk_agl = b.dma(SY, aggrow, ag1o.rearrange("(o a) b -> o (a b)", o=1), [tk_cc1],
                   sem=d_h1)

    def combine(e):
        e.memset(runrow[:, :], 0.0)
        for j in range(NCORES):
            if j == 4:
                e.drain()
                e.memset(runrow[:, :], 0.0)
            e.drain()
            e.tensor_copy(carrrow[0:1, NST * j:NST * (j + 1)], runrow[:, :])
            Pj = aggrow[0:1, 32 * j:32 * j + 16]
            Bj = aggrow[0:1, 32 * j + 16:32 * j + 32]
            e.drain()
            e.tensor_mul(runtrow[:, :], Pj, runrow[:, :])
            e.drain()
            e.tensor_add(runrow[:, :], runtrow[:, :], Bj)
        e.drain()
        e.tensor_mul(carrrow[:, :], carrrow[:, :], s_hsel[:, :])
        e.drain()
        e.tensor_add(carrrow[0:1, 0:64], carrrow[0:1, 0:64], carrrow[0:1, 64:128])
        e.drain()
        e.tensor_add(carrrow[0:1, 0:32], carrrow[0:1, 0:32], carrrow[0:1, 32:64])
        e.drain()
        return e.tensor_add(h0row[:, :], carrrow[0:1, 0:16], carrrow[0:1, 16:32])
    tk_h0 = b.step(V, combine, [tk_agl, tk_boot[BT_HSEL]])

    tk_h0s = b.dma(SY, h0b, h0row, [tk_h0], sem=d_h1)
    tk_h0t = b.dma(SY, h0til.rearrange("o (r t) -> o r t", r=7),
                   h0b.broadcast_to((7, NST)).rearrange("(o r) t -> o r t", o=1),
                   [tk_h0s], sem=d_h1)
    def hin(e):
        e.tensor_copy(hinrow[0:1, 0:16], h0row[:, :])
        e.tensor_mul(tmp112[:, :], pincrow[0:1, 0:112], h0til[:, :])
        e.drain()
        return e.tensor_add(hinrow[0:1, 16:128], tmp112[:, :], bincrow[0:1, 0:112])
    tk_hinr = b.step(V, hin, [tk_h0t, tk_h0, tk_sscan])
    tk_hinT = b.step(TE, lambda e: e.transpose(bank(7, 0, 128, 0, 1), hinrow[:, :],
                                               s_eyef[0:1, 0:1]),
                     [tk_hinr, tk_finT, tk_finr])
    tk_hin = b.step(V, lambda e: e.tensor_copy(hincol[:, :], bank(7, 0, 128, 0, 1)),
                    [tk_hinT])

    tk_h = b.step(V, lambda e: e.scalar_tensor_tensor(
        sl(0), sl(9), hincol[:, :], sl(8), OP.mult, OP.add),
        [tk_hin, tk_cumP, tk_hloc])
    tk_Cc = b.step(V, lambda e: e.scalar_tensor_tensor(
        sl(5), sl(2), s_wsC[:, :], sl(5), OP.mult, OP.subtract),
        [tk_Cr, tk_Bx1, tk_wsneg])
    tk_hc = b.step(V, lambda e: e.tensor_mul(sl(7), sl(0), sl(5)),
                   [tk_h, tk_Cc, tk_Bx])

    tk_hcs = b.dma(GP, hcb, sl(7), [tk_hc], sem=d_msc)
    tk_hcf = b.dma(SY, hcf()[:, :].rearrange("n (c t) -> n c t", c=8),
                   hcb.rearrange("(c n) t -> n c t", c=8),
                   [tk_hcs, tk_rpev[-1], tk_wsev], sem=d_h1)

    tk_rr1 = b.dma(S, rowD, r_r8, [tk_r8, tk_op4], sem=d_h2)
    tk_rr2 = b.dma(S, rrow.rearrange("o (c t) -> o c t", c=8),
                   rowD.rearrange("o (c t) -> o c t", c=8),
                   [tk_rr1, tk_br3, tk_br4], sem=d_h2)
    tk_mr1 = b.dma(SY, rowA, r_m8, [tk_m8, tk_br2, tk_msq, tk_dt1, tk_op3, tk_br1],
                   sem=d_h1)
    tk_mr2 = b.dma(SY, mrow_neg.rearrange("o (c t) -> o c t", c=8),
                   rowA.rearrange("o (c t) -> o c t", c=8),
                   [tk_mr1, tk_mevac[-1]], sem=d_h1)
    tk_mrb = b.step(S, lambda e: e.activation(s_mrowb[:, :], mrow_neg, AF.Identity),
                    [tk_mr2, tk_wsmm])
    def rt_mm(e):
        ins = None
        for tt in range(NT):
            ins = e.transpose(bank(7, 0, 128, tt, 1), rrow[0:1, tt * 128:(tt + 1) * 128],
                              s_eyef[0:1, 0:1])
        return ins
    tk_rtT = b.step(TE, rt_mm, [tk_rr2, tk_hinT, tk_hin])
    tk_rt = b.step(V, lambda e: e.tensor_copy(s_rt[:, :], bank(7, 0, 128, 0, NT)),
                   [tk_rtT])

    NBLK = 4
    tk_comb = {}
    tk_x1 = {}
    tk_x1st = {}
    tk_xtld = {}
    tk_sqa = {}
    xtp = [0]
    def pump_xt(dep):
        if xtp[0] >= NT:
            return
        tt = xtp[0]
        xtp[0] += 1
        tk_xtld[tt] = b.dma(SY if tt % 2 == 0 else S, p3_xt(tt % 2),
                            x_tm[tt * 128:(tt + 1) * 128, :],
                            [sq_done, dep], sem=d_xt[tt % 2])
    pump_xt(None)
    pump_xt(None)
    x1_free = [sq_done, sq_done]
    attn_free = [sq_done] * NBLK
    t2_free = [sq_done, sq_done]
    pend_sq2 = [None]
    def emit_sq2():
        nonlocal_state = pend_sq2[0]
        if nonlocal_state is None:
            return
        tt, sl_ = nonlocal_state
        pend_sq2[0] = None
        tk_s2 = b.step(S, lambda e, tt=tt, sl_=sl_: e.activation(
            p3_trash, p3_x1(sl_), AF.Square, accum_out=s_sq2[:, tt:tt + 1]),
            [tk_x1[tt], _trash[0]])
        _trash[0] = tk_s2
        _lasts2[0] = tk_s2
        x1_free[sl_] = ("multi", x1_free[sl_], tk_s2)
    _trash = [sq_done]
    _lasts2 = [None]
    ybank_free = [None, None]
    last_mn = None

    for blk in range(NT // NBLK):
        for ti in range(NBLK):
            tt = blk * NBLK + ti
            par = tt % 2
            yb = (PA, PC)[par]
            sl_ = tt % 2
            def mm(e, tt=tt, yb=yb):
                for half in range(2):
                    e.matmul(yb[0:128, half * 512:(half + 1) * 512],
                             hcf(tt * 128, 128),
                             s_woutb[:, half * 512:(half + 1) * 512],
                             start=True, stop=False)
                for c in range(8):
                    e.matmul(yb[0:128, c * 128:(c + 1) * 128],
                             xf(c, tt * 128, 128), s_d8b[:, c * 128:(c + 1) * 128],
                             start=False, stop=False)
                e.matmul(yb[0:128, 0:512], s_mrowb[0:1, tt * 128:(tt + 1) * 128],
                         s_dpb[0:1, 0:512], start=False, stop=True)
                return e.matmul(yb[0:128, 512:1024],
                                s_mrowb[0:1, tt * 128:(tt + 1) * 128],
                                s_dpb[0:1, 512:1024], start=False, stop=True)
            deps = [tk_hcf, tk_mrb, tk_woutneg, tk_boot[BT_DPB], tk_boot[BT_D8B],
                    tk_rt, ybank_free[par]]
            if tt < 2:
                deps += [tk_rtT]
            tk_mm = b.step(TE, mm, deps)
            def comb(e, tt=tt, ti=ti, yb=yb):
                return e.activation(p3_attn(ti), yb[0:128, 0:1024], AF.Identity,
                                    scale=s_rt[:, tt:tt + 1],
                                    accum_out=s_suma[:, tt:tt + 1])
            tk_comb[tt] = b.step(S, comb, [tk_mm, tk_rt, attn_free[ti]])
            ybank_free[par] = tk_comb[tt]
            tk_sqa[tt] = b.step(S, lambda e, tt=tt, ti=ti: e.activation(
                p3_trash, p3_attn(ti), AF.Square, accum_out=s_sqa[:, tt:tt + 1]),
                [tk_comb[tt], _trash[0]])
            _trash[0] = tk_sqa[tt]
        b0, b1_ = blk * NBLK, (blk + 1) * NBLK
        def bst1(e, b0=b0, b1_=b1_):
            e.activation(s_m4[:, b0:b1_], s_suma[:, b0:b1_], AF.Identity, scale=-1.0 / DM)
            e.drain()
            return e.activation(s_tt1[:, b0:b1_], s_sqa[:, b0:b1_], AF.Identity,
                                scale=1.0 / DM)
        tk_b1 = b.step(S, bst1, [tk_sqa[b1_ - 1], tk_comb[b1_ - 1]])
        def bst2(e, b0=b0, b1_=b1_):
            e.tensor_mul(s_tt2[:, b0:b1_], s_m4[:, b0:b1_], s_m4[:, b0:b1_])
            e.drain()
            return e.tensor_sub(s_tt1[:, b0:b1_], s_tt1[:, b0:b1_], s_tt2[:, b0:b1_])
        tk_b2 = b.step(V, bst2, [tk_b1])
        def bst3(e, b0=b0, b1_=b1_):
            e.activation(s_tt2[:, b0:b1_], s_tt1[:, b0:b1_], AF.Ln, bias=1e-5)
            e.drain()
            return e.activation(s_r4[:, b0:b1_], s_tt2[:, b0:b1_], AF.Exp, scale=-0.5)
        tk_b3 = b.step(S, bst3, [tk_b2])
        tk_b4 = b.step(V, lambda e, b0=b0, b1_=b1_: e.tensor_mul(
            s_mr4[:, b0:b1_], s_m4[:, b0:b1_], s_r4[:, b0:b1_]), [tk_b3])
        for ti in range(NBLK):
            tt = blk * NBLK + ti
            sl_ = tt % 2
            tk_t2 = b.step(S, lambda e, tt=tt, sl_=sl_: e.activation(
                p3_t2(sl_), p3_xt(sl_), AF.Identity, bias=s_mr4[:, tt:tt + 1]),
                [tk_xtld[tt], tk_b4, t2_free[sl_]])
            pump_xt(tk_t2)
            emit_sq2()
            def x1f(e, tt=tt, ti=ti, sl_=sl_):
                return e.scalar_tensor_tensor(
                    p3_x1(sl_), p3_attn(ti), s_r4[:, tt:tt + 1], p3_t2(sl_),
                    OP.mult, OP.add, accum_out=s_sum2[:, tt:tt + 1])
            tk_x1[tt] = b.step(V, x1f, [tk_t2, tk_comb[tt], tk_b4, x1_free[sl_]])
            attn_free[ti] = tk_x1[tt]
            t2_free[sl_] = tk_x1[tt]
            tk_mn = b.step(V, lambda e, tt=tt, sl_=sl_: (
                e.tensor_reduce(s_minT[:, tt:tt + 1], p3_x1(sl_), AX.X, OP.min),
                e.tensor_reduce(s_maxT[:, tt:tt + 1], p3_x1(sl_), AX.X, OP.max))[-1],
                [tk_x1[tt]])
            last_mn = tk_mn
            pend_sq2[0] = (tt, sl_)
            tk_x1st[tt] = b.dma(GP, x1_d[tt * 128:(tt + 1) * 128, :], p3_x1(sl_),
                                [tk_x1[tt]], sem=d_x1s)
            x1_free[sl_] = ("multi", tk_mn, tk_x1st[tt])

    emit_sq2()
    def m2c(e):
        e.activation(s_m2[:, :], s_sum2[:, :], AF.Identity, scale=-1.0 / DM)
        e.drain()
        return e.activation(s_tt1[:, :], s_sq2[:, :], AF.Identity, scale=1.0 / DM)
    tk_m2 = b.step(S, m2c, [tk_x1[NT - 1], _lasts2[0]])
    def v2a(e):
        e.tensor_mul(s_tt2[:, :], s_m2[:, :], s_m2[:, :])
        e.drain()
        return e.tensor_sub(s_tt1[:, :], s_tt1[:, :], s_tt2[:, :])
    tk_v2a = b.step(V, v2a, [tk_m2])
    def s2b(e):
        e.activation(s_tt2[:, :], s_tt1[:, :], AF.Ln, bias=1e-5)
        e.drain()
        return e.activation(s_r2[:, :], s_tt2[:, :], AF.Exp, scale=-0.5)
    tk_s2b = b.step(S, s2b, [tk_v2a])
    def cnd(e):
        e.tensor_add(s_tt1[:, :], s_maxT[:, :], s_m2[:, :])
        e.tensor_add(s_tt2[:, :], s_minT[:, :], s_m2[:, :])
        e.drain()
        e.tensor_mul(s_tt1[:, :], s_tt1[:, :], s_r2[:, :])
        e.tensor_mul(s_tt2[:, :], s_tt2[:, :], s_r2[:, :])
        e.drain()
        e.tensor_single_scalar(s_tt2[:, :], s_tt2[:, :], -1.0, OP.mult)
        e.drain()
        e.tensor_reduce(s_cand[:, 0:1], s_tt1[:, :], AX.X, OP.max)
        return e.tensor_reduce(s_cand[:, 1:2], s_tt2[:, :], AX.X, OP.max)
    tk_cnd = b.step(V, cnd, [tk_s2b, last_mn])
    tk_cndT = b.step(TE, lambda e: e.transpose(bank(7, 0, 2, 0, 128), s_cand[:, :],
                                               s_eyef[:, :]), [tk_cnd, tk_rtT, tk_rt])
    tk_cnd2 = b.step(V, lambda e: e.tensor_reduce(s_cred[:, :], bank(7, 0, 2, 0, 128),
                                                  AX.X, OP.max), [tk_cndT])
    tk_ar2s = b.dma(GP, ar2i.rearrange("a b -> b a"), s_cred[:, :], [tk_cnd2], sem=d_msc)
    tk_cc2 = b.collective("AllReduce", OP.max, ar2i, ar2o, [tk_ar2s])
    tk_ar2l = b.dma(GP, s_q0[0:1, 8:10], ar2o, [tk_cc2], sem=d_msc)

    def qs1(e):
        e.tensor_add(s_q0[0:1, 10:11], s_q0[0:1, 8:9], s_q0[0:1, 9:10])
        e.drain()
        e.tensor_scalar(s_q0[0:1, 10:11], s_q0[0:1, 10:11], 1.0 / 255.0, 1e-8,
                        OP.mult, OP.max)
        e.drain()
        e.reciprocal(s_q0[0:1, 11:12], s_q0[0:1, 10:11])
        e.drain()
        e.tensor_mul(s_q0[0:1, 12:13], s_q0[0:1, 9:10], s_q0[0:1, 11:12])
        e.drain()
        e.tensor_scalar(s_q0[0:1, 14:15], s_q0[0:1, 12:13], MAGIC, MAGIC,
                        OP.add, OP.subtract)
        e.drain()
        e.tensor_scalar(s_q0[0:1, 14:15], s_q0[0:1, 14:15], -128.0, None, OP.add)
        e.drain()
        e.tensor_scalar(s_q0[0:1, 14:15], s_q0[0:1, 14:15], -128.0, 127.0,
                        OP.max, OP.min)
        e.drain()
        e.tensor_scalar(s_q0[0:1, 12:13], s_q0[0:1, 14:15], -1.0, -128.0,
                        OP.mult, OP.add)
        e.tensor_scalar(s_q0[0:1, 13:14], s_q0[0:1, 14:15], -1.0, 127.0,
                        OP.mult, OP.add)
        e.drain()
        return e.tensor_mul(s_q0[0:1, 15:16], s_q0[0:1, 10:11], s_q0[0:1, 2:3])
    tk_qs1 = b.step(V, qs1, [tk_ar2l, tk_sw])
    tk_qrmm1 = b.step(TE, lambda e: e.matmul(bank(7, 0, 128, 0, 4), s_ones1f[0:1, :],
                                             s_q0[0:1, 12:16], start=True, stop=True),
                      [tk_qs1, tk_cndT, tk_cnd2])
    tk_qrep1 = b.step(V, lambda e: (
        e.tensor_copy(s_qrep[:, 0:1], bank(7, 0, 128, 0, 1)),
        e.tensor_copy(s_qrep[:, 1:2], bank(7, 0, 128, 1, 1)),
        e.tensor_copy(s_qrep[:, 2:3], bank(7, 0, 128, 3, 1)))[-1], [tk_qrmm1])
    tk_qrmm2 = b.step(TE, lambda e: e.matmul(bank(7, 0, 128, 4, 1), s_ones1f[0:1, :],
                                             s_q0[0:1, 11:12], start=True, stop=True),
                      [tk_qs1, tk_qrmm1])
    tk_qrep2 = b.step(V, lambda e: e.tensor_copy(s_qrep[:, 7:8], bank(7, 0, 128, 4, 1)),
                      [tk_qrmm2])
    def aqbq(e):
        e.tensor_scalar(s_aq[:, :], s_r2[:, :], s_qrep[:, 7:8], None, OP.mult)
        e.drain()
        return e.tensor_mul(s_bq[:, :], s_m2[:, :], s_aq[:, :])
    tk_aq = b.step(V, aqbq, [tk_qrep2, tk_s2b, tk_cnd])

    all_x1st = Tok(d_x1s, d_x1s.n, "_", True)

    tk_bar34 = b.barrier()

    tk_w1q = {}
    w1_last_act = [None, None, None, None]
    t0scr_free = [None]
    w1_seq = [(hh, c) for hh in range(4) for c in range(8)]
    w1_slot = lambda idx: (idx % 4) if idx < 16 else (idx % 2)
    w1_next = {}
    _seen = {}
    for _i in range(32):
        _s = w1_slot(_i)
        if _s in _seen:
            w1_next[_seen[_s]] = _i
        _seen[_s] = _i
    w1_ld = {}
    def w1_load(idx, dep):
        hh, c = w1_seq[idx]
        sl_ = w1_slot(idx)
        w1_ld[idx] = b.dma(SY if sl_ % 2 == 0 else S, p4_w1raw(sl_),
                           w1t[c * 128:(c + 1) * 128, hh * 1024:(hh + 1) * 1024],
                           [tk_cnd, dep], sem=d_w1[sl_])
    def w1_emit(idx):
        hh, c = w1_seq[idx]
        sl_ = w1_slot(idx)
        tkt0 = b.step(S, lambda e, sl_=sl_: e.activation(
            p4_w1t0, p4_w1raw(sl_), AF.Identity, scale=s_wq[:, 0:1]),
            [w1_ld[idx], tk_wq, t0scr_free[0]])
        w1_last_act[sl_] = tkt0
        tkq = b.step(V, lambda e, c=c, hh=hh: e.tensor_scalar(
            w1qn(c, hh * 1024, 1024), p4_w1t0, MAGIC, MAGIC, OP.add, OP.subtract),
            [tkt0, all_x1st])
        t0scr_free[0] = tkq
        tk_w1q[(c, hh)] = tkq
        if idx in w1_next:
            w1_load(w1_next[idx], tkt0)
    for idx in range(4):
        w1_load(idx, None)
    for idx in range(32):
        w1_emit(idx)
    gsb_free2 = [w1_last_act[2], w1_last_act[3]]

    tk_w2q = {}
    w2_last_act = {0: None, 1: None}
    w2_slotmap = lambda k: (k % 2) if G > 1 else 0
    def p4_w2slot(s):
        return p4_w2raw(0) if s == 0 else p4_w1raw(0)
    w2_ld = {}
    w2lp = [0]
    def w2_pump():
        if w2lp[0] >= 32:
            return
        k = w2lp[0]
        w2lp[0] += 1
        s = w2_slotmap(k)
        dep = w2_last_act[s]
        if s == 1 and dep is None:
            dep = w1_last_act[0]
        w2_ld[k] = b.dma(SY if k % 2 == 0 else S, p4_w2slot(s),
                         w2t[k * 128:(k + 1) * 128, :],
                         [tk_bar34, dep], sem=d_w2[s])
    def w2_emit(k):
        s = w2_slotmap(k)
        tkt0 = b.step(S, lambda e, s=s: e.activation(
            p4_w2t0, p4_w2slot(s), AF.Identity, scale=s_wq[:, 1:2]),
            [w2_ld[k], tk_wq, t0scr_free[0]])
        w2_last_act[s] = tkt0
        tkq = b.step(V, lambda e, k=k: e.tensor_scalar(
            w2qn(k), p4_w2t0, MAGIC, MAGIC, OP.add, OP.subtract), [tkt0])
        t0scr_free[0] = tkq
        tk_w2q[k] = tkq
        w2_pump()
    def w2_start():
        if w2lp[0] == 0:
            w2_pump()
            w2_pump()

    xqn_free = [tk_hcf, ("multi", tk_mrb)]
    xqf_prev = {0: ("multi", tk_hcf, tk_mrb), 1: None}
    t0_free = [None, None]
    gsb_free = gsb_free2
    h_bank_free = {4: None, 5: None, 6: None, 7: None}
    tqbank_free = [None, None]
    tk_gst = {}
    gmm_last = None

    tk_xq = {}
    tk_clip4 = {}
    last_mm1_of = {}
    tk_x4ld = {}
    x4_seq = [(g_, ti) for g_ in range(G) for ti in range(4)]
    x4p = [0]
    def pump_x4(dep):
        if x4p[0] >= len(x4_seq):
            return
        g_, ti = x4_seq[x4p[0]]
        x4p[0] += 1
        tt = g_ * 4 + ti
        tk_x4ld[(g_, ti)] = b.dma(
            SY, p4_x1r(ti % 2), x1_d[tt * 128:(tt + 1) * 128, :],
            [tk_cnd, tk_x1st[tt], dep], sem=d_x1r[ti % 2])
    pump_x4(None)
    pump_x4(None)

    tk_f04 = {}
    def q4_pre(qg, ti):
        sl_ = ti % 2
        tt = qg * 4 + ti
        if qg <= 1 and ti % 2 == 1:
            tkt0 = b.step(V, lambda e, sl_=sl_, tt=tt: e.tensor_scalar(
                p4_t0(sl_), p4_x1r(sl_), s_aq[:, tt:tt + 1], s_bq[:, tt:tt + 1],
                OP.mult, OP.add), [tk_x4ld[(qg, ti)], tk_aq, t0_free[sl_]])
        else:
            tkt0 = b.step(S, lambda e, sl_=sl_, tt=tt: e.activation(
                p4_t0(sl_), p4_x1r(sl_), AF.Identity, scale=s_aq[:, tt:tt + 1],
                bias=s_bq[:, tt:tt + 1]), [tk_x4ld[(qg, ti)], tk_aq, t0_free[sl_]])
        tkf0 = b.step(V, lambda e, sl_=sl_: e.tensor_scalar(
            p4_x1r(sl_), p4_t0(sl_), MAGIC, MAGIC, OP.add, OP.subtract), [tkt0])
        t0_free[sl_] = tkf0
        tk_f04[(qg, ti)] = tkf0

    def q4_clip(qg, ti):
        sl_ = ti % 2
        tkxq = b.step(V, lambda e, sl_=sl_, ti=ti: e.tensor_scalar(
            p4_xqn(ti % 2), p4_x1r(sl_), s_qrep[:, 0:1], s_qrep[:, 1:2],
            OP.max, OP.min), [tk_f04[(qg, ti)], tk_qrep1, xqn_free[ti % 2]])
        tk_clip4[(qg, ti)] = tkxq
        pump_x4(tkxq)

    def q4_trn_ev(qg, ti):
        tb = ti % 2
        def trn(e, ti=ti, tb=tb):
            pq = (PA, PB)[tb][0:128, 0:512].bitcast(BF16)
            ins = None
            for c in range(8):
                ins = e.transpose(pq[:, c * 128:(c + 1) * 128],
                                  p4_xqn(ti % 2)[:, c * 128:(c + 1) * 128],
                                  s_eyeb[:, :])
            return ins
        tktr = b.step(TE, trn, [tk_clip4[(qg, ti)], tk_boot[BT_EYEB],
                                tqbank_free[tb]])
        xqn_free[ti % 2] = tktr
        def ev(e, qg=qg, ti=ti, tb=tb):
            dst = p4_xqf_dst(qg, ti)
            src = (PA, PB)[tb][0:128, 0:512].bitcast(BF16).rearrange(
                "p (c w) -> p c w", c=8)
            return e.activation(dst, src, AF.Identity)
        def evv(e, qg=qg, ti=ti, tb=tb):
            dst = p4_xqf_dst(qg, ti)
            src = (PA, PB)[tb][0:128, 0:512].bitcast(BF16).rearrange(
                "p (c w) -> p c w", c=8)
            return e.tensor_copy(dst, src)
        if qg == 0 and ti % 2 == 1:
            tkev = b.step(V, evv, [tktr, last_mm1_of.get(qg - 2, xqf_prev[qg % 2])])
        else:
            tkev = b.step(S, ev, [tktr, last_mm1_of.get(qg - 2, xqf_prev[qg % 2])])
        tqbank_free[tb] = tkev
        tk_xq[(qg, ti)] = tkev

    for it in range(G + 1):
        qg = it if it < G else None
        mg = it - 1 if it >= 1 else None
        if qg == 0:
            q4_pre(0, 0)
            q4_pre(0, 1)
            q4_clip(0, 0)
            q4_pre(0, 2)
            q4_clip(0, 1)
            q4_pre(0, 3)
        if mg is None:
            if qg == 0:
                q4_trn_ev(0, 0)
                q4_clip(0, 2)
                q4_trn_ev(0, 1)
                q4_clip(0, 3)
                q4_trn_ev(0, 2)
                q4_trn_ev(0, 3)
                if G > 1:
                    q4_pre(1, 0)
                    q4_pre(1, 1)
                    q4_clip(1, 0)
                    q4_pre(1, 2)
                    q4_clip(1, 1)
                    q4_pre(1, 3)
            continue
        if G == 1:
            w2ks = list(range(32)) if mg == 0 else []
        elif mg >= 1:
            w2ks = list(range(((mg - 1) * 32) // (G - 1), (mg * 32) // (G - 1)))
        else:
            w2ks = []
        w2i = 0
        if w2ks:
            w2_start()
        for hc in range(32):
            hh = hc // 8
            bk = 4 + (hc % 4)
            def mm1(e, mg=mg, hc=hc, bk=bk):
                ins = None
                for c in range(8):
                    ins = e.matmul(bank(bk), w1qn(c, hc * 128, 128),
                                   p4_xqf2(mg, c), start=(c == 0), stop=(c == 7))
                return ins
            deps = [tk_xq[(mg, t_)] for t_ in range(4)] + \
                   [tk_w1q[(c, hh)] for c in range(8)] + [h_bank_free[bk]]
            tkm = b.step(TE, mm1, deps)
            last_mm1_of[mg] = tkm
            q = (hc // 4) % 2
            ciq = hc % 4
            tkg = b.step(S, lambda e, hc=hc, bk=bk, q=q, ciq=ciq: e.activation(
                p4_gsb(q)[:, ciq * 512:(ciq + 1) * 512], bank(bk),
                AF.Gelu, scale=s_qrep[:, 2:3], bias=s_b1[:, hc:hc + 1]),
                [tkm, tk_qrep1, tk_boot[BT_B1]] +
                ([gsb_free[q]] if ciq == 0 else []))
            h_bank_free[bk] = tkg
            tkmn = b.step(V, lambda e, hc=hc, q=q, ciq=ciq: (
                e.tensor_reduce(s_gmx[:, hc:hc + 1],
                                p4_gsb(q)[:, ciq * 512:(ciq + 1) * 512], AX.X, OP.max),
                e.tensor_reduce(s_gmn[:, hc:hc + 1],
                                p4_gsb(q)[:, ciq * 512:(ciq + 1) * 512], AX.X,
                                OP.min))[-1], [tkg])
            if ciq == 3:
                q4 = hc // 4
                tkst = b.dma((GP, SY, GP, S)[q4 % 4],
                             g_d[mg, q4 * 4:(q4 + 1) * 4].rearrange("q p c -> p q c"),
                             p4_gsb(q), [tkg], sem=d_gst[q4 % 4])
                tk_gst[(mg, q4)] = tkst
                gsb_free[q] = ("multi", tkst, tkmn)
            gmm_last = tkmn
            if G > 1 and w2i < len(w2ks) and hc % 2 == 1 and hc // 2 >= w2i:
                w2_emit(w2ks[w2i])
                w2i += 1
            if qg is not None and hc % 8 == 2:
                hcq = hc // 8
                q4_trn_ev(qg, hcq)
                if hcq < 2:
                    q4_clip(qg, hcq + 2)
            nq = mg + 2
            if nq < G:
                if hc == 13:
                    q4_pre(nq, 0)
                elif hc == 15:
                    q4_pre(nq, 1)
                elif hc == 19:
                    q4_clip(nq, 0)
                elif hc == 21:
                    q4_pre(nq, 2)
                elif hc == 27:
                    q4_clip(nq, 1)
                elif hc == 29:
                    q4_pre(nq, 3)
        while w2i < len(w2ks):
            w2_emit(w2ks[w2i])
            w2i += 1
        def fold(e):
            e.tensor_reduce(s_gtmp[:, :], s_gmx[:, :], AX.X, OP.max)
            e.drain()
            e.tensor_tensor(s_gmxr[:, :], s_gmxr[:, :], s_gtmp[:, :], OP.max)
            e.drain()
            e.tensor_reduce(s_gtmp[:, :], s_gmn[:, :], AX.X, OP.min)
            e.drain()
            return e.tensor_tensor(s_gmnr[:, :], s_gmnr[:, :], s_gtmp[:, :], OP.min)
        gmm_last = b.step(V, fold, [gmm_last, tk_memset])

    def gneg(e):
        e.tensor_single_scalar(s_gmnr[:, :], s_gmnr[:, :], -1.0, OP.mult)
        e.drain()
        e.tensor_copy(s_cand[:, 0:1], s_gmxr[:, :])
        return e.tensor_copy(s_cand[:, 1:2], s_gmnr[:, :])
    tk_gneg = b.step(V, gneg, [gmm_last])
    tk_gT = b.step(TE, lambda e: e.transpose(bank(7, 0, 2, 0, 128), s_cand[:, :],
                                             s_eyef[:, :]), [tk_gneg, tk_qrmm2, tk_qrep2])
    tk_g2 = b.step(V, lambda e: e.tensor_reduce(s_cred[:, :], bank(7, 0, 2, 0, 128),
                                                AX.X, OP.max), [tk_gT])
    tk_ar3s = b.dma(GP, ar3i.rearrange("a b -> b a"), s_cred[:, :], [tk_g2], sem=d_msc)
    tk_cc3 = b.collective("AllReduce", OP.max, ar3i, ar3o, [tk_ar3s])
    tk_ar3l = b.dma(GP, s_q2[0:1, 0:2], ar3o, [tk_cc3], sem=d_msc)

    def qs2(e):
        e.tensor_add(s_q2[0:1, 2:3], s_q2[0:1, 0:1], s_q2[0:1, 1:2])
        e.drain()
        e.tensor_scalar(s_q2[0:1, 2:3], s_q2[0:1, 2:3], 1.0 / 255.0, 1e-8,
                        OP.mult, OP.max)
        e.drain()
        e.reciprocal(s_q2[0:1, 3:4], s_q2[0:1, 2:3])
        e.drain()
        e.tensor_mul(s_q2[0:1, 4:5], s_q2[0:1, 1:2], s_q2[0:1, 3:4])
        e.drain()
        e.tensor_scalar(s_q2[0:1, 5:6], s_q2[0:1, 4:5], MAGIC, MAGIC,
                        OP.add, OP.subtract)
        e.drain()
        e.tensor_scalar(s_q2[0:1, 6:7], s_q2[0:1, 5:6], -128.0, None, OP.add)
        e.drain()
        e.tensor_scalar(s_q2[0:1, 6:7], s_q2[0:1, 6:7], -128.0, 127.0, OP.max, OP.min)
        e.drain()
        e.tensor_scalar(s_q2[0:1, 7:8], s_q2[0:1, 6:7], -1.0, -128.0, OP.mult, OP.add)
        e.tensor_scalar(s_q2[0:1, 8:9], s_q2[0:1, 6:7], -1.0, 127.0, OP.mult, OP.add)
        e.drain()
        return e.tensor_mul(s_q2[0:1, 9:10], s_q2[0:1, 2:3], s_q0[0:1, 3:4])
    tk_qs2 = b.step(V, qs2, [tk_ar3l, tk_sw])
    def qrmm3(e):
        e.matmul(bank(7, 0, 128, 0, 1), s_ones1f[0:1, :], s_q2[0:1, 3:4],
                 start=True, stop=True)
        return e.matmul(bank(7, 0, 128, 1, 3), s_ones1f[0:1, :], s_q2[0:1, 7:10],
                        start=True, stop=True)
    tk_qrmm3 = b.step(TE, qrmm3, [tk_qs2, tk_gT, tk_g2])
    tk_qrep3 = b.step(V, lambda e: (
        e.tensor_copy(s_qrep[:, 3:4], bank(7, 0, 128, 0, 1)),
        e.tensor_copy(s_qrep[:, 4:5], bank(7, 0, 128, 1, 1)),
        e.tensor_copy(s_qrep[:, 5:6], bank(7, 0, 128, 2, 1)),
        e.tensor_copy(s_qrep[:, 6:7], bank(7, 0, 128, 3, 1)))[-1], [tk_qrmm3])

    graw_free = [gsb_free[0], gsb_free[1]]
    gt0_free = [None, None]
    gf0_free = [None, None]
    mm2_bank_free = {6: None, 7: None}
    tm_bank_free = [None, None, None]
    ffn_free = [None] * 8
    out_free = [None, None]
    t25_free = [None, None]

    tk_gld = {}
    tk_gq = {}
    tk_x15 = {}
    tk_ffnev = {}
    last_mm2_of = {}
    last_trn_of = {}

    def plan_gld(gi, q):
        sl_ = q % 2
        tk_gld[(gi, q)] = b.dma(
            SY if sl_ == 0 else S, p5_graw(sl_),
            g_d[gi, q * 4:(q + 1) * 4].rearrange("q p c -> p q c"),
            [tk_gst[(gi, q)], graw_free[sl_]], sem=d_gld[sl_])

    odd_g = [g_ for g_ in range(G) if g_ % 2 == 1]
    x15dep = last_mm1_of.get(odd_g[-1]) if odd_g else None
    x1q_seq = [(g_, ti) for g_ in range(G) for ti in range(4)]
    x1p = [0]
    def pump_x1(dep):
        if x1p[0] >= len(x1q_seq):
            return
        g_, ti = x1q_seq[x1p[0]]
        x1p[0] += 1
        tt = g_ * 4 + ti
        tk_x15[(g_, ti)] = b.dma(
            SY, p5_x1r(ti), x1_d[tt * 128:(tt + 1) * 128, :],
            [x15dep, tk_x1st[tt], dep], sem=d_x1r[2 + ti % 2])

    plan_gld(0, 0)
    plan_gld(0, 1)
    pump_x1(None)
    pump_x1(None)

    tk_bar45 = b.barrier()
    for it in range(G + 1):
        qg = it if it < G else None
        mg = it - 1 if it >= 1 else None
        for q in range(8):
            if qg is not None:
                for j in range(4):
                    k = q * 4 + j
                    st = k % 2
                    sl_ = q % 2
                    koff = j * 512
                    tkt0 = b.step(S, lambda e, sl_=sl_, koff=koff, st=st:
                                  e.activation(
                                      p5_gt0(st), p5_graw(sl_)[:, koff:koff + 512],
                                      AF.Identity, scale=s_qrep[:, 3:4]),
                                  [tk_gld[(qg, q)], tk_qrep3, gt0_free[st]])
                    tkf0 = b.step(V, lambda e, st=st: e.tensor_scalar(
                        p5_gf0(st), p5_gt0(st), MAGIC, MAGIC, OP.add, OP.subtract),
                        [tkt0, gf0_free[st]])
                    gt0_free[st] = tkf0
                    tkq = b.step(V, lambda e, qg=qg, k=k, st=st: e.tensor_scalar(
                        gq2(qg, k), p5_gf0(st), s_qrep[:, 4:5], s_qrep[:, 5:6],
                        OP.max, OP.min),
                        [tkf0, tk_bar45, last_mm2_of.get(qg - 2)])
                    gf0_free[st] = tkq
                    tk_gq[(qg, k)] = tkq
                    if j == 3:
                        graw_free[sl_] = tkt0
                        gqq = qg * 8 + q + 2
                        if gqq < G * 8:
                            plan_gld(gqq // 8, gqq % 8)
            if mg is not None:
                dj = q
                bk = 6 + dj % 2
                def mm2(e, mg=mg, dj=dj, bk=bk, k=None):
                    if k is not None:
                        return e.matmul(bank(bk), w2qn(k, dj * 128, 128),
                                        gq2(mg, k), start=(k == 0), stop=(k == 31))
                    ins = None
                    for k_ in range(32):
                        ins = e.matmul(bank(bk), w2qn(k_, dj * 128, 128), gq2(mg, k_),
                                       start=(k_ == 0), stop=(k_ == 31))
                    return ins
                if mg == 0:
                    tkm = None
                    for k in range(32):
                        kdeps = [tk_gq[(mg, k)], tk_w2q[k]] + \
                                ([mm2_bank_free[bk]] if k == 0 else [])
                        tkm = b.step(TE, lambda e, mm2=mm2, k=k: mm2(e, k=k), kdeps)
                else:
                    deps = [tk_gq[(mg, k)] for k in range(32)] + \
                           [tk_w2q[k] for k in range(32)] + [mm2_bank_free[bk]]
                    tkm = b.step(TE, mm2, deps)
                last_mm2_of[mg] = tkm
                tkev = b.step(S, lambda e, dj=dj, bk=bk: e.activation(
                    p5_ffn(dj), bank(bk), AF.Identity, scale=s_qrep[:, 6:7],
                    bias=s_b2[:, dj:dj + 1]),
                    [tkm, tk_qrep3, tk_boot[BT_B2], ffn_free[dj]])
                mm2_bank_free[bk] = tkev
                tk_ffnev[dj] = tkev
        if mg is not None:
            for ti in range(4):
                tt = mg * 4 + ti
                par = ti % 2
                b3 = tt % 3
                tmb = (PA, PB, PC)[b3]
                def trnh(e, ti=ti, tmb=tmb, djs=None):
                    ins = None
                    for dj in djs:
                        ins = e.transpose(tmb[0:128, dj * 128:(dj + 1) * 128],
                                          p5_ffn(dj)[:, ti * 128:(ti + 1) * 128],
                                          s_eyef[:, :])
                    return ins
                tktrA = b.step(TE, lambda e, trnh=trnh: trnh(e, djs=range(4)),
                               [tk_ffnev[dj] for dj in range(4)] +
                               [tk_boot[BT_EYEF], tm_bank_free[b3]])
                tktr = b.step(TE, lambda e, trnh=trnh: trnh(e, djs=range(4, 8)),
                              [tk_ffnev[dj] for dj in range(4, 8)] + [tktrA])
                last_trn_of[mg] = tktr
                def bn3(e, tmb=tmb):
                    e.bn_stats(s_bn[:, 0:6], tmb[0:128, 0:512])
                    e.bn_stats(s_bn[:, 6:12], tmb[0:128, 512:1024])
                    e.drain()
                    return e.bn_aggr(s_ag2[:, :], s_bn[:, :])
                tkbn = b.step(V, bn3, [tktr])
                def st3(e, ti=ti):
                    e.activation(s_m3[:, ti:ti + 1], s_ag2[:, 0:1], AF.Identity,
                                 scale=-1.0)
                    e.activation(s_r3[:, ti:ti + 1], s_ag2[:, 1:2], AF.Ln, bias=1e-5)
                    e.drain()
                    return e.activation(s_r3[:, ti:ti + 1], s_r3[:, ti:ti + 1],
                                        AF.Exp, scale=-0.5)
                tkst3 = b.step(S, st3, [tkbn])
                tkmr3 = b.step(V, lambda e, ti=ti: e.tensor_mul(
                    s_mr3[:, ti:ti + 1], s_m3[:, ti:ti + 1], s_r3[:, ti:ti + 1]),
                    [tkst3])
                tkt2 = b.step(S, lambda e, ti=ti, par=par: e.activation(
                    p5_t2(par), p5_x1r(ti), AF.Identity, bias=s_mr3[:, ti:ti + 1]),
                    [tk_x15[(mg, ti)], tkmr3, t25_free[par]])
                pump_x1(tkt2)
                def x2f(e, ti=ti, par=par, tmb=tmb):
                    return e.scalar_tensor_tensor(
                        out_sl(ti), tmb[0:128, 0:1024], s_r3[:, ti:ti + 1],
                        p5_t2(par), OP.mult, OP.add)
                tkx2 = b.step(V, x2f, [tkt2, tkbn, tkmr3, out_free[ti % 2]])
                t25_free[par] = tkx2
                tm_bank_free[b3] = tkx2
                tkos = b.dma(SY if ti % 2 == 0 else S,
                             out_d[tt * 128:(tt + 1) * 128, :], out_sl(ti),
                             [tkx2], sem=d_outs[ti % 2])
                out_free[ti % 2] = tkos
            for dj in range(8):
                ffn_free[dj] = last_trn_of[mg]

    b.step(SY, lambda e: None, [Tok(d_outs[0], d_outs[0].n, "_", True),
                                Tok(d_outs[1], d_outs[1].n, "_", True)])

    b.emit()
    return nc


def _host_layout(inputs, T):
    x = np.ascontiguousarray(np.asarray(inputs["L0_attn_x"], dtype=np.float32))
    Wxp = np.asarray(inputs["L0_attn_Wxproj"], dtype=np.float32)
    Wdt = np.asarray(inputs["L0_attn_Wdt"], dtype=np.float32)
    bdt = np.asarray(inputs["L0_attn_bdt"], dtype=np.float32)
    Wxs = np.asarray(inputs["L0_attn_Wxs"], dtype=np.float32)
    Alog = np.asarray(inputs["L0_attn_A_log"], dtype=np.float32)
    Wout = np.asarray(inputs["L0_attn_Wout"], dtype=np.float32)
    Dp = np.asarray(inputs["L0_attn_Dp"], dtype=np.float32)
    W1 = np.asarray(inputs["L0_ffn_W1"], dtype=np.float32)
    b1 = np.asarray(inputs["L0_ffn_b1"], dtype=np.float32)
    W2 = np.asarray(inputs["L0_ffn_W2"], dtype=np.float32)
    b2 = np.asarray(inputs["L0_ffn_b2"], dtype=np.float32)

    Bsz, L, D = x.shape
    xs = x.reshape(Bsz * L, D)
    wc_t = np.ascontiguousarray(np.concatenate([Wxp, Wxs], 0).T)
    w1t = np.ascontiguousarray(W1.T)
    w2t = np.ascontiguousarray(W2.T)
    d8 = np.zeros((128, DM), np.float32)
    idx = np.arange(128)
    for cb in range(8):
        d8[idx, cb * 128 + idx] = Dp[cb * 128:(cb + 1) * 128]
    bdiagp = np.zeros((NCH, 128), np.float32)
    for c in range(NCH):
        bdiagp[c, c * NST:(c + 1) * NST] = 1.0
    r48 = np.zeros((48, 3 * 128), np.float32)
    for c in range(NCH):
        for n in range(NST):
            r48[n, c * NST + n] = 1.0
            r48[16 + n, 128 + c * NST + n] = 1.0
            r48[32 + n, 256 + c * NST + n] = 1.0
    common = dict(
        wc_t=wc_t, wdt_c=np.ascontiguousarray(Wdt.reshape(64, 1)),
        bdt_i=bdt.reshape(1, 1), alog=Alog.reshape(1, NST),
        woutT=np.ascontiguousarray(Wout.T), dp_row=Dp.reshape(1, DM), d8=d8,
        w1t=w1t, b1sb=np.ascontiguousarray(b1.reshape(32, 128).T),
        w2t=w2t, b2sb=np.ascontiguousarray(b2.reshape(8, 128).T),
        eyef=np.eye(128, dtype=np.float32),
        eyeb=np.eye(128, dtype=ml_dtypes.bfloat16),
        bdiagp=bdiagp, bdiagn=-bdiagp, r48=r48,
    )
    in_maps = []
    for i in range(NCORES):
        shard = xs[i * T:(i + 1) * T]
        hsel = np.zeros((1, 128), np.float32)
        hsel[0, NST * i:NST * (i + 1)] = 1.0
        wsh = np.ascontiguousarray(np.concatenate(
            [w1t[i * 128:(i + 1) * 128, :],
             w2t[i * 512:(i + 1) * 512, :].reshape(128, 4096)], axis=1))
        m = dict(common)
        m.update(x_fm=np.ascontiguousarray(shard.T),
                 x_tm=np.ascontiguousarray(shard), hsel=hsel, wsh=wsh)
        in_maps.append(m)
    return in_maps, (Bsz, L, D)


def kernel(**inputs):
    x = np.asarray(inputs["L0_attn_x"])
    Bsz, L, D = x.shape
    T = (Bsz * L) // NCORES
    in_maps, shape = _host_layout(inputs, T)
    nc = build_program(T)
    res = run_bass_kernel_spmd(nc, in_maps, core_ids=list(range(NCORES)))
    out = np.concatenate([np.asarray(r["out"]) for r in res.results], axis=0)
    return out.reshape(Bsz, L, D).astype(np.float32)


# version 28
# speedup vs baseline: 1.0212x; 1.0029x over previous
import numpy as np
import ml_dtypes

import concourse.bass as bass
import concourse.mybir as mybir
from concourse.bass_utils import run_bass_kernel_spmd
from contextlib import ExitStack

F32 = mybir.dt.float32
BF16 = mybir.dt.bfloat16
AF = mybir.ActivationFunctionType
OP = mybir.AluOpType
AX = mybir.AxisListType

NCORES = 8
MAGIC = 12582912.0
DM = 1024
DH = 4096
NST = 16
NCH = 8


class Sem:
    __slots__ = ("h", "n")
    def __init__(self, h):
        self.h = h
        self.n = 0


class Tok:
    __slots__ = ("sem", "val", "eng", "dma")
    def __init__(self, sem, val, eng, dma=False):
        self.sem, self.val, self.eng, self.dma = sem, val, eng, dma


def flat(*deps):
    out = []
    for d in deps:
        if d is None or isinstance(d, str):
            continue
        if isinstance(d, (tuple, list)):
            out.extend(flat(*d))
        else:
            out.append(d)
    return out


class B:
    ENGINES = ("sync", "tensor", "vector", "scalar", "gpsimd")

    def __init__(self, nc):
        self.nc = nc
        self.ops = {e: [] for e in self.ENGINES}
        self.es = ExitStack()
        self.esem = {e: Sem(self.es.enter_context(nc.semaphore(f"e_{e}")))
                     for e in self.ENGINES}
        self.bar = Sem(self.es.enter_context(nc.semaphore("bar")))
        self.cc = Sem(self.es.enter_context(nc.semaphore("cc")))
        self.seen = {e: {} for e in self.ENGINES}

    def sem(self, name):
        return Sem(self.es.enter_context(self.nc.semaphore(name)))

    def sbuf(self, name, shape, dtype=F32):
        return self.es.enter_context(self.nc.sbuf_tensor(name, list(shape), dtype))

    def psum(self, name, shape):
        return self.es.enter_context(self.nc.psum_tensor(name, list(shape), F32))

    def _waits(self, eng, deps):
        waits = {}
        drain = False
        for d in flat(*deps):
            if d.eng == eng and not d.dma:
                drain = True
                continue
            sm = d.sem
            k = id(sm)
            if self.seen[eng].get(k, -1) >= d.val:
                continue
            if k in waits:
                waits[k] = (sm, max(waits[k][1], d.val))
            else:
                waits[k] = (sm, d.val)
        for k, (sm, v) in waits.items():
            self.seen[eng][k] = max(self.seen[eng].get(k, -1), v)
        return list(waits.values()), drain

    def step(self, eng, emit, deps=(), sem=None, inc=1, dma=False):
        waits, drain = self._waits(eng, deps)
        if sem is None:
            sem = self.esem[eng]
        sem.n += inc
        tok = Tok(sem, sem.n, eng, dma)
        def f(e, waits=waits, drain=drain, emit=emit, sem=sem, inc=inc):
            if drain:
                e.drain()
            for sm, v in waits:
                e.wait_ge(sm.h, v)
            ins = emit(e)
            if ins is not None:
                ins.then_inc(sem.h, inc)
            else:
                e.sem_inc(sem.h, inc)
        self.ops[eng].append(f)
        return tok

    def dma(self, eng, out, in_, deps=(), sem=None):
        deps = flat(*deps)
        if sem is not None and sem.n > 0:
            deps.append(Tok(sem, sem.n, "_", True))
        return self.step(eng, lambda e, o=out, i=in_: e.dma_start(out=o, in_=i),
                         deps, sem=sem, inc=16, dma=True)

    def barrier(self):
        self.bar.n += len(self.ENGINES)
        tgt = self.bar.n
        for eng in self.ENGINES:
            def f(e, tgt=tgt):
                e.sem_inc(self.bar.h, 1)
                e.wait_ge(self.bar.h, tgt)
            self.ops[eng].append(f)
            self.seen[eng][id(self.bar)] = tgt
        return Tok(self.bar, tgt, "_", True)

    def collective(self, kind, op, ins_, outs_, deps=()):
        def emit(e):
            return e.collective_compute(
                kind, op, replica_groups=[list(range(NCORES))],
                ins=[ins_.opt()], outs=[outs_.opt()])
        return self.step("gpsimd", emit, deps, sem=self.cc, inc=1, dma=True)

    def emit(self):
        with self.nc.Block() as block:
            for eng in self.ENGINES:
                ops = self.ops[eng]
                getattr(block, eng)(lambda e, ops=ops: [f(e) for f in ops])


def build_program(T):
    G = T // 512
    NT = T // 128
    LC = T // NCH
    nc = bass.Bass("TRN2", target_bir_lowering=False)
    b = B(nc)

    def dram(n, s, d=F32, k=None):
        t = nc.dram_tensor(n, list(s), d, kind=k) if k else nc.dram_tensor(n, list(s), d)
        return t.ap()

    x_fm = dram("x_fm", (DM, T), F32, "ExternalInput")
    x_tm = dram("x_tm", (T, DM), F32, "ExternalInput")
    wc_t = dram("wc_t", (DM, 112), F32, "ExternalInput")
    wdt_c = dram("wdt_c", (64, 1), F32, "ExternalInput")
    bdt_i = dram("bdt_i", (1, 1), F32, "ExternalInput")
    alog = dram("alog", (1, NST), F32, "ExternalInput")
    woutT = dram("woutT", (NST, DM), F32, "ExternalInput")
    dp_row = dram("dp_row", (1, DM), F32, "ExternalInput")
    d8 = dram("d8", (128, DM), F32, "ExternalInput")
    w1t = dram("w1t", (DM, DH), F32, "ExternalInput")
    b1sb = dram("b1sb", (128, 32), F32, "ExternalInput")
    w2t = dram("w2t", (DH, DM), F32, "ExternalInput")
    b2sb = dram("b2sb", (128, 8), F32, "ExternalInput")
    wsh_d = dram("wsh", (128, 8192), F32, "ExternalInput")
    eyef_d = dram("eyef", (128, 128), F32, "ExternalInput")
    eyeb_d = dram("eyeb", (128, 128), BF16, "ExternalInput")
    bdiagp_d = dram("bdiagp", (NCH, 128), F32, "ExternalInput")
    bdiagn_d = dram("bdiagn", (NCH, 128), F32, "ExternalInput")
    r48_d = dram("r48", (48, 3 * 128), F32, "ExternalInput")
    hsel_d = dram("hsel", (1, 128), F32, "ExternalInput")
    out_d = dram("out", (T, DM), F32, "ExternalOutput")

    rs48 = dram("rs48", (48, T))
    rowA = dram("rowA", (1, T))
    rowB = dram("rowB", (1, T))
    rowC = dram("rowC", (1, T))
    rowD = dram("rowD", (1, T))
    hcb = dram("hcb", (128, LC), BF16)
    h0b = dram("h0b", (1, NST))
    arow_b = dram("arow_b", (1, NST))
    x1_d = dram("x1_d", (T, DM))
    g_d = dram("g_d", (G, 32, 128, 512))
    ar0i = dram("ar0i", (1, 2)); ar0o = dram("ar0o", (1, 2))
    ag1i = dram("ag1i", (2, NST)); ag1o = dram("ag1o", (2 * NCORES, NST))
    ar2i = dram("ar2i", (1, 2)); ar2o = dram("ar2o", (1, 2))
    ar3i = dram("ar3i", (1, 2)); ar3o = dram("ar3o", (1, 2))

    sb = b.sbuf
    s_eyef = sb("s_eyef", (128, 128))
    s_eyeb = sb("s_eyeb", (128, 128), BF16)
    s_wdt = sb("s_wdt", (64, 1))
    s_bdt = sb("s_bdt", (1, 1))
    s_bdt8 = sb("s_bdt8", (NCH, 1))
    s_alog = sb("s_alog", (1, NST))
    s_apos = sb("s_apos", (1, NST))
    s_ones1f = sb("s_ones1f", (1, 128))
    s_hsel = sb("s_hsel", (1, 128))
    CS2 = sb("CS2", (NCH, 3 * 128))
    s_atil = CS2[:, 0:128]
    s_bdiagn = CS2[:, 128:256]
    s_bdA = CS2[:, 256:384]
    s_bdiagp = sb("s_bdiagp", (NCH, 128))
    s_r48 = sb("s_r48", (112, 3 * 128))
    s_r48B = s_r48[64:112, 0:128]
    s_r48C = s_r48[64:112, 128:256]
    s_r48X = s_r48[64:112, 256:384]
    CSB = sb("CSB", (128, DM), BF16)
    s_woutb = CSB[0:NST, :]
    s_dpb = CSB[32:33, :]
    s_b1 = sb("s_b1", (128, 32))
    s_b2 = sb("s_b2", (128, 8))
    s_onesb = sb("s_onesb", (128, 1), BF16)

    AR0 = sb("AR0", (128, max(8 * T, 32 * DM)), BF16)
    AR1 = sb("AR1", (128, max(8 * T, 8 * DH)), BF16)
    AR3 = sb("AR3", (128, 14336), F32)
    BFX = sb("BFX", (128, 6144), BF16)
    AR1f = AR1[:, :].bitcast(F32)

    def a0(c0, w, p0=0, p1=128):
        return AR0[p0:p1, c0:c0 + w]
    def a1(c0, w, p0=0, p1=128):
        return AR1[p0:p1, c0:c0 + w]
    def a1f(c0, w, p0=0, p1=128):
        return AR1f[p0:p1, c0:c0 + w]
    def a3(c0, w, p0=0, p1=128):
        return AR3[p0:p1, c0:c0 + w]
    def bfx(c0, w, p0=0, p1=128):
        return BFX[p0:p1, c0:c0 + w]

    xf = lambda c, g0=0, w=None: a0(c * T + g0, w if w is not None else T)
    w2qn = lambda k, d0=0, w=DM: a0(k * DM + d0, w)
    xsq = lambda c, g0=0, w=None: a1(((c + 4) % 8) * T + g0,
                                     w if w is not None else T)
    w1qn = lambda c, h0=0, w=DH: a1(c * DH + h0, w)
    out_sl = lambda i: a1f(8192 + (i % 2) * DM, DM)

    def gq2(g_, k):
        if g_ % 2 == 0:
            return a1(k * 512, 512)
        if k < 24:
            return a1(20480 + k * 512, 512)
        return bfx((k - 24) * 512, 512)
    wsh_s = a1f(0, 8192)
    p3_xt = lambda i: a1f(i * DM, DM)
    p3_x1 = lambda i: a1f(2048 + i * DM, DM)
    p3_attn = lambda i: a1f(4096 + i * DM, DM)
    p3_t2 = lambda i: a1f(8192 + i * DM, DM)
    p3_trash = a1f(10240, DM)

    rp = lambda p0, p1, c0=0, w=None: a3(c0, w if w is not None else T, p0, p1)
    SL2 = max(T, 8 * LC)
    def sl(i, p0=0, p1=128):
        if i == 10:
            return a3(13584, LC, p0, p1)
        if i < 8:
            return a3(i * LC, LC, p0, p1)
        return a3(SL2 + (i - 8) * LC, LC, p0, p1)
    RW = SL2 + 3 * LC
    r_m8 = a3(RW + 0 * LC, LC, 0, NCH)
    r_r8 = a3(RW + 1 * LC, LC, 0, NCH)
    r_sq8 = a3(RW + 2 * LC, LC, 0, NCH)
    r_dt8 = r_sq8
    r_dtraw = a3(RW + 3 * LC, LC, 0, NCH)
    r_tmp8 = a3(RW + 4 * LC, LC, 0, NCH)
    SM = RW + 5 * LC
    rrow = a3(SM, T, 0, 1)
    mrow_neg = a3(SM, T, 32, 33)
    msum_row = mrow_neg
    sqsum_row = a3(SM, T, 64, 65)
    dtraw_row = a3(SM, T, 96, 97)
    MS1 = SM + T
    pfinrow = a3(MS1, 128, 0, 1)
    bfinrow = a3(MS1 + 128, 128, 0, 1)
    pincrow = a3(MS1 + 256, 128, 0, 1)
    bincrow = a3(MS1 + 384, 128, 0, 1)
    aggrow = a3(MS1 + 512, 256, 0, 1)
    carrrow = a3(MS1 + 768, 128, 0, 1)
    runrow = a3(MS1 + 896, NST, 0, 1)
    runtrow = a3(MS1 + 912, NST, 0, 1)
    h0row = a3(MS1 + 928, NST, 0, 1)
    h0til = a3(MS1 + 944, 112, 0, 1)
    tmp112 = a3(MS1 + 1056, 112, 0, 1)
    hinrow = a3(MS1 + 1168, 128, 0, 1)
    assert MS1 + 1296 <= 14336, MS1 + 1296

    p4_gsb = lambda q: a3(q * 2048, 2048)
    p4_w1raw = lambda i: a3(4096 + i * DM, DM) if i < 2 else a3(1024 + (i - 2) * DM, DM)
    p4_w1t0 = a3(6144, DM)
    p4_w2raw = lambda i: a3(7168, DM)
    p4_w2t0 = p4_w1t0
    p4_x1r = lambda i: a3(8192 + i * DM, DM)
    p4_t0 = lambda i: a3(10240 + i * DM, DM)
    p5_graw = lambda i: a3(0 + i * 2048, 2048)
    p5_ffn = lambda dj: a3(4096 + dj * 512, 512)
    p5_t2 = lambda i: a3(8192 + i * DM, DM)
    p5_gt0 = lambda i: a3(10240 + i * 512, 512)
    p5_gf0 = lambda i: a3(11264 + i * 512, 512)
    p5_x1r = lambda i: a3(12288 + (i % 2) * DM, DM)

    hcf = lambda t0=0, w=None: bfx(t0, w if w is not None else T, 0, NST)
    s_mrowb = bfx(0, T, 32, 33)
    s_wcb = bfx(0, 8 * 112)
    s_d8b = bfx(4096, DM)
    p4_xqn = lambda i: bfx(i * DM, DM)
    AR3B = AR3[:, :].bitcast(BF16)
    def p4_xqf2(g_, c):
        if g_ % 2 == 0:
            return bfx(2048 + c * 512, 512)
        return AR3B[:, 24576 + c * 512:24576 + (c + 1) * 512]
    def p4_xqf_dst(g_, ti):
        base = BFX[:, 2048:6144] if g_ % 2 == 0 else AR3B[:, 24576:28672]
        return base.rearrange("p (c w) -> p c w", c=8)[:, :, ti * 128:(ti + 1) * 128]

    s_rt = sb("s_rt", (128, NT))
    s_aq = sb("s_aq", (128, NT)); s_bq = sb("s_bq", (128, NT))
    _p3s = lambda i: a3(i * 64, NT)
    s_suma = _p3s(0); s_sqa = _p3s(1)
    s_m4 = _p3s(2); s_r4 = _p3s(3); s_mr4 = _p3s(4)
    s_sum2 = _p3s(5); s_sq2 = _p3s(6)
    s_m2 = _p3s(7); s_r2 = _p3s(8)
    s_minT = _p3s(9); s_maxT = _p3s(10)
    s_tt1 = _p3s(11); s_tt2 = _p3s(12)
    s_cand = sb("s_cand", (128, 2)); s_cred = sb("s_cred", (2, 1))
    s_gmx = sb("s_gmx", (128, 32)); s_gmn = sb("s_gmn", (128, 32))
    s_gmxr = sb("s_gmxr", (128, 1)); s_gmnr = sb("s_gmnr", (128, 1))
    s_gtmp = sb("s_gtmp", (128, 1))
    s_wsc = sb("s_wsc", (112, 1))
    s_c08 = sb("s_c08", (NCH, 1))
    s_wsB = sb("s_wsB", (128, 1)); s_wsC = sb("s_wsC", (128, 1)); s_wsX = sb("s_wsX", (128, 1))
    fincol = sb("fincol", (128, 2))
    hincol = sb("hincol", (128, 1))
    s_q0 = sb("s_q0", (1, 16))
    s_q2 = sb("s_q2", (1, 16))
    s_qrep = sb("s_qrep", (128, 8))
    s_wq = sb("s_wq", (128, 2))
    s_bn = sb("s_bn", (128, 12)); s_ag2 = sb("s_ag2", (128, 2))
    s_kc = sb("s_kc", (128, 8))
    KCONST = [0.0, 1.0, 0.5, 1e-5, -0.5, -1.0, -1.0 / DM, 1.0 / DM]
    for _i, _v in enumerate(KCONST):
        nc.const_aps.register(F32, _v, s_kc[:, _i:_i + 1])
    s_m3 = sb("s_m3", (128, 4)); s_r3 = sb("s_r3", (128, 4)); s_mr3 = sb("s_mr3", (128, 4))

    PA = b.psum("PA", (128, 1024)); PB = b.psum("PB", (128, 1024))
    PC = b.psum("PC", (128, 1024)); PD = b.psum("PD", (128, 1024))
    def bank(i, p0=0, p1=128, c0=0, w=512):
        t_ = (PA, PB, PC, PD)[i // 2]
        return t_[p0:p1, (i % 2) * 512 + c0:(i % 2) * 512 + c0 + w]

    d_boot = b.sem("d_boot"); d_bootg = b.sem("d_bootg")
    d_h1 = b.sem("d_h1"); d_h2 = b.sem("d_h2")
    d_xf = b.sem("d_xf"); d_msc = b.sem("d_msc"); d_mscs = b.sem("d_mscs")
    d_xt = [b.sem(f"d_xt{i}") for i in range(2)]
    d_x1s = b.sem("d_x1s")
    d_x1r = [b.sem(f"d_x1r{i}") for i in range(4)]
    d_w1 = [b.sem(f"d_w1{i}") for i in range(4)]
    d_w2 = [b.sem("d_w20"), b.sem("d_w21")]
    d_gst = [b.sem(f"d_gst{i}") for i in range(4)]
    d_gld = [b.sem(f"d_gld{i}") for i in range(2)]
    d_outs = [b.sem("d_outs0"), b.sem("d_outs1")]

    V, S, TE, GP, SY = "vector", "scalar", "tensor", "gpsimd", "sync"

    tk_boot = []
    boot_specs = [
        (GP, s_eyef[:, :], eyef_d), (GP, s_eyeb[:, :], eyeb_d),
        (GP, s_wdt[:, :], wdt_c), (GP, s_bdt[:, :], bdt_i),
        (GP, s_alog[:, :], alog), (GP, s_b1[:, :], b1sb), (GP, s_b2[:, :], b2sb),
        (GP, s_bdiagp[:, :], bdiagp_d), (GP, s_bdiagn[:, :], bdiagn_d),
        (GP, s_r48[64:112, :], r48_d), (GP, s_hsel[:, :], hsel_d),
        ("wsh", None, None),
        (GP, s_wcb[:, :].rearrange("p (c w) -> p c w", c=8),
         wc_t.rearrange("(c p) w -> p c w", c=8)),
        (GP, s_woutb[:, :], woutT), (GP, s_dpb[:, :], dp_row), (GP, s_d8b[:, :], d8),
    ]
    bsems = [d_boot, d_bootg, b.sem("d_boot2")]
    d_wsh = [b.sem("d_wsh0"), b.sem("d_wsh1")]
    tk_wsh = ("multi",
              b.dma(SY, wsh_s[:, 0:4096], wsh_d[:, 0:4096], sem=d_wsh[0]),
              b.dma(S, wsh_s[:, 4096:8192], wsh_d[:, 4096:8192], sem=d_wsh[1]))
    gpi = [0]
    for qe, dst, src in boot_specs:
        if qe == "wsh":
            tk_boot.append(tk_wsh)
        else:
            tk_boot.append(b.dma(GP, dst, src, sem=bsems[gpi[0] % 3]))
            gpi[0] += 1
    (BT_EYEF, BT_EYEB, BT_WDT, BT_BDT, BT_ALOG, BT_B1, BT_B2, BT_BDP,
     BT_BDN, BT_R48, BT_HSEL, BT_WSH, BT_WCB, BT_WOUT, BT_DPB,
     BT_D8B) = range(16)
    NXS = 3 if 3 * T <= 12288 else 2
    xstage = lambda c: a3((c % NXS) * T, T)
    d_xs = [b.sem(f"d_xs{i}") for i in range(NXS)]
    tk_xfld = {}
    xfp = [0]
    def pump_xf(dep):
        if xfp[0] >= 8:
            return
        c = xfp[0]
        xfp[0] += 1
        tk_xfld[c] = b.dma(SY if c % 2 == 0 else S, xstage(c),
                           x_fm[c * 128:(c + 1) * 128, :], [dep],
                           sem=d_xs[c % NXS])
    for _ in range(NXS):
        pump_xf(None)

    def memsets(e):
        e.memset(s_onesb[:, :], 1.0)
        e.memset(s_ones1f[:, :], 1.0)
        e.memset(sl(10), 0.0)
        e.memset(s_gmxr[:, :], -3e38)
        e.memset(s_gmnr[:, :], 3e38)
        ins = None
        for _i, _v in enumerate(KCONST):
            ins = e.memset(s_kc[:, _i:_i + 1], _v)
        return ins
    tk_memset = b.step(V, memsets)
    b.step(S, lambda e: None, [tk_memset])

    tk_sq = []
    tk_xf = []
    def cast_sq(c):
        tkc = b.step(V, lambda e, c=c: e.tensor_copy(xf(c), xstage(c)),
                     [tk_xfld[c]])
        tks = b.step(S, lambda e, c=c: e.activation(
            xsq(c), xstage(c), AF.Square),
            [tk_xfld[c]] + ([tk_wred] if ((c + 4) % 8) * T < 16384 else []))
        tk_xf.append(tkc)
        tk_sq.append(tks)
        pump_xf(("multi", tkc, tks))
    def wred(e):
        e.tensor_reduce(s_cand[:, 0:1], wsh_s[:, 0:4096], AX.X, OP.max,
                        apply_absolute_value=True)
        return e.tensor_reduce(s_cand[:, 1:2], wsh_s[:, 4096:8192], AX.X, OP.max,
                               apply_absolute_value=True)
    if T >= 4096:
        for c in range(4):
            cast_sq(c)
        tk_wred = b.step(V, wred, [tk_boot[BT_WSH]])
        for c in range(4, 8):
            cast_sq(c)
    else:
        tk_wred = b.step(V, wred, [tk_boot[BT_WSH]])
        for c in range(8):
            cast_sq(c)
    tk_stgdead = ("multi", tk_sq[6], tk_sq[7], tk_xf[6], tk_xf[7])

    tk_wredT = b.step(TE, lambda e: e.transpose(bank(6, 0, 2, 0, 128), s_cand[:, :],
                                                s_eyef[:, :]), [tk_wred, tk_boot[BT_EYEF]])
    tk_wcred = b.step(V, lambda e: e.tensor_reduce(s_cred[:, :], bank(6, 0, 2, 0, 128),
                                                   AX.X, OP.max), [tk_wredT])
    tk_ar0s = b.dma(GP, ar0i.rearrange("a b -> b a"), s_cred[:, :], [tk_wcred], sem=d_msc)
    tk_cc0 = b.collective("AllReduce", OP.max, ar0i, ar0o, [tk_ar0s])
    tk_ar0l = b.dma(GP, s_q0[0:1, 0:2], ar0o, [tk_cc0], sem=d_msc)
    tk_woutneg = b.step(V, lambda e: e.tensor_single_scalar(
        s_woutb[:, :], s_woutb[:, :], -1.0, OP.mult), [tk_boot[BT_WOUT]])
    tk_apos = b.step(S, lambda e: e.activation(s_apos[:, :], s_alog[:, :], AF.Exp),
                     [tk_boot[BT_ALOG]])
    tk_ab = b.dma(GP, arow_b, s_apos[:, :], [tk_apos], sem=d_msc)
    tk_atil = b.dma(GP, s_atil[:, :].rearrange("c (r t) -> c r t", r=8),
                    arow_b.broadcast_to((NCH * 8, NST)).rearrange("(c r) t -> c r t", c=NCH),
                    [tk_ab], sem=d_msc)
    tk_bdA = b.step(V, lambda e: e.tensor_mul(s_bdA[:, :], s_atil[:, :], s_bdiagn[:, :]),
                    [tk_atil, tk_boot[BT_BDN]])
    tk_bdt8mm = b.step(TE, lambda e: e.matmul(bank(4, 0, NCH, 0, 1), s_ones1f[0:1, 0:NCH],
                                              s_bdt[:, :], start=True, stop=True),
                       [tk_boot[BT_BDT], tk_memset])
    tk_bdt8 = b.step(V, lambda e: e.tensor_copy(s_bdt8[:, :], bank(4, 0, NCH, 0, 1)),
                     [tk_bdt8mm])

    tk_mevac = []
    prev = {0: None, 1: None, 2: None, 3: None}
    for gi in range(G):
        bk = gi % 4
        def mm(e, gi=gi, bk=bk):
            ins = None
            for c in range(8):
                ins = e.matmul(bank(bk, 0, 1), s_onesb[:, :],
                               xf(c, gi * 512, 512), start=(c == 0), stop=(c == 7))
            return ins
        tkm = b.step(TE, mm, [tk_xf[7], tk_memset, prev[bk]])
        tkev = b.step(V, lambda e, gi=gi, bk=bk: e.tensor_copy(
            msum_row[0:1, gi * 512:(gi + 1) * 512], bank(bk, 0, 1)), [tkm])
        prev[bk] = tkev
        tk_mevac.append(tkev)
    tk_sqevac = []
    for gi in range(G):
        bk = gi % 4
        def mm(e, gi=gi, bk=bk):
            ins = None
            for c in range(8):
                ins = e.matmul(bank(bk, 0, 1), s_onesb[:, :],
                               xsq(c, gi * 512, 512), start=(c == 0), stop=(c == 7))
            return ins
        tkm = b.step(TE, mm, [tk_sq[7], prev[bk]])
        tkev = b.step(V, lambda e, gi=gi, bk=bk: e.tensor_copy(
            sqsum_row[0:1, gi * 512:(gi + 1) * 512], bank(bk, 0, 1)), [tkm])
        prev[bk] = tkev
        tk_sqevac.append(tkev)
    sq_done = tk_sqevac[-1]

    tk_rpev = []
    prevrp = {4: None, 5: None, 6: tk_wcred, 7: None}
    for gi in range(G):
        bk = 4 + gi % 4
        def mm(e, gi=gi, bk=bk):
            ins = None
            for c in range(8):
                ins = e.matmul(bank(bk, 0, 112), s_wcb[:, c * 112:(c + 1) * 112],
                               xf(c, gi * 512, 512), start=(c == 0), stop=(c == 7))
            return ins
        tkm = b.step(TE, mm, [tk_xf[7], tk_sq[6], tk_boot[BT_WCB], prevrp[bk]])
        tkev = b.step(V, lambda e, gi=gi, bk=bk: e.tensor_copy(
            rp(0, 112, gi * 512, 512), bank(bk, 0, 112)), [tkm])
        prevrp[bk] = tkev
        tk_rpev.append(tkev)

    def mm_wsum(e):
        ins = None
        for c in range(8):
            ins = e.matmul(bank(4, 0, 112, 0, 1), s_wcb[:, c * 112:(c + 1) * 112],
                           s_onesb[:, :], start=(c == 0), stop=(c == 7))
        return ins
    tk_wsmm = b.step(TE, mm_wsum, [tk_boot[BT_WCB], tk_memset, tk_bdt8,
                                   tk_rpev[-2] if G >= 2 else tk_rpev[-1]])
    tk_wsev = b.step(V, lambda e: e.tensor_copy(s_wsc[:, :], bank(4, 0, 112, 0, 1)),
                     [tk_wsmm])
    def mm_ws2(e):
        e.matmul(bank(4, 0, 1, 0, 1), s_wdt[:, :], s_wsc[0:64, 0:1], start=True, stop=True)
        e.matmul(bank(5, 0, 128, 0, 1), s_r48B[:, :], s_wsc[64:112, 0:1],
                 start=True, stop=True)
        e.matmul(bank(5, 0, 128, 1, 1), s_r48C[:, :], s_wsc[64:112, 0:1],
                 start=True, stop=True)
        return e.matmul(bank(5, 0, 128, 2, 1), s_r48X[:, :], s_wsc[64:112, 0:1],
                        start=True, stop=True)
    tk_ws2 = b.step(TE, mm_ws2, [tk_wsev, tk_boot[BT_WDT], tk_boot[BT_R48],
                                 tk_rpev[-1]])
    tk_wsv = b.step(V, lambda e: (
        e.tensor_copy(s_wsB[:, :], bank(5, 0, 128, 0, 1)),
        e.tensor_copy(s_wsC[:, :], bank(5, 0, 128, 1, 1)),
        e.tensor_copy(s_wsX[:, :], bank(5, 0, 128, 2, 1)))[-1], [tk_ws2])
    tk_wsneg = b.step(V, lambda e: (
        e.tensor_single_scalar(s_wsB[:, :], s_wsB[:, :], -1.0, OP.mult),
        e.tensor_single_scalar(s_wsC[:, :], s_wsC[:, :], -1.0, OP.mult),
        e.tensor_single_scalar(s_wsX[:, :], s_wsX[:, :], -1.0, OP.mult))[-1], [tk_wsv])
    tk_c0ev = b.step(V, lambda e: e.tensor_copy(s_q0[0:1, 6:7], bank(4, 0, 1, 0, 1)),
                     [tk_ws2])
    tk_c0mm = b.step(TE, lambda e: e.matmul(bank(4, 0, NCH, 4, 1), s_ones1f[0:1, 0:NCH],
                                            s_q0[0:1, 6:7], start=True, stop=True),
                     [tk_c0ev])
    tk_c08 = b.step(V, lambda e: e.tensor_copy(s_c08[:, :], bank(4, 0, NCH, 4, 1)),
                    [tk_c0mm])

    tk_dtev = []
    prevdt = {0: prev[0], 1: prev[1]}
    for gi in range(G):
        bk = gi % 2
        tkm = b.step(TE, lambda e, gi=gi, bk=bk: e.matmul(
            bank(bk, 0, 1), s_wdt[:, :], rp(0, 64, gi * 512, 512),
            start=True, stop=True),
            [tk_rpev[gi], tk_boot[BT_WDT], prevdt[bk]])
        tkev = b.step(V, lambda e, gi=gi, bk=bk: e.tensor_copy(
            dtraw_row[0:1, gi * 512:(gi + 1) * 512], bank(bk, 0, 1)), [tkm])
        prevdt[bk] = tkev
        tk_dtev.append(tkev)

    def swcalc(e):
        e.tensor_scalar(s_q0[0:1, 2:4], s_q0[0:1, 0:2], 1.0 / 127.0, 1e-8, OP.mult, OP.max)
        e.drain()
        return e.reciprocal(s_q0[0:1, 4:6], s_q0[0:1, 2:4])
    tk_sw = b.step(V, swcalc, [tk_ar0l])
    tk_swmm = b.step(TE, lambda e: e.matmul(bank(6, 0, 128, 0, 2), s_ones1f[0:1, :],
                                            s_q0[0:1, 4:6], start=True, stop=True),
                     [tk_sw, tk_dtev[-1]])
    tk_wq = b.step(V, lambda e: e.tensor_copy(s_wq[:, :], bank(6, 0, 128, 0, 2)), [tk_swmm])

    tk_br1 = b.dma(SY, rowA, msum_row, [tk_mevac[-1]], sem=d_h1)
    tk_br2 = b.dma(SY, r_m8, rowA.rearrange("o (c t) -> (o c) t", c=8),
                   [tk_br1, tk_stgdead], sem=d_h1)
    tk_br3 = b.dma(S, rowB, sqsum_row, [tk_sqevac[-1]], sem=d_h2)
    tk_br4 = b.dma(S, r_sq8, rowB.rearrange("o (c t) -> (o c) t", c=8),
                   [tk_br3, tk_stgdead], sem=d_h2)
    tk_br5 = b.dma(SY, rowC, dtraw_row, [tk_dtev[-1]], sem=d_h1)
    tk_br6 = b.dma(SY, r_dtraw, rowC.rearrange("o (c t) -> (o c) t", c=8),
                   [tk_br5, tk_stgdead], sem=d_h1)

    tk_m8 = b.step(S, lambda e: e.activation(r_m8, r_m8, AF.Identity, scale=-1.0 / DM),
                   [tk_br2])
    tk_e2 = b.step(S, lambda e: e.activation(r_sq8, r_sq8, AF.Identity, scale=1.0 / DM),
                   [tk_br4])
    tk_msq = b.step(V, lambda e: e.tensor_mul(r_tmp8, r_m8, r_m8), [tk_m8])
    tk_var = b.step(V, lambda e: e.tensor_sub(r_sq8, r_sq8, r_tmp8), [tk_e2, tk_msq])
    tk_lnv = b.step(S, lambda e: e.activation(r_tmp8, r_sq8, AF.Ln, bias=1e-5), [tk_var])
    tk_r8 = b.step(S, lambda e: e.activation(r_r8, r_tmp8, AF.Exp, scale=-0.5), [tk_lnv])

    tk_dt1 = b.step(V, lambda e: e.scalar_tensor_tensor(
        r_tmp8, r_m8, s_c08[:, :], r_dtraw, OP.mult, OP.add),
        [tk_m8, tk_br6, tk_c08, tk_r8])
    tk_dt2 = b.step(V, lambda e: e.tensor_mul(r_tmp8, r_tmp8, r_r8), [tk_dt1, tk_r8])
    tk_dt3 = b.step(S, lambda e: e.activation(r_dt8, r_tmp8, AF.Exp, bias=s_bdt8[:, :]),
                    [tk_dt2, tk_bdt8])
    tk_dt4 = b.step(S, lambda e: e.activation(r_dt8, r_dt8, AF.Ln, bias=1.0), [tk_dt3])

    tk_bx1 = b.dma(S, rs48, rp(64, 112), [tk_rpev[-1]], sem=d_h2)
    tk_Br = b.dma(SY, sl(4), rs48[0:16, :].rearrange("n (c t) -> c n t", c=8),
                  [tk_bx1, tk_dtev[-1], tk_stgdead], sem=d_h1)
    tk_Cr = b.dma(S, sl(5), rs48[16:32, :].rearrange("n (c t) -> c n t", c=8),
                  [tk_bx1, tk_dtev[-1], tk_stgdead], sem=d_h2)
    tk_Xr = b.dma(SY, sl(6), rs48[32:48, :].rearrange("n (c t) -> c n t", c=8),
                  [tk_bx1, tk_dtev[-1], tk_stgdead], sem=d_h1)

    tk_op1 = b.step(TE, lambda e: e.matmul(bank(5, 0, 128, 0, LC), s_bdA[:, :], r_dt8,
                                           start=True, stop=True),
                    [tk_dt4, tk_bdA, tk_wsv])
    tk_Abar = b.step(S, lambda e: e.activation(sl(0), bank(5, 0, 128, 0, LC), AF.Exp),
                     [tk_op1, tk_bx1, tk_dtev[-1]])
    tk_op2 = b.step(TE, lambda e: e.matmul(bank(6, 0, 128, 0, LC), s_bdiagp[:, :], r_dt8,
                                           start=True, stop=True),
                    [tk_dt4, tk_boot[BT_BDP], tk_dtev[-1]])
    tk_dts = b.step(V, lambda e: e.tensor_copy(sl(1), bank(6, 0, 128, 0, LC)),
                    [tk_op2, tk_bx1])
    tk_op3 = b.step(TE, lambda e: e.matmul(bank(0, 0, 128, 0, LC), s_bdiagp[:, :], r_m8,
                                           start=True, stop=True),
                    [tk_m8, tk_op1, tk_dtev[-1]])
    tk_ms = b.step(V, lambda e: e.tensor_copy(sl(2), bank(0, 0, 128, 0, LC)),
                   [tk_op3, tk_bx1])
    tk_op4 = b.step(TE, lambda e: e.matmul(bank(1, 0, 128, 0, LC), s_bdiagp[:, :], r_r8,
                                           start=True, stop=True),
                    [tk_r8, tk_op2, tk_dtev[-1]])
    tk_rs = b.step(V, lambda e: e.tensor_copy(sl(3), bank(1, 0, 128, 0, LC)),
                   [tk_op4, tk_bx1])
    def dtr2(e):
        e.tensor_mul(sl(1), sl(1), sl(3))
        e.drain()
        return e.tensor_mul(sl(1), sl(1), sl(3))
    tk_dtr2s = b.step(V, dtr2, [tk_dts, tk_rs])

    tk_Bc = b.step(V, lambda e: e.scalar_tensor_tensor(
        sl(4), sl(2), s_wsB[:, :], sl(4), OP.mult, OP.subtract),
        [tk_ms, tk_Br, tk_wsneg])
    tk_Xc = b.step(V, lambda e: e.scalar_tensor_tensor(
        sl(6), sl(2), s_wsX[:, :], sl(6), OP.mult, OP.subtract),
        [tk_ms, tk_Xr, tk_wsneg])
    tk_Bx1 = b.step(V, lambda e: e.tensor_mul(sl(7), sl(4), sl(6)), [tk_Bc, tk_Xc])
    tk_Bx = b.step(V, lambda e: e.tensor_mul(sl(7), sl(7), sl(1)), [tk_Bx1, tk_dtr2s])

    tk_hloc = b.step(V, lambda e: e.tensor_tensor_scan(
        sl(8), sl(0), sl(7), 0.0, OP.mult, OP.add), [tk_Abar, tk_Bx])
    tk_cumP = b.step(V, lambda e: e.tensor_tensor_scan(
        sl(9), sl(0), sl(10), 1.0, OP.mult, OP.add), [tk_hloc, tk_memset])

    tk_fin = b.step(V, lambda e: (
        e.tensor_copy(fincol[:, 0:1], sl(9)[:, LC - 1:LC]),
        e.tensor_copy(fincol[:, 1:2], sl(8)[:, LC - 1:LC]))[-1], [tk_cumP])
    def finT(e):
        e.transpose(bank(7, 0, 1, 0, 128), fincol[:, 0:1], s_eyef[:, :])
        return e.transpose(bank(7, 0, 1, 128, 128), fincol[:, 1:2], s_eyef[:, :])
    tk_finT = b.step(TE, finT, [tk_fin])
    tk_finr = b.step(V, lambda e: (
        e.tensor_copy(pfinrow[:, :], bank(7, 0, 1, 0, 128)),
        e.tensor_copy(bfinrow[:, :], bank(7, 0, 1, 128, 128)))[-1], [tk_finT])

    def sumscan(e):
        Prow = pfinrow[0:1, :].rearrange("o (c n) -> o c n", c=8)
        Brow = bfinrow[0:1, :].rearrange("o (c n) -> o c n", c=8)
        pv = pincrow[0:1, :].rearrange("o (c n) -> o c n", c=8)
        bv = bincrow[0:1, :].rearrange("o (c n) -> o c n", c=8)
        ins = None
        for n in range(NST):
            e.tensor_tensor_scan(pv[:, :, n], Prow[:, :, n], sl(10)[0:1, 0:NCH],
                                 1.0, OP.mult, OP.add)
        for n in range(NST):
            ins = e.tensor_tensor_scan(bv[:, :, n], Prow[:, :, n], Brow[:, :, n],
                                       0.0, OP.mult, OP.add)
        return ins
    tk_sscan = b.step(V, sumscan, [tk_finr, tk_memset])

    tk_ag_s1 = b.dma(SY, ag1i[0:1, :], pincrow[0:1, 112:128], [tk_sscan], sem=d_h1)
    tk_ag_s2 = b.dma(S, ag1i[1:2, :], bincrow[0:1, 112:128], [tk_sscan], sem=d_h2)
    tk_cc1 = b.collective("AllGather", OP.bypass, ag1i, ag1o, [tk_ag_s1, tk_ag_s2])
    tk_agl = b.dma(SY, aggrow, ag1o.rearrange("(o a) b -> o (a b)", o=1), [tk_cc1],
                   sem=d_h1)

    def combine(e):
        e.memset(runrow[:, :], 0.0)
        for j in range(NCORES):
            if j == 4:
                e.drain()
                e.memset(runrow[:, :], 0.0)
            e.drain()
            e.tensor_copy(carrrow[0:1, NST * j:NST * (j + 1)], runrow[:, :])
            Pj = aggrow[0:1, 32 * j:32 * j + 16]
            Bj = aggrow[0:1, 32 * j + 16:32 * j + 32]
            e.drain()
            e.tensor_mul(runtrow[:, :], Pj, runrow[:, :])
            e.drain()
            e.tensor_add(runrow[:, :], runtrow[:, :], Bj)
        e.drain()
        e.tensor_mul(carrrow[:, :], carrrow[:, :], s_hsel[:, :])
        e.drain()
        e.tensor_add(carrrow[0:1, 0:64], carrrow[0:1, 0:64], carrrow[0:1, 64:128])
        e.drain()
        e.tensor_add(carrrow[0:1, 0:32], carrrow[0:1, 0:32], carrrow[0:1, 32:64])
        e.drain()
        return e.tensor_add(h0row[:, :], carrrow[0:1, 0:16], carrrow[0:1, 16:32])
    tk_h0 = b.step(V, combine, [tk_agl, tk_boot[BT_HSEL]])

    tk_h0s = b.dma(SY, h0b, h0row, [tk_h0], sem=d_h1)
    tk_h0t = b.dma(SY, h0til.rearrange("o (r t) -> o r t", r=7),
                   h0b.broadcast_to((7, NST)).rearrange("(o r) t -> o r t", o=1),
                   [tk_h0s], sem=d_h1)
    def hin(e):
        e.tensor_copy(hinrow[0:1, 0:16], h0row[:, :])
        e.tensor_mul(tmp112[:, :], pincrow[0:1, 0:112], h0til[:, :])
        e.drain()
        return e.tensor_add(hinrow[0:1, 16:128], tmp112[:, :], bincrow[0:1, 0:112])
    tk_hinr = b.step(V, hin, [tk_h0t, tk_h0, tk_sscan])
    tk_hinT = b.step(TE, lambda e: e.transpose(bank(7, 0, 128, 0, 1), hinrow[:, :],
                                               s_eyef[0:1, 0:1]),
                     [tk_hinr, tk_finT, tk_finr])
    tk_hin = b.step(V, lambda e: e.tensor_copy(hincol[:, :], bank(7, 0, 128, 0, 1)),
                    [tk_hinT])

    tk_h = b.step(V, lambda e: e.scalar_tensor_tensor(
        sl(0), sl(9), hincol[:, :], sl(8), OP.mult, OP.add),
        [tk_hin, tk_cumP, tk_hloc])
    tk_Cc = b.step(V, lambda e: e.scalar_tensor_tensor(
        sl(5), sl(2), s_wsC[:, :], sl(5), OP.mult, OP.subtract),
        [tk_Cr, tk_Bx1, tk_wsneg])
    tk_hc = b.step(V, lambda e: e.tensor_mul(sl(7), sl(0), sl(5)),
                   [tk_h, tk_Cc, tk_Bx])

    tk_hcs = b.dma(GP, hcb, sl(7), [tk_hc], sem=d_msc)
    tk_hcf = b.dma(SY, hcf()[:, :].rearrange("n (c t) -> n c t", c=8),
                   hcb.rearrange("(c n) t -> n c t", c=8),
                   [tk_hcs, tk_rpev[-1], tk_wsev], sem=d_h1)

    tk_rr1 = b.dma(S, rowD, r_r8, [tk_r8, tk_op4], sem=d_h2)
    tk_rr2 = b.dma(S, rrow.rearrange("o (c t) -> o c t", c=8),
                   rowD.rearrange("o (c t) -> o c t", c=8),
                   [tk_rr1, tk_br3, tk_br4], sem=d_h2)
    tk_mr1 = b.dma(SY, rowA, r_m8, [tk_m8, tk_br2, tk_msq, tk_dt1, tk_op3, tk_br1],
                   sem=d_h1)
    tk_mr2 = b.dma(SY, mrow_neg.rearrange("o (c t) -> o c t", c=8),
                   rowA.rearrange("o (c t) -> o c t", c=8),
                   [tk_mr1, tk_mevac[-1]], sem=d_h1)
    tk_mrb = b.step(S, lambda e: e.activation(s_mrowb[:, :], mrow_neg, AF.Identity),
                    [tk_mr2, tk_wsmm])
    def rt_mm(e):
        ins = None
        for tt in range(NT):
            ins = e.transpose(bank(7, 0, 128, tt, 1), rrow[0:1, tt * 128:(tt + 1) * 128],
                              s_eyef[0:1, 0:1])
        return ins
    tk_rtT = b.step(TE, rt_mm, [tk_rr2, tk_hinT, tk_hin])
    tk_rt = b.step(V, lambda e: e.tensor_copy(s_rt[:, :], bank(7, 0, 128, 0, NT)),
                   [tk_rtT])

    NBLK = 4
    tk_comb = {}
    tk_x1 = {}
    tk_x1st = {}
    tk_xtld = {}
    tk_sqa = {}
    xtp = [0]
    def pump_xt(dep):
        if xtp[0] >= NT:
            return
        tt = xtp[0]
        xtp[0] += 1
        tk_xtld[tt] = b.dma(SY if tt % 2 == 0 else S, p3_xt(tt % 2),
                            x_tm[tt * 128:(tt + 1) * 128, :],
                            [sq_done, dep], sem=d_xt[tt % 2])
    pump_xt(None)
    pump_xt(None)
    x1_free = [sq_done, sq_done]
    attn_free = [sq_done] * NBLK
    t2_free = [sq_done, sq_done]
    pend_sq2 = [None]
    def emit_sq2():
        nonlocal_state = pend_sq2[0]
        if nonlocal_state is None:
            return
        tt, sl_ = nonlocal_state
        pend_sq2[0] = None
        tk_s2 = b.step(S, lambda e, tt=tt, sl_=sl_: e.activation(
            p3_trash, p3_x1(sl_), AF.Square, accum_out=s_sq2[:, tt:tt + 1]),
            [tk_x1[tt], _trash[0]])
        _trash[0] = tk_s2
        _lasts2[0] = tk_s2
        x1_free[sl_] = ("multi", x1_free[sl_], tk_s2)
    _trash = [sq_done]
    _lasts2 = [None]
    ybank_free = [None, None]
    last_mn = None

    for blk in range(NT // NBLK):
        for ti in range(NBLK):
            tt = blk * NBLK + ti
            par = tt % 2
            yb = (PA, PC)[par]
            sl_ = tt % 2
            def mm(e, tt=tt, yb=yb):
                for half in range(2):
                    e.matmul(yb[0:128, half * 512:(half + 1) * 512],
                             hcf(tt * 128, 128),
                             s_woutb[:, half * 512:(half + 1) * 512],
                             start=True, stop=False)
                for c in range(8):
                    e.matmul(yb[0:128, c * 128:(c + 1) * 128],
                             xf(c, tt * 128, 128), s_d8b[:, c * 128:(c + 1) * 128],
                             start=False, stop=False)
                e.matmul(yb[0:128, 0:512], s_mrowb[0:1, tt * 128:(tt + 1) * 128],
                         s_dpb[0:1, 0:512], start=False, stop=True)
                return e.matmul(yb[0:128, 512:1024],
                                s_mrowb[0:1, tt * 128:(tt + 1) * 128],
                                s_dpb[0:1, 512:1024], start=False, stop=True)
            deps = [tk_hcf, tk_mrb, tk_woutneg, tk_boot[BT_DPB], tk_boot[BT_D8B],
                    tk_rt, ybank_free[par]]
            if tt < 2:
                deps += [tk_rtT]
            tk_mm = b.step(TE, mm, deps)
            def comb(e, tt=tt, ti=ti, yb=yb):
                return e.activation(p3_attn(ti), yb[0:128, 0:1024], AF.Identity,
                                    scale=s_rt[:, tt:tt + 1],
                                    accum_out=s_suma[:, tt:tt + 1])
            tk_comb[tt] = b.step(S, comb, [tk_mm, tk_rt, attn_free[ti]])
            ybank_free[par] = tk_comb[tt]
            tk_sqa[tt] = b.step(S, lambda e, tt=tt, ti=ti: e.activation(
                p3_trash, p3_attn(ti), AF.Square, accum_out=s_sqa[:, tt:tt + 1]),
                [tk_comb[tt], _trash[0]])
            _trash[0] = tk_sqa[tt]
        b0, b1_ = blk * NBLK, (blk + 1) * NBLK
        def bst1(e, b0=b0, b1_=b1_):
            e.activation(s_m4[:, b0:b1_], s_suma[:, b0:b1_], AF.Identity, scale=-1.0 / DM)
            e.drain()
            return e.activation(s_tt1[:, b0:b1_], s_sqa[:, b0:b1_], AF.Identity,
                                scale=1.0 / DM)
        tk_b1 = b.step(S, bst1, [tk_sqa[b1_ - 1], tk_comb[b1_ - 1]])
        def bst2(e, b0=b0, b1_=b1_):
            e.tensor_mul(s_tt2[:, b0:b1_], s_m4[:, b0:b1_], s_m4[:, b0:b1_])
            e.drain()
            return e.tensor_sub(s_tt1[:, b0:b1_], s_tt1[:, b0:b1_], s_tt2[:, b0:b1_])
        tk_b2 = b.step(V, bst2, [tk_b1])
        def bst3(e, b0=b0, b1_=b1_):
            e.activation(s_tt2[:, b0:b1_], s_tt1[:, b0:b1_], AF.Ln, bias=1e-5)
            e.drain()
            return e.activation(s_r4[:, b0:b1_], s_tt2[:, b0:b1_], AF.Exp, scale=-0.5)
        tk_b3 = b.step(S, bst3, [tk_b2])
        tk_b4 = b.step(V, lambda e, b0=b0, b1_=b1_: e.tensor_mul(
            s_mr4[:, b0:b1_], s_m4[:, b0:b1_], s_r4[:, b0:b1_]), [tk_b3])
        for ti in range(NBLK):
            tt = blk * NBLK + ti
            sl_ = tt % 2
            tk_t2 = b.step(S, lambda e, tt=tt, sl_=sl_: e.activation(
                p3_t2(sl_), p3_xt(sl_), AF.Identity, bias=s_mr4[:, tt:tt + 1]),
                [tk_xtld[tt], tk_b4, t2_free[sl_]])
            pump_xt(tk_t2)
            emit_sq2()
            def x1f(e, tt=tt, ti=ti, sl_=sl_):
                return e.scalar_tensor_tensor(
                    p3_x1(sl_), p3_attn(ti), s_r4[:, tt:tt + 1], p3_t2(sl_),
                    OP.mult, OP.add, accum_out=s_sum2[:, tt:tt + 1])
            tk_x1[tt] = b.step(V, x1f, [tk_t2, tk_comb[tt], tk_b4, x1_free[sl_]])
            attn_free[ti] = tk_x1[tt]
            t2_free[sl_] = tk_x1[tt]
            tk_mn = b.step(V, lambda e, tt=tt, sl_=sl_: (
                e.tensor_reduce(s_minT[:, tt:tt + 1], p3_x1(sl_), AX.X, OP.min),
                e.tensor_reduce(s_maxT[:, tt:tt + 1], p3_x1(sl_), AX.X, OP.max))[-1],
                [tk_x1[tt]])
            last_mn = tk_mn
            pend_sq2[0] = (tt, sl_)
            tk_x1st[tt] = b.dma(GP, x1_d[tt * 128:(tt + 1) * 128, :], p3_x1(sl_),
                                [tk_x1[tt]], sem=d_x1s)
            x1_free[sl_] = ("multi", tk_mn, tk_x1st[tt])

    emit_sq2()
    def m2c(e):
        e.activation(s_m2[:, :], s_sum2[:, :], AF.Identity, scale=-1.0 / DM)
        e.drain()
        return e.activation(s_tt1[:, :], s_sq2[:, :], AF.Identity, scale=1.0 / DM)
    tk_m2 = b.step(S, m2c, [tk_x1[NT - 1], _lasts2[0]])
    def v2a(e):
        e.tensor_mul(s_tt2[:, :], s_m2[:, :], s_m2[:, :])
        e.drain()
        return e.tensor_sub(s_tt1[:, :], s_tt1[:, :], s_tt2[:, :])
    tk_v2a = b.step(V, v2a, [tk_m2])
    def s2b(e):
        e.activation(s_tt2[:, :], s_tt1[:, :], AF.Ln, bias=1e-5)
        e.drain()
        return e.activation(s_r2[:, :], s_tt2[:, :], AF.Exp, scale=-0.5)
    tk_s2b = b.step(S, s2b, [tk_v2a])
    def cnd(e):
        e.tensor_add(s_tt1[:, :], s_maxT[:, :], s_m2[:, :])
        e.tensor_add(s_tt2[:, :], s_minT[:, :], s_m2[:, :])
        e.drain()
        e.tensor_mul(s_tt1[:, :], s_tt1[:, :], s_r2[:, :])
        e.tensor_mul(s_tt2[:, :], s_tt2[:, :], s_r2[:, :])
        e.drain()
        e.tensor_single_scalar(s_tt2[:, :], s_tt2[:, :], -1.0, OP.mult)
        e.drain()
        e.tensor_reduce(s_cand[:, 0:1], s_tt1[:, :], AX.X, OP.max)
        return e.tensor_reduce(s_cand[:, 1:2], s_tt2[:, :], AX.X, OP.max)
    tk_cnd = b.step(V, cnd, [tk_s2b, last_mn])
    tk_cndT = b.step(TE, lambda e: e.transpose(bank(7, 0, 2, 0, 128), s_cand[:, :],
                                               s_eyef[:, :]), [tk_cnd, tk_rtT, tk_rt])
    tk_cnd2 = b.step(V, lambda e: e.tensor_reduce(s_cred[:, :], bank(7, 0, 2, 0, 128),
                                                  AX.X, OP.max), [tk_cndT])
    tk_ar2s = b.dma(GP, ar2i.rearrange("a b -> b a"), s_cred[:, :], [tk_cnd2], sem=d_msc)
    tk_cc2 = b.collective("AllReduce", OP.max, ar2i, ar2o, [tk_ar2s])
    tk_ar2l = b.dma(GP, s_q0[0:1, 8:10], ar2o, [tk_cc2], sem=d_msc)

    def qs1(e):
        e.tensor_add(s_q0[0:1, 10:11], s_q0[0:1, 8:9], s_q0[0:1, 9:10])
        e.drain()
        e.tensor_scalar(s_q0[0:1, 10:11], s_q0[0:1, 10:11], 1.0 / 255.0, 1e-8,
                        OP.mult, OP.max)
        e.drain()
        e.reciprocal(s_q0[0:1, 11:12], s_q0[0:1, 10:11])
        e.drain()
        e.tensor_mul(s_q0[0:1, 12:13], s_q0[0:1, 9:10], s_q0[0:1, 11:12])
        e.drain()
        e.tensor_scalar(s_q0[0:1, 14:15], s_q0[0:1, 12:13], MAGIC, MAGIC,
                        OP.add, OP.subtract)
        e.drain()
        e.tensor_scalar(s_q0[0:1, 14:15], s_q0[0:1, 14:15], -128.0, None, OP.add)
        e.drain()
        e.tensor_scalar(s_q0[0:1, 14:15], s_q0[0:1, 14:15], -128.0, 127.0,
                        OP.max, OP.min)
        e.drain()
        e.tensor_scalar(s_q0[0:1, 12:13], s_q0[0:1, 14:15], -1.0, -128.0,
                        OP.mult, OP.add)
        e.tensor_scalar(s_q0[0:1, 13:14], s_q0[0:1, 14:15], -1.0, 127.0,
                        OP.mult, OP.add)
        e.drain()
        return e.tensor_mul(s_q0[0:1, 15:16], s_q0[0:1, 10:11], s_q0[0:1, 2:3])
    tk_qs1 = b.step(V, qs1, [tk_ar2l, tk_sw])
    tk_qrmm1 = b.step(TE, lambda e: e.matmul(bank(7, 0, 128, 0, 4), s_ones1f[0:1, :],
                                             s_q0[0:1, 12:16], start=True, stop=True),
                      [tk_qs1, tk_cndT, tk_cnd2])
    tk_qrep1 = b.step(V, lambda e: (
        e.tensor_copy(s_qrep[:, 0:1], bank(7, 0, 128, 0, 1)),
        e.tensor_copy(s_qrep[:, 1:2], bank(7, 0, 128, 1, 1)),
        e.tensor_copy(s_qrep[:, 2:3], bank(7, 0, 128, 3, 1)))[-1], [tk_qrmm1])
    tk_qrmm2 = b.step(TE, lambda e: e.matmul(bank(7, 0, 128, 4, 1), s_ones1f[0:1, :],
                                             s_q0[0:1, 11:12], start=True, stop=True),
                      [tk_qs1, tk_qrmm1])
    tk_qrep2 = b.step(V, lambda e: e.tensor_copy(s_qrep[:, 7:8], bank(7, 0, 128, 4, 1)),
                      [tk_qrmm2])
    def aqbq(e):
        e.tensor_scalar(s_aq[:, :], s_r2[:, :], s_qrep[:, 7:8], None, OP.mult)
        e.drain()
        return e.tensor_mul(s_bq[:, :], s_m2[:, :], s_aq[:, :])
    tk_aq = b.step(V, aqbq, [tk_qrep2, tk_s2b, tk_cnd])

    all_x1st = Tok(d_x1s, d_x1s.n, "_", True)

    tk_bar34 = b.barrier()

    tk_w1q = {}
    w1_last_act = [None, None, None, None]
    t0scr_free = [None]
    w1_seq = [(hh, c) for hh in range(4) for c in range(8)]
    w1_slot = lambda idx: (idx % 4) if idx < 16 else (idx % 2)
    w1_next = {}
    _seen = {}
    for _i in range(32):
        _s = w1_slot(_i)
        if _s in _seen:
            w1_next[_seen[_s]] = _i
        _seen[_s] = _i
    w1_ld = {}
    def w1_load(idx, dep):
        hh, c = w1_seq[idx]
        sl_ = w1_slot(idx)
        w1_ld[idx] = b.dma(SY if sl_ % 2 == 0 else S, p4_w1raw(sl_),
                           w1t[c * 128:(c + 1) * 128, hh * 1024:(hh + 1) * 1024],
                           [tk_cnd, dep], sem=d_w1[sl_])
    def w1_emit(idx):
        hh, c = w1_seq[idx]
        sl_ = w1_slot(idx)
        tkt0 = b.step(S, lambda e, sl_=sl_: e.activation(
            p4_w1t0, p4_w1raw(sl_), AF.Identity, scale=s_wq[:, 0:1]),
            [w1_ld[idx], tk_wq, t0scr_free[0]])
        w1_last_act[sl_] = tkt0
        tkq = b.step(V, lambda e, c=c, hh=hh: e.tensor_scalar(
            w1qn(c, hh * 1024, 1024), p4_w1t0, MAGIC, MAGIC, OP.add, OP.subtract),
            [tkt0, all_x1st])
        t0scr_free[0] = tkq
        tk_w1q[(c, hh)] = tkq
        if idx in w1_next:
            w1_load(w1_next[idx], tkt0)
    for idx in range(4):
        w1_load(idx, None)
    for idx in range(32):
        w1_emit(idx)
    gsb_free2 = [w1_last_act[2], w1_last_act[3]]

    tk_w2q = {}
    w2_last_act = {0: None, 1: None}
    w2_slotmap = lambda k: (k % 2) if G > 1 else 0
    def p4_w2slot(s):
        return p4_w2raw(0) if s == 0 else p4_w1raw(0)
    w2_ld = {}
    w2lp = [0]
    def w2_pump():
        if w2lp[0] >= 32:
            return
        k = w2lp[0]
        w2lp[0] += 1
        s = w2_slotmap(k)
        dep = w2_last_act[s]
        if s == 1 and dep is None:
            dep = w1_last_act[0]
        w2_ld[k] = b.dma(SY if k % 2 == 0 else S, p4_w2slot(s),
                         w2t[k * 128:(k + 1) * 128, :],
                         [tk_bar34, dep], sem=d_w2[s])
    def w2_emit(k):
        s = w2_slotmap(k)
        tkt0 = b.step(S, lambda e, s=s: e.activation(
            p4_w2t0, p4_w2slot(s), AF.Identity, scale=s_wq[:, 1:2]),
            [w2_ld[k], tk_wq, t0scr_free[0]])
        w2_last_act[s] = tkt0
        tkq = b.step(V, lambda e, k=k: e.tensor_scalar(
            w2qn(k), p4_w2t0, MAGIC, MAGIC, OP.add, OP.subtract), [tkt0])
        t0scr_free[0] = tkq
        tk_w2q[k] = tkq
        w2_pump()
    def w2_start():
        if w2lp[0] == 0:
            w2_pump()
            w2_pump()

    xqn_free = [tk_hcf, ("multi", tk_mrb)]
    xqf_prev = {0: ("multi", tk_hcf, tk_mrb), 1: None}
    t0_free = [None, None]
    gsb_free = gsb_free2
    h_bank_free = {4: None, 5: None, 6: None, 7: None}
    tqbank_free = [None, None]
    tk_gst = {}
    gmm_last = None

    tk_xq = {}
    tk_clip4 = {}
    last_mm1_of = {}
    tk_x4ld = {}
    x4_seq = [(g_, ti) for g_ in range(G) for ti in range(4)]
    x4p = [0]
    def pump_x4(dep):
        if x4p[0] >= len(x4_seq):
            return
        g_, ti = x4_seq[x4p[0]]
        x4p[0] += 1
        tt = g_ * 4 + ti
        tk_x4ld[(g_, ti)] = b.dma(
            SY, p4_x1r(ti % 2), x1_d[tt * 128:(tt + 1) * 128, :],
            [tk_cnd, tk_x1st[tt], dep], sem=d_x1r[ti % 2])
    pump_x4(None)
    pump_x4(None)

    tk_f04 = {}
    def q4_pre(qg, ti):
        sl_ = ti % 2
        tt = qg * 4 + ti
        if qg <= 1 and ti % 2 == 1:
            tkt0 = b.step(V, lambda e, sl_=sl_, tt=tt: e.tensor_scalar(
                p4_t0(sl_), p4_x1r(sl_), s_aq[:, tt:tt + 1], s_bq[:, tt:tt + 1],
                OP.mult, OP.add), [tk_x4ld[(qg, ti)], tk_aq, t0_free[sl_]])
        else:
            tkt0 = b.step(S, lambda e, sl_=sl_, tt=tt: e.activation(
                p4_t0(sl_), p4_x1r(sl_), AF.Identity, scale=s_aq[:, tt:tt + 1],
                bias=s_bq[:, tt:tt + 1]), [tk_x4ld[(qg, ti)], tk_aq, t0_free[sl_]])
        tkf0 = b.step(V, lambda e, sl_=sl_: e.tensor_scalar(
            p4_x1r(sl_), p4_t0(sl_), MAGIC, MAGIC, OP.add, OP.subtract), [tkt0])
        t0_free[sl_] = tkf0
        tk_f04[(qg, ti)] = tkf0

    def q4_clip(qg, ti):
        sl_ = ti % 2
        tkxq = b.step(V, lambda e, sl_=sl_, ti=ti: e.tensor_scalar(
            p4_xqn(ti % 2), p4_x1r(sl_), s_qrep[:, 0:1], s_qrep[:, 1:2],
            OP.max, OP.min), [tk_f04[(qg, ti)], tk_qrep1, xqn_free[ti % 2]])
        tk_clip4[(qg, ti)] = tkxq
        pump_x4(tkxq)

    def q4_trn_ev(qg, ti):
        tb = ti % 2
        def trn(e, ti=ti, tb=tb):
            pq = (PA, PB)[tb][0:128, 0:512].bitcast(BF16)
            ins = None
            for c in range(8):
                ins = e.transpose(pq[:, c * 128:(c + 1) * 128],
                                  p4_xqn(ti % 2)[:, c * 128:(c + 1) * 128],
                                  s_eyeb[:, :])
            return ins
        tktr = b.step(TE, trn, [tk_clip4[(qg, ti)], tk_boot[BT_EYEB],
                                tqbank_free[tb]])
        xqn_free[ti % 2] = tktr
        def ev(e, qg=qg, ti=ti, tb=tb):
            dst = p4_xqf_dst(qg, ti)
            src = (PA, PB)[tb][0:128, 0:512].bitcast(BF16).rearrange(
                "p (c w) -> p c w", c=8)
            return e.activation(dst, src, AF.Identity)
        def evv(e, qg=qg, ti=ti, tb=tb):
            dst = p4_xqf_dst(qg, ti)
            src = (PA, PB)[tb][0:128, 0:512].bitcast(BF16).rearrange(
                "p (c w) -> p c w", c=8)
            return e.tensor_copy(dst, src)
        if qg <= 1 and ti % 2 == 1:
            tkev = b.step(V, evv, [tktr, last_mm1_of.get(qg - 2, xqf_prev[qg % 2])])
        else:
            tkev = b.step(S, ev, [tktr, last_mm1_of.get(qg - 2, xqf_prev[qg % 2])])
        tqbank_free[tb] = tkev
        tk_xq[(qg, ti)] = tkev

    for it in range(G + 1):
        qg = it if it < G else None
        mg = it - 1 if it >= 1 else None
        if qg == 0:
            q4_pre(0, 0)
            q4_pre(0, 1)
            q4_clip(0, 0)
            q4_pre(0, 2)
            q4_clip(0, 1)
            q4_pre(0, 3)
        if mg is None:
            if qg == 0:
                q4_trn_ev(0, 0)
                q4_clip(0, 2)
                q4_trn_ev(0, 1)
                q4_clip(0, 3)
                q4_trn_ev(0, 2)
                q4_trn_ev(0, 3)
                if G > 1:
                    q4_pre(1, 0)
                    q4_pre(1, 1)
                    q4_clip(1, 0)
                    q4_pre(1, 2)
                    q4_clip(1, 1)
                    q4_pre(1, 3)
            continue
        if G == 1:
            w2ks = list(range(32)) if mg == 0 else []
        elif mg >= 1:
            w2ks = list(range(((mg - 1) * 32) // (G - 1), (mg * 32) // (G - 1)))
        else:
            w2ks = []
        w2i = 0
        if w2ks:
            w2_start()
        for hc in range(32):
            hh = hc // 8
            bk = 4 + (hc % 4)
            def mm1(e, mg=mg, hc=hc, bk=bk):
                ins = None
                for c in range(8):
                    ins = e.matmul(bank(bk), w1qn(c, hc * 128, 128),
                                   p4_xqf2(mg, c), start=(c == 0), stop=(c == 7))
                return ins
            deps = [tk_xq[(mg, t_)] for t_ in range(4)] + \
                   [tk_w1q[(c, hh)] for c in range(8)] + [h_bank_free[bk]]
            tkm = b.step(TE, mm1, deps)
            last_mm1_of[mg] = tkm
            q = (hc // 4) % 2
            ciq = hc % 4
            tkg = b.step(S, lambda e, hc=hc, bk=bk, q=q, ciq=ciq: e.activation(
                p4_gsb(q)[:, ciq * 512:(ciq + 1) * 512], bank(bk),
                AF.Gelu, scale=s_qrep[:, 2:3], bias=s_b1[:, hc:hc + 1]),
                [tkm, tk_qrep1, tk_boot[BT_B1]] +
                ([gsb_free[q]] if ciq == 0 else []))
            h_bank_free[bk] = tkg
            tkmn = b.step(V, lambda e, hc=hc, q=q, ciq=ciq: (
                e.tensor_reduce(s_gmx[:, hc:hc + 1],
                                p4_gsb(q)[:, ciq * 512:(ciq + 1) * 512], AX.X, OP.max),
                e.tensor_reduce(s_gmn[:, hc:hc + 1],
                                p4_gsb(q)[:, ciq * 512:(ciq + 1) * 512], AX.X,
                                OP.min))[-1], [tkg])
            if ciq == 3:
                q4 = hc // 4
                tkst = b.dma((GP, SY, GP, S)[q4 % 4],
                             g_d[mg, q4 * 4:(q4 + 1) * 4].rearrange("q p c -> p q c"),
                             p4_gsb(q), [tkg], sem=d_gst[q4 % 4])
                tk_gst[(mg, q4)] = tkst
                gsb_free[q] = ("multi", tkst, tkmn)
            gmm_last = tkmn
            if G > 1 and w2i < len(w2ks) and hc % 2 == 1 and hc // 2 >= w2i:
                w2_emit(w2ks[w2i])
                w2i += 1
            if qg is not None and hc % 8 == 2:
                hcq = hc // 8
                q4_trn_ev(qg, hcq)
                if hcq < 2:
                    q4_clip(qg, hcq + 2)
            nq = mg + 2
            if nq < G:
                if hc == 13:
                    q4_pre(nq, 0)
                elif hc == 15:
                    q4_pre(nq, 1)
                elif hc == 19:
                    q4_clip(nq, 0)
                elif hc == 21:
                    q4_pre(nq, 2)
                elif hc == 27:
                    q4_clip(nq, 1)
                elif hc == 29:
                    q4_pre(nq, 3)
        while w2i < len(w2ks):
            w2_emit(w2ks[w2i])
            w2i += 1
        def fold(e):
            e.tensor_reduce(s_gtmp[:, :], s_gmx[:, :], AX.X, OP.max)
            e.drain()
            e.tensor_tensor(s_gmxr[:, :], s_gmxr[:, :], s_gtmp[:, :], OP.max)
            e.drain()
            e.tensor_reduce(s_gtmp[:, :], s_gmn[:, :], AX.X, OP.min)
            e.drain()
            return e.tensor_tensor(s_gmnr[:, :], s_gmnr[:, :], s_gtmp[:, :], OP.min)
        gmm_last = b.step(V, fold, [gmm_last, tk_memset])

    def gneg(e):
        e.tensor_single_scalar(s_gmnr[:, :], s_gmnr[:, :], -1.0, OP.mult)
        e.drain()
        e.tensor_copy(s_cand[:, 0:1], s_gmxr[:, :])
        return e.tensor_copy(s_cand[:, 1:2], s_gmnr[:, :])
    tk_gneg = b.step(V, gneg, [gmm_last])
    tk_gT = b.step(TE, lambda e: e.transpose(bank(7, 0, 2, 0, 128), s_cand[:, :],
                                             s_eyef[:, :]), [tk_gneg, tk_qrmm2, tk_qrep2])
    tk_g2 = b.step(V, lambda e: e.tensor_reduce(s_cred[:, :], bank(7, 0, 2, 0, 128),
                                                AX.X, OP.max), [tk_gT])
    tk_ar3s = b.dma(GP, ar3i.rearrange("a b -> b a"), s_cred[:, :], [tk_g2], sem=d_msc)
    tk_cc3 = b.collective("AllReduce", OP.max, ar3i, ar3o, [tk_ar3s])
    tk_ar3l = b.dma(GP, s_q2[0:1, 0:2], ar3o, [tk_cc3], sem=d_msc)

    def qs2(e):
        e.tensor_add(s_q2[0:1, 2:3], s_q2[0:1, 0:1], s_q2[0:1, 1:2])
        e.drain()
        e.tensor_scalar(s_q2[0:1, 2:3], s_q2[0:1, 2:3], 1.0 / 255.0, 1e-8,
                        OP.mult, OP.max)
        e.drain()
        e.reciprocal(s_q2[0:1, 3:4], s_q2[0:1, 2:3])
        e.drain()
        e.tensor_mul(s_q2[0:1, 4:5], s_q2[0:1, 1:2], s_q2[0:1, 3:4])
        e.drain()
        e.tensor_scalar(s_q2[0:1, 5:6], s_q2[0:1, 4:5], MAGIC, MAGIC,
                        OP.add, OP.subtract)
        e.drain()
        e.tensor_scalar(s_q2[0:1, 6:7], s_q2[0:1, 5:6], -128.0, None, OP.add)
        e.drain()
        e.tensor_scalar(s_q2[0:1, 6:7], s_q2[0:1, 6:7], -128.0, 127.0, OP.max, OP.min)
        e.drain()
        e.tensor_scalar(s_q2[0:1, 7:8], s_q2[0:1, 6:7], -1.0, -128.0, OP.mult, OP.add)
        e.tensor_scalar(s_q2[0:1, 8:9], s_q2[0:1, 6:7], -1.0, 127.0, OP.mult, OP.add)
        e.drain()
        return e.tensor_mul(s_q2[0:1, 9:10], s_q2[0:1, 2:3], s_q0[0:1, 3:4])
    tk_qs2 = b.step(V, qs2, [tk_ar3l, tk_sw])
    def qrmm3(e):
        e.matmul(bank(7, 0, 128, 0, 1), s_ones1f[0:1, :], s_q2[0:1, 3:4],
                 start=True, stop=True)
        return e.matmul(bank(7, 0, 128, 1, 3), s_ones1f[0:1, :], s_q2[0:1, 7:10],
                        start=True, stop=True)
    tk_qrmm3 = b.step(TE, qrmm3, [tk_qs2, tk_gT, tk_g2])
    tk_qrep3 = b.step(V, lambda e: (
        e.tensor_copy(s_qrep[:, 3:4], bank(7, 0, 128, 0, 1)),
        e.tensor_copy(s_qrep[:, 4:5], bank(7, 0, 128, 1, 1)),
        e.tensor_copy(s_qrep[:, 5:6], bank(7, 0, 128, 2, 1)),
        e.tensor_copy(s_qrep[:, 6:7], bank(7, 0, 128, 3, 1)))[-1], [tk_qrmm3])

    graw_free = [gsb_free[0], gsb_free[1]]
    gt0_free = [None, None]
    gf0_free = [None, None]
    mm2_bank_free = {6: None, 7: None}
    tm_bank_free = [None, None, None]
    ffn_free = [None] * 8
    out_free = [None, None]
    t25_free = [None, None]

    tk_gld = {}
    tk_gq = {}
    tk_x15 = {}
    tk_ffnev = {}
    last_mm2_of = {}
    last_trn_of = {}

    def plan_gld(gi, q):
        sl_ = q % 2
        tk_gld[(gi, q)] = b.dma(
            SY if sl_ == 0 else S, p5_graw(sl_),
            g_d[gi, q * 4:(q + 1) * 4].rearrange("q p c -> p q c"),
            [tk_gst[(gi, q)], graw_free[sl_]], sem=d_gld[sl_])

    odd_g = [g_ for g_ in range(G) if g_ % 2 == 1]
    x15dep = last_mm1_of.get(odd_g[-1]) if odd_g else None
    x1q_seq = [(g_, ti) for g_ in range(G) for ti in range(4)]
    x1p = [0]
    def pump_x1(dep):
        if x1p[0] >= len(x1q_seq):
            return
        g_, ti = x1q_seq[x1p[0]]
        x1p[0] += 1
        tt = g_ * 4 + ti
        tk_x15[(g_, ti)] = b.dma(
            SY, p5_x1r(ti), x1_d[tt * 128:(tt + 1) * 128, :],
            [x15dep, tk_x1st[tt], dep], sem=d_x1r[2 + ti % 2])

    plan_gld(0, 0)
    plan_gld(0, 1)
    pump_x1(None)
    pump_x1(None)

    tk_bar45 = b.barrier()
    for it in range(G + 1):
        qg = it if it < G else None
        mg = it - 1 if it >= 1 else None
        for q in range(8):
            if qg is not None:
                for j in range(4):
                    k = q * 4 + j
                    st = k % 2
                    sl_ = q % 2
                    koff = j * 512
                    tkt0 = b.step(S, lambda e, sl_=sl_, koff=koff, st=st:
                                  e.activation(
                                      p5_gt0(st), p5_graw(sl_)[:, koff:koff + 512],
                                      AF.Identity, scale=s_qrep[:, 3:4]),
                                  [tk_gld[(qg, q)], tk_qrep3, gt0_free[st]])
                    tkf0 = b.step(V, lambda e, st=st: e.tensor_scalar(
                        p5_gf0(st), p5_gt0(st), MAGIC, MAGIC, OP.add, OP.subtract),
                        [tkt0, gf0_free[st]])
                    gt0_free[st] = tkf0
                    tkq = b.step(V, lambda e, qg=qg, k=k, st=st: e.tensor_scalar(
                        gq2(qg, k), p5_gf0(st), s_qrep[:, 4:5], s_qrep[:, 5:6],
                        OP.max, OP.min),
                        [tkf0, tk_bar45, last_mm2_of.get(qg - 2)])
                    gf0_free[st] = tkq
                    tk_gq[(qg, k)] = tkq
                    if j == 3:
                        graw_free[sl_] = tkt0
                        gqq = qg * 8 + q + 2
                        if gqq < G * 8:
                            plan_gld(gqq // 8, gqq % 8)
            if mg is not None:
                dj = q
                bk = 6 + dj % 2
                def mm2(e, mg=mg, dj=dj, bk=bk, k=None):
                    if k is not None:
                        return e.matmul(bank(bk), w2qn(k, dj * 128, 128),
                                        gq2(mg, k), start=(k == 0), stop=(k == 31))
                    ins = None
                    for k_ in range(32):
                        ins = e.matmul(bank(bk), w2qn(k_, dj * 128, 128), gq2(mg, k_),
                                       start=(k_ == 0), stop=(k_ == 31))
                    return ins
                if mg == 0:
                    tkm = None
                    for k in range(32):
                        kdeps = [tk_gq[(mg, k)], tk_w2q[k]] + \
                                ([mm2_bank_free[bk]] if k == 0 else [])
                        tkm = b.step(TE, lambda e, mm2=mm2, k=k: mm2(e, k=k), kdeps)
                else:
                    deps = [tk_gq[(mg, k)] for k in range(32)] + \
                           [tk_w2q[k] for k in range(32)] + [mm2_bank_free[bk]]
                    tkm = b.step(TE, mm2, deps)
                last_mm2_of[mg] = tkm
                tkev = b.step(S, lambda e, dj=dj, bk=bk: e.activation(
                    p5_ffn(dj), bank(bk), AF.Identity, scale=s_qrep[:, 6:7],
                    bias=s_b2[:, dj:dj + 1]),
                    [tkm, tk_qrep3, tk_boot[BT_B2], ffn_free[dj]])
                mm2_bank_free[bk] = tkev
                tk_ffnev[dj] = tkev
        if mg is not None:
            for ti in range(4):
                tt = mg * 4 + ti
                par = ti % 2
                b3 = tt % 3
                tmb = (PA, PB, PC)[b3]
                def trnh(e, ti=ti, tmb=tmb, djs=None):
                    ins = None
                    for dj in djs:
                        ins = e.transpose(tmb[0:128, dj * 128:(dj + 1) * 128],
                                          p5_ffn(dj)[:, ti * 128:(ti + 1) * 128],
                                          s_eyef[:, :])
                    return ins
                tktrA = b.step(TE, lambda e, trnh=trnh: trnh(e, djs=range(4)),
                               [tk_ffnev[dj] for dj in range(4)] +
                               [tk_boot[BT_EYEF], tm_bank_free[b3]])
                tktr = b.step(TE, lambda e, trnh=trnh: trnh(e, djs=range(4, 8)),
                              [tk_ffnev[dj] for dj in range(4, 8)] + [tktrA])
                last_trn_of[mg] = tktr
                def bn3(e, tmb=tmb):
                    e.bn_stats(s_bn[:, 0:6], tmb[0:128, 0:512])
                    e.bn_stats(s_bn[:, 6:12], tmb[0:128, 512:1024])
                    e.drain()
                    return e.bn_aggr(s_ag2[:, :], s_bn[:, :])
                tkbn = b.step(V, bn3, [tktr])
                def st3(e, ti=ti):
                    e.activation(s_m3[:, ti:ti + 1], s_ag2[:, 0:1], AF.Identity,
                                 scale=-1.0)
                    e.activation(s_r3[:, ti:ti + 1], s_ag2[:, 1:2], AF.Ln, bias=1e-5)
                    e.drain()
                    return e.activation(s_r3[:, ti:ti + 1], s_r3[:, ti:ti + 1],
                                        AF.Exp, scale=-0.5)
                tkst3 = b.step(S, st3, [tkbn])
                tkmr3 = b.step(V, lambda e, ti=ti: e.tensor_mul(
                    s_mr3[:, ti:ti + 1], s_m3[:, ti:ti + 1], s_r3[:, ti:ti + 1]),
                    [tkst3])
                tkt2 = b.step(S, lambda e, ti=ti, par=par: e.activation(
                    p5_t2(par), p5_x1r(ti), AF.Identity, bias=s_mr3[:, ti:ti + 1]),
                    [tk_x15[(mg, ti)], tkmr3, t25_free[par]])
                pump_x1(tkt2)
                def x2f(e, ti=ti, par=par, tmb=tmb):
                    return e.scalar_tensor_tensor(
                        out_sl(ti), tmb[0:128, 0:1024], s_r3[:, ti:ti + 1],
                        p5_t2(par), OP.mult, OP.add)
                tkx2 = b.step(V, x2f, [tkt2, tkbn, tkmr3, out_free[ti % 2]])
                t25_free[par] = tkx2
                tm_bank_free[b3] = tkx2
                tkos = b.dma(SY if ti % 2 == 0 else S,
                             out_d[tt * 128:(tt + 1) * 128, :], out_sl(ti),
                             [tkx2], sem=d_outs[ti % 2])
                out_free[ti % 2] = tkos
            for dj in range(8):
                ffn_free[dj] = last_trn_of[mg]

    b.step(SY, lambda e: None, [Tok(d_outs[0], d_outs[0].n, "_", True),
                                Tok(d_outs[1], d_outs[1].n, "_", True)])

    b.emit()
    return nc


def _host_layout(inputs, T):
    x = np.ascontiguousarray(np.asarray(inputs["L0_attn_x"], dtype=np.float32))
    Wxp = np.asarray(inputs["L0_attn_Wxproj"], dtype=np.float32)
    Wdt = np.asarray(inputs["L0_attn_Wdt"], dtype=np.float32)
    bdt = np.asarray(inputs["L0_attn_bdt"], dtype=np.float32)
    Wxs = np.asarray(inputs["L0_attn_Wxs"], dtype=np.float32)
    Alog = np.asarray(inputs["L0_attn_A_log"], dtype=np.float32)
    Wout = np.asarray(inputs["L0_attn_Wout"], dtype=np.float32)
    Dp = np.asarray(inputs["L0_attn_Dp"], dtype=np.float32)
    W1 = np.asarray(inputs["L0_ffn_W1"], dtype=np.float32)
    b1 = np.asarray(inputs["L0_ffn_b1"], dtype=np.float32)
    W2 = np.asarray(inputs["L0_ffn_W2"], dtype=np.float32)
    b2 = np.asarray(inputs["L0_ffn_b2"], dtype=np.float32)

    Bsz, L, D = x.shape
    xs = x.reshape(Bsz * L, D)
    wc_t = np.ascontiguousarray(np.concatenate([Wxp, Wxs], 0).T)
    w1t = np.ascontiguousarray(W1.T)
    w2t = np.ascontiguousarray(W2.T)
    d8 = np.zeros((128, DM), np.float32)
    idx = np.arange(128)
    for cb in range(8):
        d8[idx, cb * 128 + idx] = Dp[cb * 128:(cb + 1) * 128]
    bdiagp = np.zeros((NCH, 128), np.float32)
    for c in range(NCH):
        bdiagp[c, c * NST:(c + 1) * NST] = 1.0
    r48 = np.zeros((48, 3 * 128), np.float32)
    for c in range(NCH):
        for n in range(NST):
            r48[n, c * NST + n] = 1.0
            r48[16 + n, 128 + c * NST + n] = 1.0
            r48[32 + n, 256 + c * NST + n] = 1.0
    common = dict(
        wc_t=wc_t, wdt_c=np.ascontiguousarray(Wdt.reshape(64, 1)),
        bdt_i=bdt.reshape(1, 1), alog=Alog.reshape(1, NST),
        woutT=np.ascontiguousarray(Wout.T), dp_row=Dp.reshape(1, DM), d8=d8,
        w1t=w1t, b1sb=np.ascontiguousarray(b1.reshape(32, 128).T),
        w2t=w2t, b2sb=np.ascontiguousarray(b2.reshape(8, 128).T),
        eyef=np.eye(128, dtype=np.float32),
        eyeb=np.eye(128, dtype=ml_dtypes.bfloat16),
        bdiagp=bdiagp, bdiagn=-bdiagp, r48=r48,
    )
    in_maps = []
    for i in range(NCORES):
        shard = xs[i * T:(i + 1) * T]
        hsel = np.zeros((1, 128), np.float32)
        hsel[0, NST * i:NST * (i + 1)] = 1.0
        wsh = np.ascontiguousarray(np.concatenate(
            [w1t[i * 128:(i + 1) * 128, :],
             w2t[i * 512:(i + 1) * 512, :].reshape(128, 4096)], axis=1))
        m = dict(common)
        m.update(x_fm=np.ascontiguousarray(shard.T),
                 x_tm=np.ascontiguousarray(shard), hsel=hsel, wsh=wsh)
        in_maps.append(m)
    return in_maps, (Bsz, L, D)


def kernel(**inputs):
    x = np.asarray(inputs["L0_attn_x"])
    Bsz, L, D = x.shape
    T = (Bsz * L) // NCORES
    in_maps, shape = _host_layout(inputs, T)
    nc = build_program(T)
    res = run_bass_kernel_spmd(nc, in_maps, core_ids=list(range(NCORES)))
    out = np.concatenate([np.asarray(r["out"]) for r in res.results], axis=0)
    return out.reshape(Bsz, L, D).astype(np.float32)
